# version 04
# baseline (speedup 1.0000x reference)
import numpy as np
import ml_dtypes
from contextlib import ExitStack

import concourse.bass as bass
import concourse.mybir as mybir
from concourse.bass import IndirectOffsetOnAxis
from concourse.bass_utils import run_bass_kernel_spmd

F32 = mybir.dt.float32
BF16 = mybir.dt.bfloat16
I32 = mybir.dt.int32
U32 = mybir.dt.uint32
AF = mybir.ActivationFunctionType
ALU = mybir.AluOpType
AX = mybir.AxisListType

B, S, D = 16, 2048, 1024
E, F = 8, 4096
CAP = 2 * B * S // E
NC = 8
BPC = B // NC
TOWN = BPC * S
NT = TOWN // 128
NCP = D // 64
KT = D // 128
FT = F // 128
NRB = B - BPC
RU = NRB * 4
CHAT = 1280
NQ = CHAT // 128
EXPM = -16.0
ZROW_XN = TOWN
ZROW_EO = E * CHAT


def bf(a):
    return np.asarray(a).astype(ml_dtypes.bfloat16)


class Prog:
    ENGINES = ("sync", "gpsimd", "tensor", "vector", "scalar")

    def __init__(self):
        self.ops = {e: [] for e in self.ENGINES}
        self.cnt = {}
        self.sem_names = []

    def sem(self, name):
        self.sem_names.append(name)
        self.cnt[name] = 0
        return name

    def on(self, eng, build, waits=(), incs=()):
        for s, n in incs:
            self.cnt[s] += n
        self.ops[eng].append((tuple(waits), tuple(incs), build))

    def v(self, s):
        return self.cnt[s]

    def realize(self, block, sems):
        for eng_name in self.ENGINES:
            ops = self.ops[eng_name]

            def runner(e, ops=ops):
                for waits, incs, build in ops:
                    for s, val in waits:
                        if val > 0:
                            e.wait_ge(sems[s], val)
                    ins = build(e) if build is not None else None
                    if incs:
                        assert ins is not None
                        for s, n in incs:
                            ins.then_inc(sems[s], n)

            getattr(block, eng_name)(runner)


def build_program(dev=0, dbg=False):
    nc = bass.Bass("TRN2", target_bir_lowering=False, debug=False)
    p = Prog()

    nt = NT if dev == 0 else min(NT, 4 * dev)
    ng = nt // 4
    n_ru = RU if dev == 0 else min(RU, dev)
    n_exp = E if dev == 0 else min(E, 2 * dev)

    x_own = nc.dram_tensor("x_own", [TOWN, D], F32, kind="ExternalInput").ap()
    x_rem = nc.dram_tensor("x_rem", [NRB * S, D], F32, kind="ExternalInput").ap()
    wq_in = nc.dram_tensor("wq", [D, D], F32, kind="ExternalInput").ap()
    wk_in = nc.dram_tensor("wk", [D, D], F32, kind="ExternalInput").ap()
    wv_in = nc.dram_tensor("wv", [D, D], F32, kind="ExternalInput").ap()
    wo_in = nc.dram_tensor("wo", [D, D], F32, kind="ExternalInput").ap()
    wg_in = nc.dram_tensor("wg", [D, E], F32, kind="ExternalInput").ap()
    wgate_in = nc.dram_tensor("w_gate", [E, D, F], F32, kind="ExternalInput").ap()
    wdown_in = nc.dram_tensor("w_down", [E, F, D], F32, kind="ExternalInput").ap()
    prevcols_in = nc.dram_tensor("prevcols", [128, NRB], F32, kind="ExternalInput").ap()
    y_out = nc.dram_tensor("y", [TOWN, D], F32, kind="ExternalOutput").ap()

    xn_dram = nc.dram_tensor("xn_dram", [TOWN + 1, D], BF16, kind="Internal").ap()
    xrem_bf = nc.dram_tensor("xrem_bf", [NRB * S, D], BF16, kind="Internal").ap()
    wqt_d = nc.dram_tensor("wqt_d", [D, D], BF16, kind="Internal").ap()
    wkt_d = nc.dram_tensor("wkt_d", [D, D], BF16, kind="Internal").ap()
    wvt_d = nc.dram_tensor("wvt_d", [D, D], BF16, kind="Internal").ap()
    wot_d = nc.dram_tensor("wot_d", [D, D], BF16, kind="Internal").ap()
    attn_dram = nc.dram_tensor("attn_dram", [D, TOWN], BF16, kind="Internal").ap()
    disp_dram = nc.dram_tensor("disp_dram", [E * CHAT, D], BF16, kind="Internal").ap()
    eo_dram = nc.dram_tensor("eo_dram", [E * CHAT + 1, D], BF16, kind="Internal").ap()
    glist_dram = nc.dram_tensor("glist_dram", [E * CHAT + 1, 1], I32, kind="Internal").ap()
    thr_dram = nc.dram_tensor("thr_dram", [1, 16], F32, kind="Internal").ap()
    wgate_bf = nc.dram_tensor("wgate_bf", [E, D, F], BF16, kind="Internal").ap()
    wdown_bf = nc.dram_tensor("wdown_bf", [E, F, D], BF16, kind="Internal").ap()
    cs_dram = nc.dram_tensor("cs_dram", [1, E], F32, kind="Internal").ap()

    tri_d = nc.inline_tensor(bf(np.triu(np.ones((128, 128), np.float32), 1)), name="ctri")
    onesc_d = nc.inline_tensor(bf(np.ones((128, 1), np.float32)), name="conesc")
    onesr_d = nc.inline_tensor(bf(np.ones((1, 128), np.float32)), name="conesr")
    ones1f_d = nc.inline_tensor(np.ones((1, 128), np.float32), name="cones1f")
    iota8_d = nc.inline_tensor(
        np.repeat(np.arange(E, dtype=np.float32)[None, :], 128, 0), name="ciota8")
    tokid_d = nc.inline_tensor(
        (np.arange(128, dtype=np.int32)[:, None]
         + 128 * np.arange(NT, dtype=np.int32)[None, :]), name="ctokid")
    glpre_d = nc.inline_tensor(np.full((E * CHAT + 1, 1), ZROW_XN, np.int32), name="cglpre")

    es = ExitStack()
    SB = lambda name, shape, dtype: es.enter_context(nc.sbuf_tensor(name, shape, dtype))
    PSB = lambda name, n=512: es.enter_context(nc.psum_tensor(name, [128, n], F32))
    bankp = [PSB("pbp0", 1024), PSB("pbp1", 1024)]
    bank = [bankp[0][:, 0:512], bankp[0][:, 512:1024],
            bankp[1][:, 0:512], bankp[1][:, 512:1024],
            PSB("pb4"), PSB("pb5"), PSB("pb6"), PSB("pb7")]
    sb_pair_ap = [bankp[0][:, :], bankp[1][:, :]]

    tri_sb = SB("tri_sb", [128, 128], BF16)
    onesc_sb = SB("onesc_sb", [128, 1], BF16)
    onesr_sb = SB("onesr_sb", [1, 128], BF16)
    ones1f_sb = SB("ones1f_sb", [1, 128], F32)
    iota8_sb = SB("iota8_sb", [128, E], F32)
    tokid_sb = SB("tokid_sb", [128, NT], I32)
    wg_sb = SB("wg_sb", [128, KT, E], BF16)
    wgo_sb = SB("wgo_sb", [128, KT, E + 1], BF16)
    csb_sb = SB("csb_sb", [128, E], F32)
    prevc_sb = SB("prevc_sb", [128, NRB], BF16)
    oh1_sb = SB("oh1_sb", [128, NT, E], BF16)
    oh2_sb = SB("oh2_sb", [128, NT, E], BF16)
    g1n_sb = SB("g1n_sb", [128, NT], F32)
    g2n_sb = SB("g2n_sb", [128, NT], F32)
    loc1_sb = SB("loc1_sb", [128, NT], F32)
    loc2_sb = SB("loc2_sb", [128, NT], F32)
    cidx1_sb = SB("cidx1_sb", [128, NT], I32)
    cidx2_sb = SB("cidx2_sb", [128, NT], I32)
    run_sb = SB("run_sb", [1, 24], F32)
    remprev_sb = SB("remprev_sb", [1, 16], F32)
    remall_sb = SB("remall_sb", [1, 8], F32)
    thr1b_sb = SB("thr1b_sb", [128, 8], F32)
    thr2b_sb = SB("thr2b_sb", [128, 8], F32)
    scr_sb = SB("scr_sb", [128, E + 1], F32)
    scr2_sb = SB("scr2_sb", [128, E], F32)
    mx1_sb = SB("mx1_sb", [128, 1], F32)
    mx2_sb = SB("mx2_sb", [128, 1], F32)
    ix_sb = SB("ix_sb", [128, E], U32)
    i1f_sb = SB("i1f_sb", [128, 1], F32)
    i2f_sb = SB("i2f_sb", [128, 1], F32)
    l2_sb = SB("l2_sb", [128, E], F32)
    sstat_sb = SB("sstat_sb", [128, 8], F32)
    em16_sb = SB("em16_sb", [128, 1], F32)
    sc8_sb = SB("sc8_sb", [128, 1], F32)
    eps_sb = SB("eps_sb", [128, 1], F32)
    zrow_sb = SB("zrow_sb", [1, D], BF16)
    cs1_sb = SB("cs1_sb", [1, 8], F32)
    thrr_sb = SB("thrr_sb", [1, 16], F32)
    tmpe_sb = SB("tmpe_sb", [1, 8], F32)
    pos1_sb = SB("pos1_sb", [128, 1], F32)
    pos2_sb = SB("pos2_sb", [128, 1], F32)
    tmpa_sb = SB("tmpa_sb", [128, E], F32)
    tmpn_sb = SB("tmpn_sb", [128, 1], F32)
    tmpq_sb = SB("tmpq_sb", [128, 1], F32)
    tmpw_sb = SB("tmpw_sb", [128, 1], F32)
    tmpi_sb = SB("tmpi_sb", [128, 1], I32)
    scidx_sb = [SB(f"scidx{i}_sb", [128, 1], I32) for i in range(2)]
    tmpc_sb = SB("tmpc_sb", [128, 1], F32)
    tmpd_sb = SB("tmpd_sb", [128, 1], F32)
    dbgrem_sb = SB("dbgrem_sb", [1, RU * 4], BF16) if dbg else None
    ohpr_sb = SB("ohpr_sb", [128, 2 * E], BF16)
    ohpr2_sb = SB("ohpr2_sb", [128, 3 * E], BF16)

    s_const = p.sem("s_const")
    s_wcast = p.sem("s_wcast")
    s_wtr = p.sem("s_wtr")
    s_xl = [p.sem(f"s_xl{i}") for i in range(2)]
    s_xns = [p.sem(f"s_xns{i}") for i in range(2)]
    s_xnready = p.sem("s_xnready")
    s_xnf = [p.sem(f"s_xnf{i}") for i in range(2)]
    s_te = p.sem("s_te")
    s_ve = p.sem("s_ve")
    s_se = p.sem("s_se")
    s_xrc = p.sem("s_xrc")
    s_xrf = [p.sem(f"s_xrf{i}") for i in range(2)]
    s_scat = [p.sem(f"s_scat{i}") for i in range(2)]
    s_glpre = p.sem("s_glpre")
    s_ats = [p.sem(f"s_ats{i}") for i in range(2)]
    s_atready = p.sem("s_atready")
    s_msc = p.sem("s_msc")
    s_glist = p.sem("s_glist")
    s_dg = [p.sem(f"s_dg{i}") for i in range(2)]
    s_dts = [p.sem(f"s_dts{i}") for i in range(2)]
    s_dready = p.sem("s_dready")
    s_dfm = [p.sem(f"s_dfm{i}") for i in range(2)]
    s_wgb = [p.sem(f"s_wgb{i}") for i in range(2)]
    s_wdh = [p.sem(f"s_wdh{i}") for i in range(2)]
    s_eos = [p.sem(f"s_eos{i}") for i in range(2)]
    s_eoready = p.sem("s_eoready")
    s_cmb = [p.sem(f"s_cmb{i}") for i in range(2)]
    s_att = [p.sem(f"s_att{i}") for i in range(2)]
    s_out = [p.sem(f"s_out{i}") for i in range(2)]
    s_bar = p.sem("s_bar")

    W = lambda *pairs: list(pairs)

    def onv(build, waits=(), incs=()):
        p.on("vector", build, waits=W((s_ve, p.v(s_ve)), *waits),
             incs=[(s_ve, 1), *incs])
        return p.v(s_ve)

    def ons(build, waits=(), incs=()):
        p.on("scalar", build, waits=W((s_se, p.v(s_se)), *waits),
             incs=[(s_se, 1), *incs])
        return p.v(s_se)

    def ont(build, waits=(), incs=()):
        p.on("tensor", build, waits=waits, incs=incs)
        return p.v(s_te)

    def barrier():
        tgt = p.v(s_bar) + 5
        for eng in Prog.ENGINES:
            p.on(eng, lambda e: e.nop(), incs=[(s_bar, 1)])
        for eng in Prog.ENGINES:
            p.on(eng, None, waits=W((s_bar, tgt)))

    es1 = ExitStack()
    SB1 = lambda name, shape, dtype: es1.enter_context(nc.sbuf_tensor(name, shape, dtype))
    wqT_sb = SB1("wqT_sb", [128, KT, D], BF16)
    wkT_sb = SB1("wkT_sb", [128, KT, D], BF16)
    wvT_sb = SB1("wvT_sb", [128, KT, D], BF16)
    woT_sb = SB1("woT_sb", [128, KT, D], BF16)
    xld_sb = SB1("xld_sb", [128, 2, D], F32)
    xnt_sb = SB1("xnt_sb", [128, 2, D], BF16)
    xnf_sb = SB1("xnf_sb", [128, 2, KT, 512], BF16)
    qhc_sb = SB1("qhc_sb", [128, 2, 4 * NCP * 128], BF16)
    kpr_sb = SB1("kpr_sb", [128, 2, E, 4 * 128], BF16)
    vpv_sb = SB1("vpv_sb", [128, 2, 4 * NCP * 65], BF16)
    ebuf_sb = SB1("ebuf_sb", [128, 2, 1024], BF16)
    xo_sb = SB1("xo_sb", [128, 2, KT, 512], BF16)
    rcp_sb = SB1("rcp_sb", [1, 512], F32)
    lnr_sb = SB1("lnr_sb", [1, 512], F32)
    rcpb_sb = SB1("rcpb_sb", [64, 512], BF16)
    oev_sb = SB1("oev_sb", [64, 512], F32)
    opj_sb = SB1("opj_sb", [128, 2, 512], BF16)
    xrf_sb = SB1("xrf_sb", [128, 2, KT, 512], BF16)

    for dst, src in ((tri_sb, tri_d), (onesc_sb, onesc_d), (onesr_sb, onesr_d),
                     (ones1f_sb, ones1f_d), (iota8_sb, iota8_d), (tokid_sb, tokid_d)):
        p.on("sync", lambda e, dst=dst, src=src: e.dma_start(dst[:], src.ap()),
             incs=[(s_const, 16)])
    p.on("sync", lambda e: e.dma_start(glist_dram[:], glpre_d.ap()), incs=[(s_glpre, 16)])

    p.on("gpsimd", lambda e: e.dma_start(wqt_d[:], wq_in[:]), incs=[(s_wcast, 16)])
    p.on("gpsimd", lambda e: e.dma_start(wkt_d[:], wk_in[:]), incs=[(s_wcast, 16)])
    p.on("gpsimd", lambda e: e.dma_start(wvt_d[:], wv_in[:]), incs=[(s_wcast, 16)])
    p.on("gpsimd", lambda e: e.dma_start(wot_d[:], wo_in[:]), incs=[(s_wcast, 16)])
    p.on("gpsimd", lambda e: e.dma_start(
        wg_sb[:], wg_in.rearrange("(a p) e -> p a e", p=128)), incs=[(s_wcast, 16)])
    p.on("gpsimd", lambda e: e.dma_start(prevc_sb[:], prevcols_in[:]),
         incs=[(s_wcast, 16)])
    wcast_all = p.v(s_wcast)

    for wsrc, wdst in ((wqt_d, wqT_sb), (wkt_d, wkT_sb), (wvt_d, wvT_sb), (wot_d, woT_sb)):
        for b_ in range(KT):
            p.on("scalar",
                 lambda e, wsrc=wsrc, wdst=wdst, b_=b_: e.dma_start_transpose(
                     wdst[:, b_, :], wsrc[:, b_ * 128:(b_ + 1) * 128]),
                 waits=W((s_wcast, wcast_all)), incs=[(s_wtr, 16)])
    wtr_all = p.v(s_wtr)
    ve_wqs = onv(lambda e: e.tensor_scalar(
        wqT_sb[:].rearrange("p a d -> p (a d)"),
        wqT_sb[:].rearrange("p a d -> p (a d)"), 0.125, None, ALU.mult),
        waits=W((s_wtr, wtr_all)))

    onv(lambda e: e.memset(em16_sb[:], EXPM))
    onv(lambda e: e.memset(sc8_sb[:], 0.125))
    onv(lambda e: e.memset(eps_sb[:], 1e-5))
    onv(lambda e: e.memset(zrow_sb[:], 0.0))
    onv(lambda e: e.memset(run_sb[:], 0.0))
    onv(lambda e: e.memset(remprev_sb[:], 0.0))
    ve_zeros = onv(lambda e: e.memset(remall_sb[:], 0.0))
    p.on("sync", lambda e: e.dma_start(xn_dram[ZROW_XN:ZROW_XN + 1, :], zrow_sb[:]),
         waits=W((s_ve, ve_zeros)), incs=[(s_const, 16)])
    p.on("sync", lambda e: e.dma_start(eo_dram[ZROW_EO:ZROW_EO + 1, :], zrow_sb[:]),
         waits=W((s_ve, ve_zeros)), incs=[(s_const, 16)])

    const6 = p.v(s_const)
    p.on("vector", None, waits=W((s_const, const6)))
    p.on("gpsimd", None, waits=W((s_const, const6)))
    p.on("tensor", None, waits=W((s_const, const6)))
    onv(lambda e: e.tensor_copy(wgo_sb[:, :, 0:E], wg_sb[:]),
        waits=W((s_wcast, wcast_all)))
    ve_wgo = onv(lambda e: e.memset(wgo_sb[:, :, E:E + 1], 1.0))
    for k in range(KT):
        ont(lambda e, k=k: e.matmul(
            bank[7][0:1, 0:E], onesc_sb[:], wg_sb[:, k, :],
            start=(k == 0), stop=(k == KT - 1)),
            waits=(W((s_wcast, wcast_all)) if k == 0 else ()),
            incs=([(s_te, 1)] if k == KT - 1 else ()))
    te_cs = p.v(s_te)
    ve_cs = onv(lambda e: e.tensor_copy(cs1_sb[:], bank[7][0:1, 0:E]),
                waits=W((s_te, te_cs)))
    p.on("sync", lambda e: e.dma_start(cs_dram[:], cs1_sb[:]),
         waits=W((s_ve, ve_cs)), incs=[(s_msc, 16)])
    p.on("sync", lambda e: e.dma_start(csb_sb[:], cs_dram.partition_broadcast(128)),
         waits=W((s_msc, p.v(s_msc))), incs=[(s_msc, 16)])
    msc_csb = p.v(s_msc)
    ve_csb = onv(lambda e: e.tensor_scalar(csb_sb[:], csb_sb[:], 1.0 / D, None, ALU.mult),
                 waits=W((s_msc, msc_csb)))

    n_store = 0
    xfree_ve = [0, 0]
    xns_use = [0, 0]
    for t in range(nt):
        slot = t % 2
        p.on("sync", lambda e, t=t, slot=slot: e.dma_start(
            xld_sb[:, slot, :], x_own[t * 128:(t + 1) * 128, :]),
            waits=W((s_ve, xfree_ve[slot])), incs=[(s_xl[slot], 16)])
        ld_n = p.v(s_xl[slot])
        onv(lambda e, slot=slot: e.reduce_sum(
            sstat_sb[:, 0:1], xld_sb[:, slot, :], axis=AX.X),
            waits=W((s_xl[slot], ld_n)))
        se_sq = ons(lambda e, slot=slot: e.activation(
            xnt_sb[:, slot, :], xld_sb[:, slot, :], AF.Square, bias=0.0, scale=1.0,
            accum_out=sstat_sb[:, 1:2]),
            waits=W((s_xl[slot], ld_n), (s_xns[slot], 16 * xns_use[slot])))
        onv(lambda e: e.tensor_scalar(
            sstat_sb[:, 2:3], sstat_sb[:, 0:1], 1.0 / D, None, ALU.mult))
        onv(lambda e: e.tensor_tensor(
            sstat_sb[:, 3:4], sstat_sb[:, 2:3], sstat_sb[:, 2:3], ALU.mult))
        ve_var = onv(lambda e: e.tensor_scalar(
            sstat_sb[:, 4:5], sstat_sb[:, 1:2], 1.0 / D, sstat_sb[:, 3:4],
            ALU.mult, ALU.subtract),
            waits=W((s_se, se_sq)))
        se_sqrt = ons(lambda e: e.activation(
            sstat_sb[:, 5:6], sstat_sb[:, 4:5], AF.Sqrt, bias=eps_sb[:], scale=1.0),
            waits=W((s_ve, ve_var)))
        onv(lambda e: e.reciprocal(sstat_sb[:, 6:7], sstat_sb[:, 5:6]),
            waits=W((s_se, se_sqrt)))
        ve_xn = onv(lambda e, slot=slot: e.tensor_scalar(
            xnt_sb[:, slot, :], xld_sb[:, slot, :], sstat_sb[:, 2:3],
            sstat_sb[:, 6:7], ALU.subtract, ALU.mult),
            waits=W((s_xns[slot], 16 * xns_use[slot])))
        xfree_ve[slot] = ve_xn
        p.on("sync", lambda e, t=t, slot=slot: e.dma_start(
            xn_dram[t * 128:(t + 1) * 128, :], xnt_sb[:, slot, :]),
            waits=W((s_ve, ve_xn), (s_const, p.v(s_const))), incs=[(s_xns[slot], 16)])
        xns_use[slot] += 1
        n_store += 1
        if t % 4 == 3:
            p.on("sync", lambda e: e.nop(),
                 waits=W((s_xns[0], 16 * xns_use[0]), (s_xns[1], 16 * xns_use[1])),
                 incs=[(s_xnready, 1)])

    n_cast = (n_ru + 1) // 2
    for u in range(n_cast):
        p.on("gpsimd", lambda e, u=u: e.dma_start(
            xrem_bf[u * 1024:(u + 1) * 1024, :], x_rem[u * 1024:(u + 1) * 1024, :]),
            waits=W((s_xrc, 16 * u)), incs=[(s_xrc, 16)])

    s_wpc = p.sem("s_wpc")
    for ei2 in range(n_exp):
        p.on("gpsimd", lambda e, ei2=ei2: e.dma_start(
            wgate_bf[ei2], wgate_in[ei2]),
            waits=W((s_wpc, 32 * ei2)), incs=[(s_wpc, 16)])
        p.on("gpsimd", lambda e, ei2=ei2: e.dma_start(
            wdown_bf[ei2], wdown_in[ei2]),
            waits=W((s_wpc, 32 * ei2 + 16)), incs=[(s_wpc, 16)])

    xrf_total = {u: 16 * KT * (u // 2 + 1) for u in range(n_ru)}
    xrffree_ve = [0, 0]

    def make_remote_transpose_job(u):
        slot = u % 2

        def job():
            for kp in range(KT):
                p.on("sync", lambda e, u=u, kp=kp, slot=slot: e.dma_start_transpose(
                    xrf_sb[:, slot, kp, :],
                    xrem_bf[u * 512:(u + 1) * 512, kp * 128:(kp + 1) * 128]),
                    waits=W((s_xrc, 16 * (u // 2 + 1)), (s_ve, xrffree_ve[slot]),
                            (s_wtr, wtr_all), (s_se, p.v(s_se))),
                    incs=[(s_xrf[slot], 16)])

        return job

    def make_remote_tile_job(u, tl):
        slot = u % 2
        chunk_col = u // 4

        def job():
            for k in range(KT):
                ont(lambda e, k=k: e.matmul(
                    bank[7][:, 0:E + 1],
                    xrf_sb[:, slot, k, tl * 128:(tl + 1) * 128],
                    wgo_sb[:, k, :],
                    start=(k == 0), stop=(k == KT - 1)),
                    waits=(W((s_xrf[slot], xrf_total[u]), (s_ve, p.v(s_ve)),
                             (s_se, p.v(s_se))) if k == 0 else ()),
                    incs=([(s_te, 1)] if k == KT - 1 else ()))
            te_lg = p.v(s_te)
            onv(lambda e: e.tensor_copy(scr_sb[:], bank[7][:, 0:E + 1]),
                waits=W((s_te, te_lg)))
            onv(lambda e: e.tensor_scalar(
                tmpa_sb[:], csb_sb[:], scr_sb[:, E:E + 1], None, ALU.mult),
                waits=W((s_ve, ve_csb)))
            onv(lambda e: e.tensor_tensor(
                scr2_sb[:], scr_sb[:, 0:E], tmpa_sb[:], ALU.subtract))
            onv(lambda e: e.reduce_max(mx1_sb[:], scr2_sb[:], axis=AX.X))
            onv(lambda e: e.max_index(
                ix_sb[:], mx1_sb[:, 0:1].broadcast_to((128, E)), scr2_sb[:]))
            onv(lambda e: e.tensor_copy(i1f_sb[:], ix_sb[:, 0:1]))
            onv(lambda e: e.match_replace(
                l2_sb[:], mx1_sb[:, 0:1].broadcast_to((128, E)), scr2_sb[:], -1e30))
            onv(lambda e: e.reduce_max(mx2_sb[:], l2_sb[:], axis=AX.X))
            onv(lambda e: e.max_index(
                ix_sb[:], mx2_sb[:, 0:1].broadcast_to((128, E)), l2_sb[:]))
            onv(lambda e: e.tensor_copy(i2f_sb[:], ix_sb[:, 0:1]))
            onv(lambda e: e.tensor_scalar(
                ohpr_sb[:, 0:E], iota8_sb[:], i1f_sb[:], None, ALU.is_equal))
            ve_oh = onv(lambda e: e.tensor_scalar(
                ohpr_sb[:, E:2 * E], iota8_sb[:], i2f_sb[:], None, ALU.is_equal))
            ont(lambda e: e.matmul(
                bank[7][0:1, 16:32], prevc_sb[:, chunk_col:chunk_col + 1],
                ohpr_sb[:], start=True, stop=False),
                waits=W((s_ve, ve_oh)))
            ont(lambda e: e.matmul(
                bank[7][0:1, 32:40], onesc_sb[:], ohpr_sb[:, 0:E],
                start=False, stop=True),
                incs=[(s_te, 1)])
            te_cnt = p.v(s_te)
            onv(lambda e: e.tensor_tensor(
                remprev_sb[:], remprev_sb[:], bank[7][0:1, 16:32], ALU.add),
                waits=W((s_te, te_cnt)))
            if dbg:
                def _dbgred(e, u=u, tl=tl):
                    with nc.allow_low_precision("dbg count trace"):
                        return e.reduce_sum(
                            dbgrem_sb[0:1, u * 4 + tl:u * 4 + tl + 1],
                            bank[7][0:1, 32:40], axis=AX.X)
                onv(_dbgred)
            ve_last = onv(lambda e: e.tensor_tensor(
                remall_sb[:], remall_sb[:], bank[7][0:1, 32:40], ALU.add))
            if tl == 3:
                xrffree_ve[slot] = ve_last

        return job

    remote_jobs = []
    for u in range(n_ru):
        remote_jobs.append(make_remote_transpose_job(u))
        for tl in range(4):
            remote_jobs.append(make_remote_tile_job(u, tl))
    remote_iter = iter(remote_jobs)

    def emit_some_remote(nmax):
        for _ in range(nmax):
            job = next(remote_iter, None)
            if job is None:
                return
            job()

    scat_use = [0, 0]

    def emit_own_routing_tile(t, gslot, xnf_ready):
        tl = t % 4
        for k in range(KT):
            ont(lambda e, k=k: e.matmul(
                bank[3][:, 0:E],
                xnf_sb[:, gslot, k, tl * 128:(tl + 1) * 128], wg_sb[:, k, :],
                start=(k == 0), stop=(k == KT - 1)),
                waits=(W((s_xnf[gslot], xnf_ready), (s_ve, p.v(s_ve)),
                         (s_se, p.v(s_se))) if k == 0 else ()),
                incs=([(s_te, 1)] if k == KT - 1 else ()))
        te_lg = p.v(s_te)
        onv(lambda e: e.tensor_copy(scr_sb[:, 0:E], bank[3][:, 0:E]),
            waits=W((s_te, te_lg)))
        onv(lambda e: e.reduce_max(mx1_sb[:], scr_sb[:, 0:E], axis=AX.X))
        onv(lambda e: e.max_index(
            ix_sb[:], mx1_sb[:, 0:1].broadcast_to((128, E)), scr_sb[:, 0:E]))
        onv(lambda e: e.tensor_copy(i1f_sb[:], ix_sb[:, 0:1]))
        onv(lambda e: e.match_replace(
            l2_sb[:], mx1_sb[:, 0:1].broadcast_to((128, E)), scr_sb[:, 0:E], -1e30))
        onv(lambda e: e.reduce_max(mx2_sb[:], l2_sb[:], axis=AX.X))
        onv(lambda e: e.max_index(
            ix_sb[:], mx2_sb[:, 0:1].broadcast_to((128, E)), l2_sb[:]))
        ve_idx = onv(lambda e: e.tensor_copy(i2f_sb[:], ix_sb[:, 0:1]))
        se_g = ons(lambda e, t=t: e.activation(
            g1n_sb[:, t:t + 1], mx2_sb[:], AF.Sigmoid, bias=mx1_sb[:], scale=-1.0),
            waits=W((s_ve, ve_idx)))
        onv(lambda e, t=t: e.tensor_scalar(
            g2n_sb[:, t:t + 1], g1n_sb[:, t:t + 1], -1.0, 1.0, ALU.mult, ALU.add),
            waits=W((s_se, se_g)))
        onv(lambda e, t=t: e.tensor_scalar(
            oh1_sb[:, t, :], iota8_sb[:], i1f_sb[:], None, ALU.is_equal))
        onv(lambda e, t=t: e.tensor_scalar(
            oh2_sb[:, t, :], iota8_sb[:], i2f_sb[:], None, ALU.is_equal))
        onv(lambda e, t=t: e.tensor_copy(ohpr2_sb[:, 0:E], oh1_sb[:, t, :]),
            waits=W((s_te, p.v(s_te))))
        onv(lambda e, t=t: e.tensor_copy(ohpr2_sb[:, E:2 * E], oh2_sb[:, t, :]))
        ve_oh3 = onv(lambda e, t=t: e.tensor_tensor(
            ohpr2_sb[:, 2 * E:3 * E], oh1_sb[:, t, :], oh2_sb[:, t, :], ALU.add))
        ont(lambda e: e.matmul(
            bank[3][:, 32:56], tri_sb[:], ohpr2_sb[:], start=True, stop=False),
            waits=W((s_ve, ve_oh3)))
        ont(lambda e: e.matmul(
            bank[3][:, 32:56], ones1f_sb[:], run_sb[:], start=False, stop=True),
            incs=[(s_te, 1)])
        te_loc = p.v(s_te)
        ont(lambda e: e.matmul(
            bank[2][0:1, 32:56], onesc_sb[:], ohpr2_sb[:], start=True, stop=True),
            incs=[(s_te, 1)])
        te_cs2 = p.v(s_te)
        onv(lambda e, t=t: e.tensor_tensor(
            tmpa_sb[:], bank[3][:, 32:40], oh1_sb[:, t, :], ALU.mult),
            waits=W((s_te, te_loc)))
        onv(lambda e, t=t: e.reduce_sum(loc1_sb[:, t:t + 1], tmpa_sb[:], axis=AX.X))
        onv(lambda e, t=t: e.tensor_tensor(
            tmpa_sb[:], bank[3][:, 40:48], oh2_sb[:, t, :], ALU.mult))
        onv(lambda e, t=t: e.reduce_sum(loc2_sb[:, t:t + 1], tmpa_sb[:], axis=AX.X))
        onv(lambda e, t=t: e.tensor_tensor(
            tmpa_sb[:], bank[3][:, 48:56], oh1_sb[:, t, :], ALU.mult))
        onv(lambda e: e.reduce_sum(pos1_sb[:], tmpa_sb[:], axis=AX.X))
        onv(lambda e, t=t: e.tensor_tensor(
            tmpa_sb[:], bank[3][:, 48:56], oh2_sb[:, t, :], ALU.mult))
        onv(lambda e: e.reduce_sum(pos2_sb[:], tmpa_sb[:], axis=AX.X))
        onv(lambda e: e.tensor_tensor(
            run_sb[:], run_sb[:], bank[2][0:1, 32:56], ALU.add),
            waits=W((s_te, te_cs2)))
        for ch_i, (idxf, posx, cidx) in enumerate(
                ((i1f_sb, pos1_sb, cidx1_sb), (i2f_sb, pos2_sb, cidx2_sb))):
            onv(lambda e, posx=posx: e.tensor_scalar(
                tmpd_sb[:], posx[:], float(CHAT), None, ALU.is_ge))
            onv(lambda e: e.tensor_scalar(
                tmpn_sb[:], tmpd_sb[:], -1.0, 1.0, ALU.mult, ALU.add))
            onv(lambda e, idxf=idxf: e.tensor_scalar(
                tmpc_sb[:], idxf[:], float(CHAT), None, ALU.mult))
            onv(lambda e, posx=posx: e.tensor_tensor(
                tmpc_sb[:], tmpc_sb[:], posx[:], ALU.add))
            onv(lambda e: e.tensor_tensor(tmpc_sb[:], tmpc_sb[:], tmpn_sb[:], ALU.mult))
            onv(lambda e: e.scalar_tensor_tensor(
                tmpc_sb[:], tmpd_sb[:], float(E * CHAT), tmpc_sb[:], ALU.mult, ALU.add))
            ve_sc = onv(lambda e, ch_i=ch_i: e.tensor_copy(
                scidx_sb[ch_i][:], tmpc_sb[:]),
                waits=W((s_scat[ch_i], 16 * scat_use[ch_i])))
            scat_use[ch_i] += 1
            p.on("gpsimd", lambda e, t=t, ch_i=ch_i: e.indirect_dma_start(
                glist_dram[:],
                IndirectOffsetOnAxis(scidx_sb[ch_i][:], 0),
                tokid_sb[:, t:t + 1], None),
                waits=W((s_ve, ve_sc), (s_glpre, 16)), incs=[(s_scat[ch_i], 16)])
            onv(lambda e, posx=posx: e.tensor_scalar(
                tmpc_sb[:], posx[:], 1.0 / NQ, 0.5 / NQ, ALU.mult, ALU.add))
            onv(lambda e: e.tensor_copy(tmpi_sb[:], tmpc_sb[:]))
            onv(lambda e: e.tensor_copy(tmpc_sb[:], tmpi_sb[:]))
            onv(lambda e, posx=posx: e.scalar_tensor_tensor(
                tmpq_sb[:], tmpc_sb[:], -float(NQ), posx[:], ALU.mult, ALU.add))
            onv(lambda e: e.tensor_scalar(tmpw_sb[:], tmpq_sb[:], 0.0, None, ALU.is_lt))
            onv(lambda e: e.tensor_tensor(tmpc_sb[:], tmpc_sb[:], tmpw_sb[:], ALU.subtract))
            onv(lambda e: e.scalar_tensor_tensor(
                tmpq_sb[:], tmpw_sb[:], float(NQ), tmpq_sb[:], ALU.mult, ALU.add))
            onv(lambda e: e.tensor_scalar(tmpw_sb[:], tmpq_sb[:], float(NQ), None, ALU.is_ge))
            onv(lambda e: e.tensor_tensor(tmpc_sb[:], tmpc_sb[:], tmpw_sb[:], ALU.add))
            onv(lambda e: e.scalar_tensor_tensor(
                tmpq_sb[:], tmpw_sb[:], -float(NQ), tmpq_sb[:], ALU.mult, ALU.add))
            onv(lambda e: e.tensor_scalar(
                tmpq_sb[:], tmpq_sb[:], 128.0, None, ALU.mult))
            onv(lambda e: e.tensor_tensor(tmpq_sb[:], tmpq_sb[:], tmpc_sb[:], ALU.add))
            onv(lambda e, idxf=idxf: e.tensor_scalar(
                tmpc_sb[:], idxf[:], float(CHAT), None, ALU.mult))
            onv(lambda e: e.tensor_tensor(tmpq_sb[:], tmpq_sb[:], tmpc_sb[:], ALU.add))
            onv(lambda e: e.tensor_tensor(tmpq_sb[:], tmpq_sb[:], tmpn_sb[:], ALU.mult))
            onv(lambda e: e.scalar_tensor_tensor(
                tmpq_sb[:], tmpd_sb[:], float(ZROW_EO), tmpq_sb[:], ALU.mult, ALU.add))
            onv(lambda e, t=t, cidx=cidx: e.tensor_copy(cidx[:, t:t + 1], tmpq_sb[:]))

    xnffree_te = [0, 0]
    for g in range(ng):
        gslot = g % 2
        for kp in range(KT):
            p.on("sync", lambda e, g=g, kp=kp, gslot=gslot: e.dma_start_transpose(
                xnf_sb[:, gslot, kp, :],
                xn_dram[g * 512:(g + 1) * 512, kp * 128:(kp + 1) * 128]),
                waits=W((s_xnready, g + 1), (s_te, xnffree_te[gslot]),
                        (s_wtr, wtr_all), (s_se, p.v(s_se))),
                incs=[(s_xnf[gslot], 16)])
        xnf_ready = p.v(s_xnf[gslot])

        for a in range(NCP // 2):
            for k in range(KT):
                ont(lambda e, a=a, k=k, gslot=gslot: e.matmul(
                    bank[0][0:64, :],
                    wqT_sb[:, k, (2 * a) * 64:(2 * a) * 64 + 64],
                    xnf_sb[:, gslot, k, :],
                    start=(k == 0), stop=(k == KT - 1), tile_position=(0, 0)),
                    waits=(W((s_xnf[gslot], xnf_ready), (s_ve, p.v(s_ve)))
                           if k == 0 else ()))
                ont(lambda e, a=a, k=k, gslot=gslot: e.matmul(
                    bank[1][64:128, :],
                    wqT_sb[:, k, (2 * a) * 64:(2 * a) * 64 + 64],
                    xnf_sb[:, gslot, k, :],
                    start=(k == 0), stop=(k == KT - 1), tile_position=(0, 64)))
                ont(lambda e, a=a, k=k, gslot=gslot: e.matmul(
                    bank[2][0:64, :],
                    wqT_sb[:, k, (2 * a + 1) * 64:(2 * a + 1) * 64 + 64],
                    xnf_sb[:, gslot, k, :],
                    start=(k == 0), stop=(k == KT - 1), tile_position=(0, 0)))
                ont(lambda e, a=a, k=k, gslot=gslot: e.matmul(
                    bank[3][64:128, :],
                    wqT_sb[:, k, (2 * a + 1) * 64:(2 * a + 1) * 64 + 64],
                    xnf_sb[:, gslot, k, :],
                    start=(k == 0), stop=(k == KT - 1), tile_position=(0, 64)),
                    incs=([(s_te, 1)] if k == KT - 1 else ()))
            te_q = p.v(s_te)
            for cc, bnklo, bnkhi in ((2 * a, bank[0], bank[1]),
                                     (2 * a + 1, bank[2], bank[3])):
                onv(lambda e, cc=cc, bnklo=bnklo, gslot=gslot: e.tensor_copy(
                    qhc_sb[:, gslot, :].rearrange("p (h c j) -> p h c j",
                                                  h=4, c=NCP)[0:64, :, cc, :],
                    bnklo[0:64, :].rearrange("p (h j) -> p h j", h=4)),
                    waits=W((s_te, te_q)))
                onv(lambda e, cc=cc, bnkhi=bnkhi, gslot=gslot: e.tensor_copy(
                    qhc_sb[:, gslot, :].rearrange("p (h c j) -> p h c j",
                                                  h=4, c=NCP)[64:128, :, cc, :],
                    bnkhi[64:128, :].rearrange("p (h j) -> p h j", h=4)))

        for a in range(NCP // 2):
            for k in range(KT):
                ont(lambda e, a=a, k=k, gslot=gslot: e.matmul(
                    bank[0][0:64, :],
                    wkT_sb[:, k, (2 * a) * 64:(2 * a) * 64 + 64],
                    xnf_sb[:, gslot, k, :],
                    start=(k == 0), stop=(k == KT - 1), tile_position=(0, 0)),
                    waits=(W((s_ve, p.v(s_ve)),) if k == 0 else ()))
                ont(lambda e, a=a, k=k, gslot=gslot: e.matmul(
                    bank[1][64:128, :],
                    wkT_sb[:, k, (2 * a + 1) * 64:(2 * a + 1) * 64 + 64],
                    xnf_sb[:, gslot, k, :],
                    start=(k == 0), stop=(k == KT - 1), tile_position=(0, 64)),
                    incs=([(s_te, 1)] if k == KT - 1 else ()))
            te_k = p.v(s_te)
            onv(lambda e, a=a, gslot=gslot: e.tensor_copy(
                kpr_sb[0:64, gslot, a, :], bank[0][0:64, :]),
                waits=W((s_te, te_k)))
            onv(lambda e, a=a, gslot=gslot: e.tensor_copy(
                kpr_sb[64:128, gslot, a, :], bank[1][64:128, :]))

        onv(lambda e, gslot=gslot: e.memset(vpv_sb[:, gslot, :], 1.0),
            waits=W((s_te, p.v(s_te))))
        for tt in range(4):
            for oc in range(2):
                bnk = bank[2 + oc]
                for k in range(KT):
                    ont(lambda e, tt=tt, oc=oc, k=k, gslot=gslot, bnk=bnk: e.matmul(
                        bnk[:, :],
                        xnf_sb[:, gslot, k, tt * 128:(tt + 1) * 128],
                        wvT_sb[:, k, oc * 512:(oc + 1) * 512],
                        start=(k == 0), stop=(k == KT - 1)),
                        waits=(W((s_ve, p.v(s_ve)),) if k == 0 else ()),
                        incs=([(s_te, 1)] if k == KT - 1 else ()))
                te_v = p.v(s_te)
                onv(lambda e, tt=tt, oc=oc, gslot=gslot, bnk=bnk: e.tensor_copy(
                    vpv_sb[:, gslot, :].rearrange("p (h c w) -> p h c w",
                                                  h=4, c=NCP)[:, tt, oc * 8:(oc + 1) * 8, 0:64],
                    bnk[:, :].rearrange("p (c w) -> p c w", c=8)),
                    waits=W((s_te, te_v)))

        for tl in range(4):
            emit_own_routing_tile(g * 4 + tl, gslot, xnf_ready)
        xnffree_te[gslot] = p.v(s_te)

        for h in range(4):
            for ch in range(4):
                obank = bank[4 + (ch % 2)]
                for a in range(NCP // 2):
                    pair = a % 2
                    sb0, sb1 = bank[2 * pair], bank[2 * pair + 1]
                    ont(lambda e, a=a, h=h, ch=ch, gslot=gslot, sb0=sb0: e.matmul(
                        sb0[:, :],
                        kpr_sb[0:64, gslot, a, h * 128:(h + 1) * 128],
                        qhc_sb[0:64, gslot, h * 2048 + ch * 512:h * 2048 + (ch + 1) * 512],
                        start=True, stop=True, tile_position=(0, 0)),
                        waits=W((s_se, p.v(s_se)), (s_ve, p.v(s_ve))),
                        incs=[(s_te, 1)])
                    ont(lambda e, a=a, h=h, ch=ch, gslot=gslot, sb1=sb1: e.matmul(
                        sb1[:, :],
                        kpr_sb[64:128, gslot, a, h * 128:(h + 1) * 128],
                        qhc_sb[64:128, gslot, h * 2048 + ch * 512:h * 2048 + (ch + 1) * 512],
                        start=True, stop=True, tile_position=(64, 0)),
                        incs=[(s_te, 1)])
                    te_s1 = p.v(s_te)
                    ons(lambda e, pair=pair, a=a: e.activation(
                        ebuf_sb[:, a % 2, :], sb_pair_ap[pair],
                        AF.Exp, bias=0.0, scale=1.0),
                        waits=W((s_te, te_s1)))
                    se_e1 = p.v(s_se)
                    ont(lambda e, a=a, h=h, gslot=gslot, obank=obank: e.matmul(
                        obank[0:65, :],
                        vpv_sb[:, gslot, (h * NCP + 2 * a) * 65:(h * NCP + 2 * a) * 65 + 65],
                        ebuf_sb[:, a % 2, 0:512],
                        start=(a == 0), stop=False),
                        waits=(W((s_se, se_e1), (s_ve, p.v(s_ve)))
                               if a == 0 else W((s_se, se_e1))))
                    ont(lambda e, a=a, h=h, gslot=gslot, obank=obank: e.matmul(
                        obank[0:65, :],
                        vpv_sb[:, gslot, (h * NCP + 2 * a + 1) * 65:(h * NCP + 2 * a + 1) * 65 + 65],
                        ebuf_sb[:, a % 2, 512:1024],
                        start=False, stop=(a == NCP // 2 - 1)),
                        incs=([(s_te, 1)] if a == NCP // 2 - 1 else ()))
                te_o = p.v(s_te)
                ons(lambda e, obank=obank: e.activation(
                    lnr_sb[:], obank[64:65, :], AF.Ln, bias=0.0, scale=1.0),
                    waits=W((s_te, te_o)))
                se_rc = ons(lambda e: e.activation(
                    rcp_sb[:], lnr_sb[:], AF.Exp, bias=0.0, scale=-1.0))
                ont(lambda e: e.matmul(
                    bank[6][0:64, :], ones1f_sb[0:1, 0:64], rcp_sb[:],
                    start=True, stop=True),
                    waits=W((s_se, se_rc)), incs=[(s_te, 1)])
                te_rb = p.v(s_te)
                onv(lambda e: e.tensor_copy(rcpb_sb[:], bank[6][0:64, :]),
                    waits=W((s_te, te_rb)))
                onv(lambda e, obank=obank: e.tensor_copy(oev_sb[:], obank[0:64, :]))
                for ci in range(4):
                    cplane = ch * 4 + ci
                    ptile, phalf = cplane // 2, cplane % 2
                    onv(lambda e, ci=ci, ptile=ptile, phalf=phalf, h=h, gslot=gslot:
                        e.tensor_tensor(
                            xo_sb[64 * phalf:64 * phalf + 64, gslot, ptile,
                                  h * 128:(h + 1) * 128],
                            oev_sb[:, ci * 128:(ci + 1) * 128],
                            rcpb_sb[:, ci * 128:(ci + 1) * 128], ALU.mult))
                emit_some_remote(2)

        for mt in range(KT):
            pbank = bank[mt % 2]
            for k in range(KT):
                ont(lambda e, mt=mt, k=k, gslot=gslot, pbank=pbank: e.matmul(
                    pbank[:, :],
                    woT_sb[:, k, mt * 128:(mt + 1) * 128],
                    xo_sb[:, gslot, k, :],
                    start=(k == 0), stop=(k == KT - 1)),
                    waits=(W((s_ve, p.v(s_ve)), (s_se, p.v(s_se))) if k == 0 else ()),
                    incs=([(s_te, 1)] if k == KT - 1 else ()))
            te_op = p.v(s_te)
            ve_opj = onv(lambda e, mt=mt, pbank=pbank: e.tensor_copy(
                opj_sb[:, mt % 2, :], pbank[:, :]),
                waits=W((s_te, te_op), (s_ats[mt % 2], p.v(s_ats[mt % 2]))))
            p.on("sync", lambda e, g=g, mt=mt: e.dma_start(
                attn_dram[mt * 128:(mt + 1) * 128, g * 512:(g + 1) * 512],
                opj_sb[:, mt % 2, :]),
                waits=W((s_ve, ve_opj)), incs=[(s_ats[mt % 2], 16)])
        p.on("sync", lambda e: e.nop(),
             waits=W((s_ats[0], p.v(s_ats[0])), (s_ats[1], p.v(s_ats[1]))),
             incs=[(s_atready, 1)])

    emit_some_remote(10 ** 9)
    atready_all = p.v(s_atready)

    onv(lambda e: e.tensor_scalar(
        thrr_sb[:, 0:8], remprev_sb[:, 0:8], -1.0, float(CAP), ALU.mult, ALU.add))
    onv(lambda e: e.tensor_tensor(tmpe_sb[:], remall_sb[:], run_sb[:, 0:8], ALU.add))
    onv(lambda e: e.tensor_tensor(tmpe_sb[:], tmpe_sb[:], remprev_sb[:, 8:16], ALU.add))
    ve_thr = onv(lambda e: e.tensor_scalar(
        thrr_sb[:, 8:16], tmpe_sb[:], -1.0, float(CAP), ALU.mult, ALU.add))
    p.on("sync", lambda e: e.dma_start(thr_dram[:], thrr_sb[:]),
         waits=W((s_ve, ve_thr)), incs=[(s_msc, 16)])
    p.on("sync", lambda e: e.dma_start(
        thr1b_sb[:], thr_dram[0:1, 0:8].partition_broadcast(128)),
        waits=W((s_msc, p.v(s_msc))), incs=[(s_msc, 16)])
    p.on("sync", lambda e: e.dma_start(
        thr2b_sb[:], thr_dram[0:1, 8:16].partition_broadcast(128)),
        waits=W((s_msc, p.v(s_msc))), incs=[(s_msc, 16)])
    msc_thr = p.v(s_msc)

    barrier()
    es1.close()

    es2 = ExitStack()
    SB2 = lambda name, shape, dtype: es2.enter_context(nc.sbuf_tensor(name, shape, dtype))
    glist_sb = SB2("glist_sb", [128, 2, NQ], I32)
    dtm_sb = SB2("dtm_sb", [128, 2, D], BF16)
    dfm_sb = SB2("dfm_sb", [128, 2, KT, CHAT], BF16)
    wgb_sb = SB2("wgb_sb", [128, 2, KT, 256], BF16)
    wdh_sb = SB2("wdh_sb", [128, 2, FT, 512], BF16)
    hfm_sb = SB2("hfm_sb", [128, FT, CHAT], BF16)
    eo_sb = SB2("eo_sb", [128, 2, 512], BF16)


    CH1 = (512, 512, 256)
    dtm_use = [0, 0]
    prev_dg = (0, 0)
    eo_use = [0, 0]
    wgbfree_se = [0, 0]
    wdhfree_ve = [0, 0]
    dfmfree_se = [0, 0]
    gelu_cnt = {0: 0, 1: 0, 2: 0}
    for ei in range(n_exp):
        eslot = ei % 2
        p.on("sync", lambda e, ei=ei, eslot=eslot: e.dma_start(
            glist_sb[:, eslot, :],
            glist_dram[ei * CHAT:(ei + 1) * CHAT, 0].rearrange("(p q) -> p q", q=NQ)),
            waits=W((s_scat[0], 16 * scat_use[0]), (s_scat[1], 16 * scat_use[1]),
                    (s_glist, 16 * ei),
                    (s_dg[0], prev_dg[0]), (s_dg[1], prev_dg[1])),
            incs=[(s_glist, 16)])
        gl_n = p.v(s_glist)
        for q in range(NQ):
            dslot = q % 2
            p.on("gpsimd", lambda e, q=q, dslot=dslot, eslot=eslot: e.indirect_dma_start(
                dtm_sb[:, dslot, :], None, xn_dram[:],
                IndirectOffsetOnAxis(glist_sb[:, eslot, q:q + 1], 0),
                ),
                waits=W((s_glist, gl_n), (s_dts[dslot], 16 * dtm_use[dslot]),
                        (s_xns[0], 16 * xns_use[0]), (s_xns[1], 16 * xns_use[1]),
                        (s_const, p.v(s_const))),
                incs=[(s_dg[dslot], 16)])
            p.on("sync", lambda e, ei=ei, q=q, dslot=dslot: e.dma_start(
                disp_dram[ei * CHAT + q * 128:ei * CHAT + (q + 1) * 128, :],
                dtm_sb[:, dslot, :]),
                waits=W((s_dg[dslot], p.v(s_dg[dslot]))),
                incs=[(s_dts[dslot], 16)])
            dtm_use[dslot] += 1
        prev_dg = (p.v(s_dg[0]), p.v(s_dg[1]))
        p.on("sync", lambda e: e.nop(),
             waits=W((s_dts[0], p.v(s_dts[0])), (s_dts[1], p.v(s_dts[1]))),
             incs=[(s_dready, 1)])
        for kp in range(KT):
            p.on("sync", lambda e, ei=ei, kp=kp, eslot=eslot: e.dma_start_transpose(
                dfm_sb[:, eslot, kp, :],
                disp_dram[ei * CHAT:(ei + 1) * CHAT, kp * 128:(kp + 1) * 128]),
                waits=W((s_dready, ei + 1), (s_se, dfmfree_se[eslot]),
                        (s_se, p.v(s_se))),
                incs=[(s_dfm[eslot], 16)])
        dfm_ready = p.v(s_dfm[eslot])

        for fg in range(FT // 2):
            wslot = fg % 2
            p.on("scalar", lambda e, ei=ei, fg=fg, wslot=wslot: e.dma_start(
                wgb_sb[:, wslot, :, :],
                wgate_bf[ei, :, fg * 256:(fg + 1) * 256].rearrange(
                    "(a p) f -> p a f", p=128)),
                waits=W((s_se, wgbfree_se[wslot]), (s_wpc, 32 * ei + 16)),
                incs=[(s_wgb[wslot], 16)])
            wgb_n = p.v(s_wgb[wslot])
            for fi in range(2):
                ft = fg * 2 + fi
                co = 0
                for ci, cw in enumerate(CH1):
                    bnk = bank[ci]
                    for k in range(KT):
                        ont(lambda e, k=k, cw=cw, co=co, fi=fi, wslot=wslot,
                            eslot=eslot, bnk=bnk: e.matmul(
                                bnk[:, 0:cw],
                                wgb_sb[:, wslot, k, fi * 128:(fi + 1) * 128],
                                dfm_sb[:, eslot, k, co:co + cw],
                                start=(k == 0), stop=(k == KT - 1)),
                            waits=(W((s_wgb[wslot], wgb_n), (s_dfm[eslot], dfm_ready),
                                     (s_se, gelu_cnt[ci])) if k == 0 else ()),
                            incs=([(s_te, 1)] if k == KT - 1 else ()))
                    te_m1 = p.v(s_te)
                    gelu_cnt[ci] = ons(lambda e, ft=ft, cw=cw, co=co, bnk=bnk:
                                       e.activation(
                                           hfm_sb[:, ft, co:co + cw], bnk[:, 0:cw],
                                           AF.Gelu, bias=0.0, scale=1.0),
                                       waits=W((s_te, te_m1)))
                    if fi == 1 and ci == len(CH1) - 1:
                        wgbfree_se[wslot] = gelu_cnt[ci]
                    co += cw
        se_h = p.v(s_se)

        for dh_ in range(2):
            wdslot = dh_
            wduse = ei * 2 + dh_
            for part in range(2):
                p.on("scalar", lambda e, ei=ei, dh_=dh_, part=part, wdslot=wdslot:
                     e.dma_start(
                         wdh_sb[:, wdslot, part * 16:(part + 1) * 16, :],
                         wdown_bf[ei, part * 2048:(part + 1) * 2048,
                                  dh_ * 512:(dh_ + 1) * 512].rearrange(
                             "(a p) d -> p a d", p=128)),
                     waits=W((s_ve, wdhfree_ve[wdslot]), (s_wpc, 32 * (ei + 1))),
                     incs=[(s_wdh[wdslot], 16)])
            wdh_n = p.v(s_wdh[wdslot])
            for rt in range(NQ):
                bnk = bank[3 + rt % 2]
                for k in range(FT):
                    ont(lambda e, rt=rt, k=k, wdslot=wdslot, bnk=bnk: e.matmul(
                        bnk[:, :],
                        hfm_sb[:, k, rt * 128:(rt + 1) * 128],
                        wdh_sb[:, wdslot, k, :],
                        start=(k == 0), stop=(k == FT - 1)),
                        waits=(W((s_wdh[wdslot], wdh_n), (s_se, se_h),
                                 (s_ve, p.v(s_ve))) if k == 0 else ()),
                        incs=([(s_te, 1)] if k == FT - 1 else ()))
                te_m2 = p.v(s_te)
                eslot2 = rt % 2
                ve_eo = onv(lambda e, bnk=bnk, eslot2=eslot2: e.tensor_copy(
                    eo_sb[:, eslot2, 0:512], bnk[:, :]),
                    waits=W((s_te, te_m2), (s_eos[eslot2], 16 * eo_use[eslot2])))
                if rt == NQ - 1:
                    wdhfree_ve[wdslot] = ve_eo
                p.on("sync", lambda e, ei=ei, rt=rt, dh_=dh_, eslot2=eslot2: e.dma_start(
                    eo_dram[ei * CHAT + rt * 128:ei * CHAT + (rt + 1) * 128,
                            dh_ * 512:(dh_ + 1) * 512],
                    eo_sb[:, eslot2, 0:512]),
                    waits=W((s_ve, ve_eo)), incs=[(s_eos[eslot2], 16)])
                eo_use[eslot2] += 1
        dfmfree_se[eslot] = se_h
    p.on("sync", lambda e: e.nop(),
         waits=W((s_eos[0], p.v(s_eos[0])), (s_eos[1], p.v(s_eos[1]))),
         incs=[(s_eoready, 1)])
    eoready = p.v(s_eoready)

    barrier()
    es2.close()
    es3 = ExitStack()
    SB3 = lambda name, shape, dtype: es3.enter_context(nc.sbuf_tensor(name, shape, dtype))
    cgA_sb = SB3("cgA_sb", [128, 2, D], BF16)
    cgB_sb = SB3("cgB_sb", [128, 2, D], BF16)
    att_sb = SB3("att_sb", [128, 2, D], BF16)
    xr2_sb = SB3("xr2_sb", [128, 2, D], F32)
    outb_sb = SB3("outb_sb", [128, 2, D], F32)
    ga_sb = SB3("ga_sb", [128, 1], F32)
    gb_sb = SB3("gb_sb", [128, 1], F32)
    cmbfree_ve = [0, 0]
    attfree_ve = [0, 0]
    out_use = [0, 0]
    for t in range(nt):
        slot = t % 2
        p.on("gpsimd", lambda e, t=t, slot=slot: e.indirect_dma_start(
            cgA_sb[:, slot, :], None, eo_dram[:],
            IndirectOffsetOnAxis(cidx1_sb[:, t:t + 1], 0)),
            waits=W((s_eoready, eoready), (s_ve, cmbfree_ve[slot]),
                    (s_const, p.v(s_const))),
            incs=[(s_cmb[slot], 16)])
        p.on("gpsimd", lambda e, t=t, slot=slot: e.indirect_dma_start(
            cgB_sb[:, slot, :], None, eo_dram[:],
            IndirectOffsetOnAxis(cidx2_sb[:, t:t + 1], 0)),
            incs=[(s_cmb[slot], 16)])
        p.on("sync", lambda e, t=t, slot=slot: e.dma_start_transpose(
            att_sb[:, slot, :], attn_dram[:, t * 128:(t + 1) * 128]),
            waits=W((s_atready, atready_all), (s_ve, attfree_ve[slot]),
                    (s_se, p.v(s_se))),
            incs=[(s_att[slot], 16)])
        p.on("sync", lambda e, t=t, slot=slot: e.dma_start(
            xr2_sb[:, slot, :], x_own[t * 128:(t + 1) * 128, :]),
            waits=W((s_ve, attfree_ve[slot])),
            incs=[(s_att[slot], 16)])
        cmb_n = p.v(s_cmb[slot])
        att_n = p.v(s_att[slot])
        onv(lambda e, t=t: e.tensor_tensor(
            tmpa_sb[:], thr1b_sb[:], oh1_sb[:, t, :], ALU.mult),
            waits=W((s_msc, msc_thr)))
        onv(lambda e: e.reduce_sum(tmpc_sb[:], tmpa_sb[:], axis=AX.X))
        onv(lambda e, t=t: e.tensor_tensor(
            tmpd_sb[:], loc1_sb[:, t:t + 1], tmpc_sb[:], ALU.is_lt))
        onv(lambda e, t=t: e.tensor_tensor(
            ga_sb[:], g1n_sb[:, t:t + 1], tmpd_sb[:], ALU.mult))
        onv(lambda e, t=t: e.tensor_tensor(
            tmpa_sb[:], thr2b_sb[:], oh2_sb[:, t, :], ALU.mult))
        onv(lambda e: e.reduce_sum(tmpc_sb[:], tmpa_sb[:], axis=AX.X))
        onv(lambda e, t=t: e.tensor_tensor(
            tmpd_sb[:], loc2_sb[:, t:t + 1], tmpc_sb[:], ALU.is_lt))
        onv(lambda e, t=t: e.tensor_tensor(
            gb_sb[:], g2n_sb[:, t:t + 1], tmpd_sb[:], ALU.mult))
        onv(lambda e, slot=slot: e.tensor_scalar(
            outb_sb[:, slot, :], cgA_sb[:, slot, :], ga_sb[:], None, ALU.mult),
            waits=W((s_cmb[slot], cmb_n), (s_out[slot], 16 * out_use[slot])))
        onv(lambda e, slot=slot: e.scalar_tensor_tensor(
            outb_sb[:, slot, :], cgB_sb[:, slot, :], gb_sb[:], outb_sb[:, slot, :],
            ALU.mult, ALU.add))
        cmbfree_ve[slot] = onv(lambda e, slot=slot: e.tensor_tensor(
            outb_sb[:, slot, :], outb_sb[:, slot, :], att_sb[:, slot, :], ALU.add),
            waits=W((s_att[slot], att_n)))
        ve_out = onv(lambda e, slot=slot: e.tensor_tensor(
            outb_sb[:, slot, :], outb_sb[:, slot, :], xr2_sb[:, slot, :], ALU.add))
        attfree_ve[slot] = ve_out
        p.on("sync", lambda e, t=t, slot=slot: e.dma_start(
            y_out[t * 128:(t + 1) * 128, :], outb_sb[:, slot, :]),
            waits=W((s_ve, ve_out)), incs=[(s_out[slot], 16)])
        out_use[slot] += 1
    p.on("sync", None, waits=W((s_out[0], p.v(s_out[0])), (s_out[1], p.v(s_out[1]))))

    if dbg:
        dbg_xn = nc.dram_tensor("dbg_xn", [TOWN + 1, D], BF16, kind="ExternalOutput").ap()
        dbg_attn = nc.dram_tensor("dbg_attn", [D, TOWN], BF16, kind="ExternalOutput").ap()
        dbg_gl = nc.dram_tensor("dbg_gl", [E * CHAT + 1, 1], I32, kind="ExternalOutput").ap()
        dbg_eo = nc.dram_tensor("dbg_eo", [E * CHAT + 1, D], BF16, kind="ExternalOutput").ap()
        dbg_thr = nc.dram_tensor("dbg_thr", [1, 16], F32, kind="ExternalOutput").ap()
        dbg_rt = nc.dram_tensor("dbg_rt", [128, 8 * NT], F32, kind="ExternalOutput").ap()
        dbg_cnt = nc.dram_tensor("dbg_cnt", [1, 48], F32, kind="ExternalOutput").ap()
        dbg_disp = nc.dram_tensor("dbg_disp", [E * CHAT, D], BF16, kind="ExternalOutput").ap()
        s_dbg = p.sem("s_dbg")
        rtbuf = es3.enter_context(nc.sbuf_tensor("rtbuf", [128, 8 * NT], F32))
        ve0 = p.v(s_ve)
        onv(lambda e: e.tensor_copy(rtbuf[:, 0:NT], g1n_sb[:]))
        onv(lambda e: e.tensor_copy(rtbuf[:, NT:2 * NT], g2n_sb[:]))
        onv(lambda e: e.tensor_copy(rtbuf[:, 2 * NT:3 * NT], loc1_sb[:]))
        onv(lambda e: e.tensor_copy(rtbuf[:, 3 * NT:4 * NT], loc2_sb[:]))
        onv(lambda e: e.tensor_copy(rtbuf[:, 4 * NT:5 * NT], cidx1_sb[:]))
        onv(lambda e: e.tensor_copy(rtbuf[:, 5 * NT:6 * NT], cidx2_sb[:]))
        onv(lambda e: e.tensor_copy(rtbuf[:, 6 * NT:7 * NT], g1n_sb[:]))
        ved = onv(lambda e: e.tensor_copy(rtbuf[:, 7 * NT:8 * NT], g2n_sb[:]))
        p.on("sync", lambda e: e.dma_start(dbg_xn[:], xn_dram[:]),
             waits=W((s_out[0], p.v(s_out[0])), (s_out[1], p.v(s_out[1])),
                     (s_ve, ved)),
             incs=[(s_dbg, 16)])
        p.on("sync", lambda e: e.dma_start(dbg_attn[:], attn_dram[:]), incs=[(s_dbg, 16)])
        p.on("sync", lambda e: e.dma_start(dbg_gl[:], glist_dram[:]), incs=[(s_dbg, 16)])
        p.on("sync", lambda e: e.dma_start(dbg_eo[:], eo_dram[:]), incs=[(s_dbg, 16)])
        p.on("sync", lambda e: e.dma_start(dbg_thr[:], thr_dram[:]), incs=[(s_dbg, 16)])
        p.on("sync", lambda e: e.dma_start(dbg_rt[:], rtbuf[:]), incs=[(s_dbg, 16)])
        dbg_rem = nc.dram_tensor("dbg_rem", [1, RU * 4], BF16, kind="ExternalOutput").ap()
        def _dbgcp(e):
            with nc.allow_low_precision("dbg"):
                return e.tensor_copy(dbgrem_sb[:], dbgrem_sb[:])
        vrem = onv(_dbgcp)
        p.on("sync", lambda e: e.dma_start(dbg_rem[:], dbgrem_sb[:]),
             waits=W((s_ve, vrem)), incs=[(s_dbg, 16)])
        cntbuf = es3.enter_context(nc.sbuf_tensor("cntbuf", [1, 48], F32))
        onv(lambda e: e.tensor_copy(cntbuf[:, 0:16], remprev_sb[:]))
        onv(lambda e: e.tensor_copy(cntbuf[:, 16:24], remall_sb[:]))
        vec = onv(lambda e: e.tensor_copy(cntbuf[:, 24:48], run_sb[:]))
        p.on("sync", lambda e: e.dma_start(dbg_cnt[:], cntbuf[:]),
             waits=W((s_ve, vec)), incs=[(s_dbg, 16)])
        p.on("sync", lambda e: e.dma_start(dbg_disp[:], disp_dram[:]), incs=[(s_dbg, 16)])
        p.on("sync", None, waits=W((s_dbg, p.v(s_dbg))))

    sems = {}
    for name in p.sem_names:
        sems[name] = es.enter_context(nc.semaphore(name))
    block = es.enter_context(nc.Block())
    p.realize(block, sems)
    es3.close()
    es.close()
    return nc


def make_in_maps(x, wq, wk, wv, wo, wg, w_gate, w_down):
    in_maps = []
    for c in range(NC):
        own = np.ascontiguousarray(x[c * BPC:(c + 1) * BPC].reshape(TOWN, D))
        rem = np.ascontiguousarray(
            np.concatenate([x[b] for b in range(B) if b // BPC != c], axis=0))
        prevcols = np.zeros((128, NRB), np.float32)
        prevcols[:, :2 * c] = 1.0
        in_maps.append(dict(
            x_own=own, x_rem=rem, wq=wq, wk=wk, wv=wv, wo=wo, wg=wg,
            w_gate=w_gate, w_down=w_down, prevcols=prevcols))
    return in_maps


def kernel(L0_attn_x, L0_attn_wq, L0_attn_wk, L0_attn_wv, L0_attn_wo,
           L0_ffn_wg, L0_ffn_w_gate, L0_ffn_w_down):
    x = np.ascontiguousarray(np.asarray(L0_attn_x))
    nc = build_program()
    in_maps = make_in_maps(
        x, np.asarray(L0_attn_wq), np.asarray(L0_attn_wk), np.asarray(L0_attn_wv),
        np.asarray(L0_attn_wo), np.asarray(L0_ffn_wg), np.asarray(L0_ffn_w_gate),
        np.asarray(L0_ffn_w_down))
    res = run_bass_kernel_spmd(nc, in_maps, core_ids=list(range(NC)))
    out = np.concatenate([r["y"].reshape(BPC, S, D) for r in res.results], axis=0)
    return out.astype(np.float32)


if __name__ == "__main__":
    import reference as ref
    inputs = ref.setup_inputs()
    out = kernel(**{k: np.asarray(v) for k, v in inputs.items()})
    exp = np.asarray(ref.reference(**inputs))
    rvr = float(((out - exp) ** 2).mean() / (exp ** 2).mean())
    print("resid var ratio:", rvr)


# version 05
# speedup vs baseline: 1.2934x; 1.2934x over previous
import numpy as np
import ml_dtypes
from contextlib import ExitStack

import concourse.bass as bass
import concourse.mybir as mybir
from concourse.bass import IndirectOffsetOnAxis
from concourse.bass_utils import run_bass_kernel_spmd

F32 = mybir.dt.float32
BF16 = mybir.dt.bfloat16
I32 = mybir.dt.int32
U32 = mybir.dt.uint32
AF = mybir.ActivationFunctionType
ALU = mybir.AluOpType
AX = mybir.AxisListType

B, S, D = 16, 2048, 1024
E, F = 8, 4096
CAP = 2 * B * S // E
NC = 8
BPC = B // NC
TOWN = BPC * S
NT = TOWN // 128
NCP = D // 64
KT = D // 128
FT = F // 128
NRB = B - BPC
RU = NRB * 4
CHAT = 1280
NQ = CHAT // 128
EXPM = -16.0
ZROW_XN = TOWN
ZROW_EO = E * CHAT


def bf(a):
    return np.asarray(a).astype(ml_dtypes.bfloat16)


class Prog:
    ENGINES = ("sync", "gpsimd", "tensor", "vector", "scalar")

    def __init__(self):
        self.ops = {e: [] for e in self.ENGINES}
        self.cnt = {}
        self.sem_names = []

    def sem(self, name):
        self.sem_names.append(name)
        self.cnt[name] = 0
        return name

    def on(self, eng, build, waits=(), incs=()):
        for s, n in incs:
            self.cnt[s] += n
        self.ops[eng].append((tuple(waits), tuple(incs), build))

    def v(self, s):
        return self.cnt[s]

    def realize(self, block, sems):
        for eng_name in self.ENGINES:
            ops = self.ops[eng_name]

            def runner(e, ops=ops):
                for waits, incs, build in ops:
                    for s, val in waits:
                        if val > 0:
                            e.wait_ge(sems[s], val)
                    ins = build(e) if build is not None else None
                    if incs:
                        assert ins is not None
                        for s, n in incs:
                            ins.then_inc(sems[s], n)

            getattr(block, eng_name)(runner)


def build_program(dev=0, dbg=False):
    nc = bass.Bass("TRN2", target_bir_lowering=False, debug=False)
    p = Prog()

    nt = NT if dev == 0 else min(NT, 4 * dev)
    ng = nt // 4
    n_ru = RU if dev == 0 else min(RU, dev)
    n_exp = E if dev == 0 else min(E, 2 * dev)

    x_own = nc.dram_tensor("x_own", [TOWN, D], F32, kind="ExternalInput").ap()
    x_rem = nc.dram_tensor("x_rem", [NRB * S, D], F32, kind="ExternalInput").ap()
    wq_in = nc.dram_tensor("wq", [D, D], F32, kind="ExternalInput").ap()
    wk_in = nc.dram_tensor("wk", [D, D], F32, kind="ExternalInput").ap()
    wv_in = nc.dram_tensor("wv", [D, D], F32, kind="ExternalInput").ap()
    wo_in = nc.dram_tensor("wo", [D, D], F32, kind="ExternalInput").ap()
    wg_in = nc.dram_tensor("wg", [D, E], F32, kind="ExternalInput").ap()
    wgate_in = nc.dram_tensor("w_gate", [E, D, F], F32, kind="ExternalInput").ap()
    wdown_in = nc.dram_tensor("w_down", [E, F, D], F32, kind="ExternalInput").ap()
    prevcols_in = nc.dram_tensor("prevcols", [128, NRB], F32, kind="ExternalInput").ap()
    y_out = nc.dram_tensor("y", [TOWN, D], F32, kind="ExternalOutput").ap()

    xn_dram = nc.dram_tensor("xn_dram", [TOWN + 1, D], BF16, kind="Internal").ap()
    xrem_bf = nc.dram_tensor("xrem_bf", [NRB * S, D], BF16, kind="Internal").ap()
    wqt_d = nc.dram_tensor("wqt_d", [D, D], BF16, kind="Internal").ap()
    wkt_d = nc.dram_tensor("wkt_d", [D, D], BF16, kind="Internal").ap()
    wvt_d = nc.dram_tensor("wvt_d", [D, D], BF16, kind="Internal").ap()
    wot_d = nc.dram_tensor("wot_d", [D, D], BF16, kind="Internal").ap()
    attn_dram = nc.dram_tensor("attn_dram", [D, TOWN], BF16, kind="Internal").ap()
    disp_dram = nc.dram_tensor("disp_dram", [E * CHAT, D], BF16, kind="Internal").ap()
    eo_dram = nc.dram_tensor("eo_dram", [E * CHAT + 1, D], BF16, kind="Internal").ap()
    glist_dram = nc.dram_tensor("glist_dram", [E * CHAT + 1, 1], I32, kind="Internal").ap()
    thr_dram = nc.dram_tensor("thr_dram", [1, 16], F32, kind="Internal").ap()
    wgate_bf = nc.dram_tensor("wgate_bf", [E, D, F], BF16, kind="Internal").ap()
    wdown_bf = nc.dram_tensor("wdown_bf", [E, F, D], BF16, kind="Internal").ap()
    cs_dram = nc.dram_tensor("cs_dram", [1, E], F32, kind="Internal").ap()

    tri_d = nc.inline_tensor(bf(np.triu(np.ones((128, 128), np.float32), 1)), name="ctri")
    onesc_d = nc.inline_tensor(bf(np.ones((128, 1), np.float32)), name="conesc")
    onesr_d = nc.inline_tensor(bf(np.ones((1, 128), np.float32)), name="conesr")
    ones1f_d = nc.inline_tensor(np.ones((1, 128), np.float32), name="cones1f")
    iota8_d = nc.inline_tensor(
        np.repeat(np.arange(E, dtype=np.float32)[None, :], 128, 0), name="ciota8")
    tokid_d = nc.inline_tensor(
        (np.arange(128, dtype=np.int32)[:, None]
         + 128 * np.arange(NT, dtype=np.int32)[None, :]), name="ctokid")
    glpre_d = nc.inline_tensor(np.full((E * CHAT + 1, 1), ZROW_XN, np.int32), name="cglpre")

    es = ExitStack()
    SB = lambda name, shape, dtype: es.enter_context(nc.sbuf_tensor(name, shape, dtype))
    PSB = lambda name, n=512: es.enter_context(nc.psum_tensor(name, [128, n], F32))
    bankp = [PSB("pbp0", 1024), PSB("pbp1", 1024)]
    bank = [bankp[0][:, 0:512], bankp[0][:, 512:1024],
            bankp[1][:, 0:512], bankp[1][:, 512:1024],
            PSB("pb4"), PSB("pb5"), PSB("pb6"), PSB("pb7")]
    sb_pair_ap = [bankp[0][:, :], bankp[1][:, :]]

    tri_sb = SB("tri_sb", [128, 128], BF16)
    onesc_sb = SB("onesc_sb", [128, 1], BF16)
    onesr_sb = SB("onesr_sb", [1, 128], BF16)
    ones1f_sb = SB("ones1f_sb", [1, 128], F32)
    iota8_sb = SB("iota8_sb", [128, E], F32)
    tokid_sb = SB("tokid_sb", [128, NT], I32)
    wg_sb = SB("wg_sb", [128, KT, E], BF16)
    wgo_sb = SB("wgo_sb", [128, KT, E + 1], BF16)
    csb_sb = SB("csb_sb", [128, E], F32)
    prevc_sb = SB("prevc_sb", [128, NRB], BF16)
    oh1_sb = SB("oh1_sb", [128, NT, E], BF16)
    oh2_sb = SB("oh2_sb", [128, NT, E], BF16)
    g1n_sb = SB("g1n_sb", [128, NT], F32)
    g2n_sb = SB("g2n_sb", [128, NT], F32)
    loc1_sb = SB("loc1_sb", [128, NT], F32)
    loc2_sb = SB("loc2_sb", [128, NT], F32)
    cidx1_sb = SB("cidx1_sb", [128, NT], I32)
    cidx2_sb = SB("cidx2_sb", [128, NT], I32)
    run_sb = SB("run_sb", [1, 24], F32)
    remprev_sb = SB("remprev_sb", [1, 16], F32)
    remall_sb = SB("remall_sb", [1, 8], F32)
    thr1b_sb = SB("thr1b_sb", [128, 8], F32)
    thr2b_sb = SB("thr2b_sb", [128, 8], F32)
    scr_sb = SB("scr_sb", [128, E + 1], F32)
    scr2_sb = SB("scr2_sb", [128, E], F32)
    mx1_sb = SB("mx1_sb", [128, 1], F32)
    mx2_sb = SB("mx2_sb", [128, 1], F32)
    ix_sb = SB("ix_sb", [128, E], U32)
    i1f_sb = SB("i1f_sb", [128, 1], F32)
    i2f_sb = SB("i2f_sb", [128, 1], F32)
    l2_sb = SB("l2_sb", [128, E], F32)
    sstat_sb = SB("sstat_sb", [128, 8], F32)
    em16_sb = SB("em16_sb", [128, 1], F32)
    sc8_sb = SB("sc8_sb", [128, 1], F32)
    eps_sb = SB("eps_sb", [128, 1], F32)
    zrow_sb = SB("zrow_sb", [1, D], BF16)
    cs1_sb = SB("cs1_sb", [1, 8], F32)
    thrr_sb = SB("thrr_sb", [1, 16], F32)
    tmpe_sb = SB("tmpe_sb", [1, 8], F32)
    pos1_sb = SB("pos1_sb", [128, 1], F32)
    pos2_sb = SB("pos2_sb", [128, 1], F32)
    tmpa_sb = SB("tmpa_sb", [128, E], F32)
    tmpn_sb = SB("tmpn_sb", [128, 1], F32)
    tmpq_sb = SB("tmpq_sb", [128, 1], F32)
    tmpw_sb = SB("tmpw_sb", [128, 1], F32)
    tmpi_sb = SB("tmpi_sb", [128, 1], I32)
    scidx_sb = [SB(f"scidx{i}_sb", [128, 1], I32) for i in range(2)]
    tmpc_sb = SB("tmpc_sb", [128, 1], F32)
    tmpd_sb = SB("tmpd_sb", [128, 1], F32)
    dbgrem_sb = SB("dbgrem_sb", [1, RU * 4], BF16) if dbg else None
    ohpr_sb = SB("ohpr_sb", [128, 4, 2 * E], BF16)
    ohpr2_sb = SB("ohpr2_sb", [128, 3 * E], BF16)

    s_const = p.sem("s_const")
    s_wcast = p.sem("s_wcast")
    s_wtr = p.sem("s_wtr")
    s_xl = [p.sem(f"s_xl{i}") for i in range(2)]
    s_xns = [p.sem(f"s_xns{i}") for i in range(2)]
    s_xnready = p.sem("s_xnready")
    s_xnf = [p.sem(f"s_xnf{i}") for i in range(2)]
    s_te = p.sem("s_te")
    s_ve = p.sem("s_ve")
    s_se = p.sem("s_se")
    s_xrc = p.sem("s_xrc")
    s_xrf = [p.sem(f"s_xrf{i}") for i in range(2)]
    s_scat = [p.sem(f"s_scat{i}") for i in range(2)]
    s_glpre = p.sem("s_glpre")
    s_ats = [p.sem(f"s_ats{i}") for i in range(2)]
    s_atready = p.sem("s_atready")
    s_msc = p.sem("s_msc")
    s_glist = p.sem("s_glist")
    s_dg = [p.sem(f"s_dg{i}") for i in range(2)]
    s_dts = [p.sem(f"s_dts{i}") for i in range(2)]
    s_dready = p.sem("s_dready")
    s_dfm = [p.sem(f"s_dfm{i}") for i in range(2)]
    s_wgb = [p.sem(f"s_wgb{i}") for i in range(2)]
    s_wdh = [p.sem(f"s_wdh{i}") for i in range(2)]
    s_eos = [p.sem(f"s_eos{i}") for i in range(2)]
    s_eoready = p.sem("s_eoready")
    s_cmb = [p.sem(f"s_cmb{i}") for i in range(2)]
    s_att = [p.sem(f"s_att{i}") for i in range(2)]
    s_out = [p.sem(f"s_out{i}") for i in range(2)]
    s_bar = p.sem("s_bar")

    W = lambda *pairs: list(pairs)

    def onv(build, waits=(), incs=()):
        p.on("vector", build, waits=W((s_ve, p.v(s_ve)), *waits),
             incs=[(s_ve, 1), *incs])
        return p.v(s_ve)

    def ons(build, waits=(), incs=()):
        p.on("scalar", build, waits=W((s_se, p.v(s_se)), *waits),
             incs=[(s_se, 1), *incs])
        return p.v(s_se)

    def ont(build, waits=(), incs=()):
        p.on("tensor", build, waits=waits, incs=incs)
        return p.v(s_te)

    def barrier():
        tgt = p.v(s_bar) + 5
        for eng in Prog.ENGINES:
            p.on(eng, lambda e: e.nop(), incs=[(s_bar, 1)])
        for eng in Prog.ENGINES:
            p.on(eng, None, waits=W((s_bar, tgt)))

    es1 = ExitStack()
    SB1 = lambda name, shape, dtype: es1.enter_context(nc.sbuf_tensor(name, shape, dtype))
    wqT_sb = SB1("wqT_sb", [128, KT, D], BF16)
    wkT_sb = SB1("wkT_sb", [128, KT, D], BF16)
    wvT_sb = SB1("wvT_sb", [128, KT, D], BF16)
    woT_sb = SB1("woT_sb", [128, KT, D], BF16)
    xld_sb = SB1("xld_sb", [128, 2, D], F32)
    xnt_sb = SB1("xnt_sb", [128, 2, D], BF16)
    xnf_sb = SB1("xnf_sb", [128, 2, KT, 512], BF16)
    qhc_sb = SB1("qhc_sb", [128, 2, 4 * NCP * 128], BF16)
    kpr_sb = SB1("kpr_sb", [128, 2, E, 4 * 128], BF16)
    vpv_sb = SB1("vpv_sb", [128, 2, 4 * NCP * 65], BF16)
    ebuf_sb = SB1("ebuf_sb", [128, 2, 1024], BF16)
    xo_sb = SB1("xo_sb", [128, 2, KT, 512], BF16)
    rcp_sb = SB1("rcp_sb", [1, 512], F32)
    lnr_sb = SB1("lnr_sb", [1, 512], F32)
    rcpb_sb = SB1("rcpb_sb", [64, 512], BF16)
    oev_sb = SB1("oev_sb", [64, 512], F32)
    opj_sb = SB1("opj_sb", [128, 2, 512], BF16)
    xrf_sb = SB1("xrf_sb", [128, 2, KT, 512], BF16)

    for dst, src in ((tri_sb, tri_d), (onesc_sb, onesc_d), (onesr_sb, onesr_d),
                     (ones1f_sb, ones1f_d), (iota8_sb, iota8_d), (tokid_sb, tokid_d)):
        p.on("sync", lambda e, dst=dst, src=src: e.dma_start(dst[:], src.ap()),
             incs=[(s_const, 16)])
    p.on("sync", lambda e: e.dma_start(glist_dram[:], glpre_d.ap()), incs=[(s_glpre, 16)])

    p.on("gpsimd", lambda e: e.dma_start(wqt_d[:], wq_in[:]), incs=[(s_wcast, 16)])
    p.on("gpsimd", lambda e: e.dma_start(wkt_d[:], wk_in[:]), incs=[(s_wcast, 16)])
    p.on("gpsimd", lambda e: e.dma_start(wvt_d[:], wv_in[:]), incs=[(s_wcast, 16)])
    p.on("gpsimd", lambda e: e.dma_start(wot_d[:], wo_in[:]), incs=[(s_wcast, 16)])
    p.on("gpsimd", lambda e: e.dma_start(
        wg_sb[:], wg_in.rearrange("(a p) e -> p a e", p=128)), incs=[(s_wcast, 16)])
    p.on("gpsimd", lambda e: e.dma_start(prevc_sb[:], prevcols_in[:]),
         incs=[(s_wcast, 16)])
    wcast_all = p.v(s_wcast)

    for wsrc, wdst in ((wqt_d, wqT_sb), (wkt_d, wkT_sb), (wvt_d, wvT_sb), (wot_d, woT_sb)):
        for b_ in range(KT):
            p.on("scalar",
                 lambda e, wsrc=wsrc, wdst=wdst, b_=b_: e.dma_start_transpose(
                     wdst[:, b_, :], wsrc[:, b_ * 128:(b_ + 1) * 128]),
                 waits=W((s_wcast, wcast_all)), incs=[(s_wtr, 16)])
    wtr_all = p.v(s_wtr)
    ve_wqs = onv(lambda e: e.tensor_scalar(
        wqT_sb[:].rearrange("p a d -> p (a d)"),
        wqT_sb[:].rearrange("p a d -> p (a d)"), 0.125, None, ALU.mult),
        waits=W((s_wtr, wtr_all)))

    onv(lambda e: e.memset(em16_sb[:], EXPM))
    onv(lambda e: e.memset(sc8_sb[:], 0.125))
    onv(lambda e: e.memset(eps_sb[:], 1e-5))
    onv(lambda e: e.memset(zrow_sb[:], 0.0))
    onv(lambda e: e.memset(run_sb[:], 0.0))
    onv(lambda e: e.memset(remprev_sb[:], 0.0))
    ve_zeros = onv(lambda e: e.memset(remall_sb[:], 0.0))
    p.on("sync", lambda e: e.dma_start(xn_dram[ZROW_XN:ZROW_XN + 1, :], zrow_sb[:]),
         waits=W((s_ve, ve_zeros)), incs=[(s_const, 16)])
    p.on("sync", lambda e: e.dma_start(eo_dram[ZROW_EO:ZROW_EO + 1, :], zrow_sb[:]),
         waits=W((s_ve, ve_zeros)), incs=[(s_const, 16)])

    const6 = p.v(s_const)
    p.on("vector", None, waits=W((s_const, const6)))
    p.on("gpsimd", None, waits=W((s_const, const6)))
    p.on("tensor", None, waits=W((s_const, const6)))
    onv(lambda e: e.tensor_copy(wgo_sb[:, :, 0:E], wg_sb[:]),
        waits=W((s_wcast, wcast_all)))
    ve_wgo = onv(lambda e: e.memset(wgo_sb[:, :, E:E + 1], 1.0))
    for k in range(KT):
        ont(lambda e, k=k: e.matmul(
            bank[7][0:1, 0:E], onesc_sb[:], wg_sb[:, k, :],
            start=(k == 0), stop=(k == KT - 1)),
            waits=(W((s_wcast, wcast_all)) if k == 0 else ()),
            incs=([(s_te, 1)] if k == KT - 1 else ()))
    te_cs = p.v(s_te)
    ve_cs = onv(lambda e: e.tensor_copy(cs1_sb[:], bank[7][0:1, 0:E]),
                waits=W((s_te, te_cs)))
    p.on("sync", lambda e: e.dma_start(cs_dram[:], cs1_sb[:]),
         waits=W((s_ve, ve_cs)), incs=[(s_msc, 16)])
    p.on("sync", lambda e: e.dma_start(csb_sb[:], cs_dram.partition_broadcast(128)),
         waits=W((s_msc, p.v(s_msc))), incs=[(s_msc, 16)])
    msc_csb = p.v(s_msc)
    ve_csb = onv(lambda e: e.tensor_scalar(csb_sb[:], csb_sb[:], 1.0 / D, None, ALU.mult),
                 waits=W((s_msc, msc_csb)))

    n_store = 0
    xfree_ve = [0, 0]
    xns_use = [0, 0]
    for t in range(nt):
        slot = t % 2
        p.on("sync", lambda e, t=t, slot=slot: e.dma_start(
            xld_sb[:, slot, :], x_own[t * 128:(t + 1) * 128, :]),
            waits=W((s_ve, xfree_ve[slot])), incs=[(s_xl[slot], 16)])
        ld_n = p.v(s_xl[slot])
        onv(lambda e, slot=slot: e.reduce_sum(
            sstat_sb[:, 0:1], xld_sb[:, slot, :], axis=AX.X),
            waits=W((s_xl[slot], ld_n)))
        se_sq = ons(lambda e, slot=slot: e.activation(
            xnt_sb[:, slot, :], xld_sb[:, slot, :], AF.Square, bias=0.0, scale=1.0,
            accum_out=sstat_sb[:, 1:2]),
            waits=W((s_xl[slot], ld_n), (s_xns[slot], 16 * xns_use[slot])))
        onv(lambda e: e.tensor_scalar(
            sstat_sb[:, 2:3], sstat_sb[:, 0:1], 1.0 / D, None, ALU.mult))
        onv(lambda e: e.tensor_tensor(
            sstat_sb[:, 3:4], sstat_sb[:, 2:3], sstat_sb[:, 2:3], ALU.mult))
        ve_var = onv(lambda e: e.tensor_scalar(
            sstat_sb[:, 4:5], sstat_sb[:, 1:2], 1.0 / D, sstat_sb[:, 3:4],
            ALU.mult, ALU.subtract),
            waits=W((s_se, se_sq)))
        se_sqrt = ons(lambda e: e.activation(
            sstat_sb[:, 5:6], sstat_sb[:, 4:5], AF.Sqrt, bias=eps_sb[:], scale=1.0),
            waits=W((s_ve, ve_var)))
        onv(lambda e: e.reciprocal(sstat_sb[:, 6:7], sstat_sb[:, 5:6]),
            waits=W((s_se, se_sqrt)))
        ve_xn = onv(lambda e, slot=slot: e.tensor_scalar(
            xnt_sb[:, slot, :], xld_sb[:, slot, :], sstat_sb[:, 2:3],
            sstat_sb[:, 6:7], ALU.subtract, ALU.mult),
            waits=W((s_xns[slot], 16 * xns_use[slot])))
        xfree_ve[slot] = ve_xn
        p.on("sync", lambda e, t=t, slot=slot: e.dma_start(
            xn_dram[t * 128:(t + 1) * 128, :], xnt_sb[:, slot, :]),
            waits=W((s_ve, ve_xn), (s_const, p.v(s_const))), incs=[(s_xns[slot], 16)])
        xns_use[slot] += 1
        n_store += 1
        if t % 4 == 3:
            p.on("sync", lambda e: e.nop(),
                 waits=W((s_xns[0], 16 * xns_use[0]), (s_xns[1], 16 * xns_use[1])),
                 incs=[(s_xnready, 1)])

    n_cast = (n_ru + 1) // 2
    for u in range(n_cast):
        p.on("gpsimd", lambda e, u=u: e.dma_start(
            xrem_bf[u * 1024:(u + 1) * 1024, :], x_rem[u * 1024:(u + 1) * 1024, :]),
            waits=W((s_xrc, 16 * u),
                    *(((s_xnready, 1),) if u == 0 else ())),
            incs=[(s_xrc, 16)])

    s_wpc = p.sem("s_wpc")
    cast_jobs = []
    for ei2 in range(n_exp):
        for wsel in range(2):
            def cast_job(ei2=ei2, wsel=wsel):
                ci = 2 * ei2 + wsel
                src = wgate_in[ei2] if wsel == 0 else wdown_in[ei2]
                dst = wgate_bf[ei2] if wsel == 0 else wdown_bf[ei2]
                p.on("gpsimd", lambda e, dst=dst, src=src: e.dma_start(dst, src),
                     waits=W((s_wpc, 16 * ci)), incs=[(s_wpc, 16)])
            cast_jobs.append(cast_job)

    xrf_total = {u: 16 * KT * (u // 2 + 1) for u in range(n_ru)}
    xrffree_te = [0, 0]
    scrfree_ve = [ve_cs]
    ohprfree_te = [0, 0, 0, 0]
    p2free_ve = [0]
    ve_oh_rec = {}
    rem_state = {"p2s": 0}

    def make_remote_transpose_job(u):
        slot = u % 2

        def job():
            for kp in range(KT):
                p.on("sync", lambda e, u=u, kp=kp, slot=slot: e.dma_start_transpose(
                    xrf_sb[:, slot, kp, :],
                    xrem_bf[u * 512:(u + 1) * 512, kp * 128:(kp + 1) * 128]),
                    waits=W((s_xrc, 16 * (u // 2 + 1)), (s_te, xrffree_te[slot]),
                            (s_wtr, wtr_all), (s_se, p.v(s_se))),
                    incs=[(s_xrf[slot], 16)])

        return job

    def make_remote_p1_job(u, tl):
        slot = u % 2

        def job():
            for k in range(KT):
                ont(lambda e, k=k: e.matmul(
                    bank[7][:, 0:E + 1],
                    xrf_sb[:, slot, k, tl * 128:(tl + 1) * 128],
                    wgo_sb[:, k, :],
                    start=(k == 0), stop=(k == KT - 1)),
                    waits=(W((s_xrf[slot], xrf_total[u]), (s_ve, scrfree_ve[0]))
                           if k == 0 else ()),
                    incs=([(s_te, 1)] if k == KT - 1 else ()))
            if tl == 3:
                xrffree_te[slot] = p.v(s_te)
            te_lg = p.v(s_te)
            scrfree_ve[0] = onv(
                lambda e: e.tensor_copy(scr_sb[:], bank[7][:, 0:E + 1]),
                waits=W((s_te, te_lg)))
            onv(lambda e: e.tensor_scalar(
                tmpa_sb[:], csb_sb[:], scr_sb[:, E:E + 1], None, ALU.mult),
                waits=W((s_ve, ve_csb)))
            onv(lambda e: e.tensor_tensor(
                scr2_sb[:], scr_sb[:, 0:E], tmpa_sb[:], ALU.subtract))
            onv(lambda e: e.reduce_max(mx1_sb[:], scr2_sb[:], axis=AX.X))
            onv(lambda e: e.max_index(
                ix_sb[:], mx1_sb[:, 0:1].broadcast_to((128, E)), scr2_sb[:]))
            onv(lambda e: e.tensor_copy(i1f_sb[:], ix_sb[:, 0:1]))
            onv(lambda e: e.match_replace(
                l2_sb[:], mx1_sb[:, 0:1].broadcast_to((128, E)), scr2_sb[:], -1e30))
            onv(lambda e: e.reduce_max(mx2_sb[:], l2_sb[:], axis=AX.X))
            onv(lambda e: e.max_index(
                ix_sb[:], mx2_sb[:, 0:1].broadcast_to((128, E)), l2_sb[:]))
            onv(lambda e: e.tensor_copy(i2f_sb[:], ix_sb[:, 0:1]))
            onv(lambda e, tl=tl: e.tensor_scalar(
                ohpr_sb[:, tl, 0:E], iota8_sb[:], i1f_sb[:], None, ALU.is_equal),
                waits=W((s_te, ohprfree_te[tl])))
            ve_oh_rec[(u, tl)] = onv(lambda e, tl=tl: e.tensor_scalar(
                ohpr_sb[:, tl, E:2 * E], iota8_sb[:], i2f_sb[:], None, ALU.is_equal))

        return job

    def make_remote_p2_job(u, tl):
        chunk_col = u // 4

        def job():
            ve_need = max(ve_oh_rec[(u, tl)], p2free_ve[0])
            ont(lambda e, tl=tl: e.matmul(
                bank[7][0:1, 16:32], prevc_sb[:, chunk_col:chunk_col + 1],
                ohpr_sb[:, tl, :], start=True, stop=False),
                waits=W((s_ve, ve_need)))
            ont(lambda e, tl=tl: e.matmul(
                bank[7][0:1, 32:40], onesc_sb[:], ohpr_sb[:, tl, 0:E],
                start=False, stop=True),
                incs=[(s_te, 1)])
            te_cnt = p.v(s_te)
            ohprfree_te[tl] = te_cnt
            onv(lambda e: e.tensor_tensor(
                remprev_sb[:], remprev_sb[:], bank[7][0:1, 16:32], ALU.add),
                waits=W((s_te, te_cnt)))
            if dbg:
                def _dbgred(e, u=u, tl=tl):
                    with nc.allow_low_precision("dbg count trace"):
                        return e.reduce_sum(
                            dbgrem_sb[0:1, u * 4 + tl:u * 4 + tl + 1],
                            bank[7][0:1, 32:40], axis=AX.X)
                onv(_dbgred)
            p2free_ve[0] = onv(lambda e: e.tensor_tensor(
                remall_sb[:], remall_sb[:], bank[7][0:1, 32:40], ALU.add))
            rem_state["p2s"] += 1

        return job

    remote_jobs = []
    for u in range(n_ru):
        remote_jobs.append(make_remote_transpose_job(u))
        for tl in range(4):
            remote_jobs.append(make_remote_p1_job(u, tl))
        for tl in range(4):
            remote_jobs.append(make_remote_p2_job(u, tl))
        if u % 3 == 2 and cast_jobs:
            remote_jobs.append(cast_jobs.pop(0))
    remote_jobs.extend(cast_jobs)
    remote_iter = iter(remote_jobs)

    def emit_some_remote(nmax):
        for _ in range(nmax):
            job = next(remote_iter, None)
            if job is None:
                return
            job()

    scat_use = [0, 0]

    def emit_own_routing_tile(t, gslot, xnf_ready):
        tl = t % 4
        for k in range(KT):
            ont(lambda e, k=k: e.matmul(
                bank[3][:, 0:E],
                xnf_sb[:, gslot, k, tl * 128:(tl + 1) * 128], wg_sb[:, k, :],
                start=(k == 0), stop=(k == KT - 1)),
                waits=(W((s_xnf[gslot], xnf_ready), (s_ve, p.v(s_ve)),
                         (s_se, p.v(s_se))) if k == 0 else ()),
                incs=([(s_te, 1)] if k == KT - 1 else ()))
        te_lg = p.v(s_te)
        onv(lambda e: e.tensor_copy(scr_sb[:, 0:E], bank[3][:, 0:E]),
            waits=W((s_te, te_lg)))
        onv(lambda e: e.reduce_max(mx1_sb[:], scr_sb[:, 0:E], axis=AX.X))
        onv(lambda e: e.max_index(
            ix_sb[:], mx1_sb[:, 0:1].broadcast_to((128, E)), scr_sb[:, 0:E]))
        onv(lambda e: e.tensor_copy(i1f_sb[:], ix_sb[:, 0:1]))
        onv(lambda e: e.match_replace(
            l2_sb[:], mx1_sb[:, 0:1].broadcast_to((128, E)), scr_sb[:, 0:E], -1e30))
        onv(lambda e: e.reduce_max(mx2_sb[:], l2_sb[:], axis=AX.X))
        onv(lambda e: e.max_index(
            ix_sb[:], mx2_sb[:, 0:1].broadcast_to((128, E)), l2_sb[:]))
        ve_idx = onv(lambda e: e.tensor_copy(i2f_sb[:], ix_sb[:, 0:1]))
        se_g = ons(lambda e, t=t: e.activation(
            g1n_sb[:, t:t + 1], mx2_sb[:], AF.Sigmoid, bias=mx1_sb[:], scale=-1.0),
            waits=W((s_ve, ve_idx)))
        onv(lambda e, t=t: e.tensor_scalar(
            g2n_sb[:, t:t + 1], g1n_sb[:, t:t + 1], -1.0, 1.0, ALU.mult, ALU.add),
            waits=W((s_se, se_g)))
        onv(lambda e, t=t: e.tensor_scalar(
            oh1_sb[:, t, :], iota8_sb[:], i1f_sb[:], None, ALU.is_equal))
        onv(lambda e, t=t: e.tensor_scalar(
            oh2_sb[:, t, :], iota8_sb[:], i2f_sb[:], None, ALU.is_equal))
        onv(lambda e, t=t: e.tensor_copy(ohpr2_sb[:, 0:E], oh1_sb[:, t, :]),
            waits=W((s_te, p.v(s_te))))
        onv(lambda e, t=t: e.tensor_copy(ohpr2_sb[:, E:2 * E], oh2_sb[:, t, :]))
        ve_oh3 = onv(lambda e, t=t: e.tensor_tensor(
            ohpr2_sb[:, 2 * E:3 * E], oh1_sb[:, t, :], oh2_sb[:, t, :], ALU.add))
        ont(lambda e: e.matmul(
            bank[3][:, 32:56], tri_sb[:], ohpr2_sb[:], start=True, stop=False),
            waits=W((s_ve, ve_oh3)))
        ont(lambda e: e.matmul(
            bank[3][:, 32:56], ones1f_sb[:], run_sb[:], start=False, stop=True),
            incs=[(s_te, 1)])
        te_loc = p.v(s_te)
        ont(lambda e: e.matmul(
            bank[2][0:1, 32:56], onesc_sb[:], ohpr2_sb[:], start=True, stop=True),
            incs=[(s_te, 1)])
        te_cs2 = p.v(s_te)
        onv(lambda e, t=t: e.tensor_tensor(
            tmpa_sb[:], bank[3][:, 32:40], oh1_sb[:, t, :], ALU.mult),
            waits=W((s_te, te_loc)))
        onv(lambda e, t=t: e.reduce_sum(loc1_sb[:, t:t + 1], tmpa_sb[:], axis=AX.X))
        onv(lambda e, t=t: e.tensor_tensor(
            tmpa_sb[:], bank[3][:, 40:48], oh2_sb[:, t, :], ALU.mult))
        onv(lambda e, t=t: e.reduce_sum(loc2_sb[:, t:t + 1], tmpa_sb[:], axis=AX.X))
        onv(lambda e, t=t: e.tensor_tensor(
            tmpa_sb[:], bank[3][:, 48:56], oh1_sb[:, t, :], ALU.mult))
        onv(lambda e: e.reduce_sum(pos1_sb[:], tmpa_sb[:], axis=AX.X))
        onv(lambda e, t=t: e.tensor_tensor(
            tmpa_sb[:], bank[3][:, 48:56], oh2_sb[:, t, :], ALU.mult))
        onv(lambda e: e.reduce_sum(pos2_sb[:], tmpa_sb[:], axis=AX.X))
        ve_run = onv(lambda e: e.tensor_tensor(
            run_sb[:], run_sb[:], bank[2][0:1, 32:56], ALU.add),
            waits=W((s_te, te_cs2)))
        for ch_i, (idxf, posx, cidx) in enumerate(
                ((i1f_sb, pos1_sb, cidx1_sb), (i2f_sb, pos2_sb, cidx2_sb))):
            onv(lambda e, posx=posx: e.tensor_scalar(
                tmpd_sb[:], posx[:], float(CHAT), None, ALU.is_ge))
            onv(lambda e: e.tensor_scalar(
                tmpn_sb[:], tmpd_sb[:], -1.0, 1.0, ALU.mult, ALU.add))
            onv(lambda e, idxf=idxf: e.tensor_scalar(
                tmpc_sb[:], idxf[:], float(CHAT), None, ALU.mult))
            onv(lambda e, posx=posx: e.tensor_tensor(
                tmpc_sb[:], tmpc_sb[:], posx[:], ALU.add))
            onv(lambda e: e.tensor_tensor(tmpc_sb[:], tmpc_sb[:], tmpn_sb[:], ALU.mult))
            onv(lambda e: e.scalar_tensor_tensor(
                tmpc_sb[:], tmpd_sb[:], float(E * CHAT), tmpc_sb[:], ALU.mult, ALU.add))
            ve_sc = onv(lambda e, ch_i=ch_i: e.tensor_copy(
                scidx_sb[ch_i][:], tmpc_sb[:]),
                waits=W((s_scat[ch_i], 16 * scat_use[ch_i])))
            scat_use[ch_i] += 1
            p.on("gpsimd", lambda e, t=t, ch_i=ch_i: e.indirect_dma_start(
                glist_dram[:],
                IndirectOffsetOnAxis(scidx_sb[ch_i][:], 0),
                tokid_sb[:, t:t + 1], None),
                waits=W((s_ve, ve_sc), (s_glpre, 16)), incs=[(s_scat[ch_i], 16)])
            onv(lambda e, posx=posx: e.tensor_scalar(
                tmpc_sb[:], posx[:], 1.0 / NQ, 0.5 / NQ, ALU.mult, ALU.add))
            onv(lambda e: e.tensor_copy(tmpi_sb[:], tmpc_sb[:]))
            onv(lambda e: e.tensor_copy(tmpc_sb[:], tmpi_sb[:]))
            onv(lambda e, posx=posx: e.scalar_tensor_tensor(
                tmpq_sb[:], tmpc_sb[:], -float(NQ), posx[:], ALU.mult, ALU.add))
            onv(lambda e: e.tensor_scalar(tmpw_sb[:], tmpq_sb[:], 0.0, None, ALU.is_lt))
            onv(lambda e: e.tensor_tensor(tmpc_sb[:], tmpc_sb[:], tmpw_sb[:], ALU.subtract))
            onv(lambda e: e.scalar_tensor_tensor(
                tmpq_sb[:], tmpw_sb[:], float(NQ), tmpq_sb[:], ALU.mult, ALU.add))
            onv(lambda e: e.tensor_scalar(tmpw_sb[:], tmpq_sb[:], float(NQ), None, ALU.is_ge))
            onv(lambda e: e.tensor_tensor(tmpc_sb[:], tmpc_sb[:], tmpw_sb[:], ALU.add))
            onv(lambda e: e.scalar_tensor_tensor(
                tmpq_sb[:], tmpw_sb[:], -float(NQ), tmpq_sb[:], ALU.mult, ALU.add))
            onv(lambda e: e.tensor_scalar(
                tmpq_sb[:], tmpq_sb[:], 128.0, None, ALU.mult))
            onv(lambda e: e.tensor_tensor(tmpq_sb[:], tmpq_sb[:], tmpc_sb[:], ALU.add))
            onv(lambda e, idxf=idxf: e.tensor_scalar(
                tmpc_sb[:], idxf[:], float(CHAT), None, ALU.mult))
            onv(lambda e: e.tensor_tensor(tmpq_sb[:], tmpq_sb[:], tmpc_sb[:], ALU.add))
            onv(lambda e: e.tensor_tensor(tmpq_sb[:], tmpq_sb[:], tmpn_sb[:], ALU.mult))
            onv(lambda e: e.scalar_tensor_tensor(
                tmpq_sb[:], tmpd_sb[:], float(ZROW_EO), tmpq_sb[:], ALU.mult, ALU.add))
            onv(lambda e, t=t, cidx=cidx: e.tensor_copy(cidx[:, t:t + 1], tmpq_sb[:]))
        return ve_run

    xnffree_te = [0, 0]
    last_exp_pair = [0, 0]
    obank_free = [(0, 0), (0, 0)]
    rcpbfree_ve = [0]
    terb_prev = [0]
    opjfree_ve = [0, 0]
    for g in range(ng):
        gslot = g % 2
        for kp in range(KT):
            p.on("sync", lambda e, g=g, kp=kp, gslot=gslot: e.dma_start_transpose(
                xnf_sb[:, gslot, kp, :],
                xn_dram[g * 512:(g + 1) * 512, kp * 128:(kp + 1) * 128]),
                waits=W((s_xnready, g + 1), (s_te, xnffree_te[gslot]),
                        (s_wtr, wtr_all), (s_se, p.v(s_se))),
                incs=[(s_xnf[gslot], 16)])
        xnf_ready = p.v(s_xnf[gslot])

        for a in range(NCP // 2):
            for k in range(KT):
                ont(lambda e, a=a, k=k, gslot=gslot: e.matmul(
                    bank[0][0:64, :],
                    wqT_sb[:, k, (2 * a) * 64:(2 * a) * 64 + 64],
                    xnf_sb[:, gslot, k, :],
                    start=(k == 0), stop=(k == KT - 1), tile_position=(0, 0)),
                    waits=(W((s_xnf[gslot], xnf_ready), (s_ve, p.v(s_ve)))
                           if k == 0 else ()))
                ont(lambda e, a=a, k=k, gslot=gslot: e.matmul(
                    bank[1][64:128, :],
                    wqT_sb[:, k, (2 * a) * 64:(2 * a) * 64 + 64],
                    xnf_sb[:, gslot, k, :],
                    start=(k == 0), stop=(k == KT - 1), tile_position=(0, 64)))
                ont(lambda e, a=a, k=k, gslot=gslot: e.matmul(
                    bank[2][0:64, :],
                    wqT_sb[:, k, (2 * a + 1) * 64:(2 * a + 1) * 64 + 64],
                    xnf_sb[:, gslot, k, :],
                    start=(k == 0), stop=(k == KT - 1), tile_position=(0, 0)))
                ont(lambda e, a=a, k=k, gslot=gslot: e.matmul(
                    bank[3][64:128, :],
                    wqT_sb[:, k, (2 * a + 1) * 64:(2 * a + 1) * 64 + 64],
                    xnf_sb[:, gslot, k, :],
                    start=(k == 0), stop=(k == KT - 1), tile_position=(0, 64)),
                    incs=([(s_te, 1)] if k == KT - 1 else ()))
            te_q = p.v(s_te)
            for cc, bnklo, bnkhi in ((2 * a, bank[0], bank[1]),
                                     (2 * a + 1, bank[2], bank[3])):
                onv(lambda e, cc=cc, bnklo=bnklo, gslot=gslot: e.tensor_copy(
                    qhc_sb[:, gslot, :].rearrange("p (h c j) -> p h c j",
                                                  h=4, c=NCP)[0:64, :, cc, :],
                    bnklo[0:64, :].rearrange("p (h j) -> p h j", h=4)),
                    waits=W((s_te, te_q)))
                onv(lambda e, cc=cc, bnkhi=bnkhi, gslot=gslot: e.tensor_copy(
                    qhc_sb[:, gslot, :].rearrange("p (h c j) -> p h c j",
                                                  h=4, c=NCP)[64:128, :, cc, :],
                    bnkhi[64:128, :].rearrange("p (h j) -> p h j", h=4)))

        for a in range(NCP // 2):
            for k in range(KT):
                ont(lambda e, a=a, k=k, gslot=gslot: e.matmul(
                    bank[0][0:64, :],
                    wkT_sb[:, k, (2 * a) * 64:(2 * a) * 64 + 64],
                    xnf_sb[:, gslot, k, :],
                    start=(k == 0), stop=(k == KT - 1), tile_position=(0, 0)),
                    waits=(W((s_ve, p.v(s_ve)),) if k == 0 else ()))
                ont(lambda e, a=a, k=k, gslot=gslot: e.matmul(
                    bank[1][64:128, :],
                    wkT_sb[:, k, (2 * a + 1) * 64:(2 * a + 1) * 64 + 64],
                    xnf_sb[:, gslot, k, :],
                    start=(k == 0), stop=(k == KT - 1), tile_position=(0, 64)),
                    incs=([(s_te, 1)] if k == KT - 1 else ()))
            te_k = p.v(s_te)
            onv(lambda e, a=a, gslot=gslot: e.tensor_copy(
                kpr_sb[0:64, gslot, a, :], bank[0][0:64, :]),
                waits=W((s_te, te_k)))
            ve_kevac = onv(lambda e, a=a, gslot=gslot: e.tensor_copy(
                kpr_sb[64:128, gslot, a, :], bank[1][64:128, :]))

        onv(lambda e, gslot=gslot: e.memset(vpv_sb[:, gslot, :], 1.0),
            waits=W((s_te, p.v(s_te))))
        for tt in range(4):
            for oc in range(2):
                bnk = bank[2 + oc]
                for k in range(KT):
                    ont(lambda e, tt=tt, oc=oc, k=k, gslot=gslot, bnk=bnk: e.matmul(
                        bnk[:, :],
                        xnf_sb[:, gslot, k, tt * 128:(tt + 1) * 128],
                        wvT_sb[:, k, oc * 512:(oc + 1) * 512],
                        start=(k == 0), stop=(k == KT - 1)),
                        waits=(W((s_ve, p.v(s_ve)),) if k == 0 else ()),
                        incs=([(s_te, 1)] if k == KT - 1 else ()))
                te_v = p.v(s_te)
                onv(lambda e, tt=tt, oc=oc, gslot=gslot, bnk=bnk: e.tensor_copy(
                    vpv_sb[:, gslot, :].rearrange("p (h c w) -> p h c w",
                                                  h=4, c=NCP)[:, tt, oc * 8:(oc + 1) * 8, 0:64],
                    bnk[:, :].rearrange("p (c w) -> p c w", c=8)),
                    waits=W((s_te, te_v)))

        ve_bank3 = 0
        for tl in range(4):
            ve_bank3 = emit_own_routing_tile(g * 4 + tl, gslot, xnf_ready)
        xnffree_te[gslot] = p.v(s_te)

        def tail_act(pend):
            pobk, pte_o, ph, pch = pend
            pobank = bank[4 + pobk]
            se_ln = ons(lambda e, pobank=pobank: e.activation(
                lnr_sb[:], pobank[64:65, :], AF.Ln, bias=0.0, scale=1.0),
                waits=W((s_te, pte_o)))
            se_rc = ons(lambda e: e.activation(
                rcp_sb[:], lnr_sb[:], AF.Exp, bias=0.0, scale=-1.0),
                waits=W((s_te, terb_prev[0])))
            return se_ln, se_rc

        def tail_rest(pend, se_ln, se_rc, gslot=gslot):
            pobk, pte_o, ph, pch = pend
            pobank = bank[4 + pobk]
            ont(lambda e: e.matmul(
                bank[6][0:64, :], ones1f_sb[0:1, 0:64], rcp_sb[:],
                start=True, stop=True),
                waits=W((s_se, se_rc), (s_ve, rcpbfree_ve[0])),
                incs=[(s_te, 1)])
            te_rb = p.v(s_te)
            terb_prev[0] = te_rb
            rcpbfree_ve[0] = onv(lambda e: e.tensor_copy(
                rcpb_sb[:], bank[6][0:64, :]), waits=W((s_te, te_rb)))
            ve_oev = onv(lambda e, pobank=pobank: e.tensor_copy(
                oev_sb[:], pobank[0:64, :]))
            ve_l = ve_oev
            for ci in range(4):
                cplane = pch * 4 + ci
                ptile, phalf = cplane // 2, cplane % 2
                ve_l = onv(lambda e, ci=ci, ptile=ptile, phalf=phalf, ph=ph,
                           gslot=gslot: e.tensor_tensor(
                    xo_sb[64 * phalf:64 * phalf + 64, gslot, ptile,
                          ph * 128:(ph + 1) * 128],
                    oev_sb[:, ci * 128:(ci + 1) * 128],
                    rcpb_sb[:, ci * 128:(ci + 1) * 128], ALU.mult))
            obank_free[pobk] = (se_ln, ve_oev)
            return ve_l

        pending = None
        ve_xo_last = 0
        for h in range(4):
            for ch in range(4):
                obk = ch % 2
                obank = bank[4 + obk]
                se_e1 = {}

                def sc_exp(a, extra_ve=0, h=h, ch=ch, gslot=gslot):
                    pair = a % 2
                    sb0, sb1 = bank[2 * pair], bank[2 * pair + 1]
                    w0 = [(s_se, last_exp_pair[pair])]
                    if extra_ve:
                        w0.append((s_ve, extra_ve))
                    ont(lambda e, a=a, h=h, ch=ch, gslot=gslot, sb0=sb0: e.matmul(
                        sb0[:, :],
                        kpr_sb[0:64, gslot, a, h * 128:(h + 1) * 128],
                        qhc_sb[0:64, gslot, h * 2048 + ch * 512:h * 2048 + (ch + 1) * 512],
                        start=True, stop=True, tile_position=(0, 0)),
                        waits=w0, incs=[(s_te, 1)])
                    ont(lambda e, a=a, h=h, ch=ch, gslot=gslot, sb1=sb1: e.matmul(
                        sb1[:, :],
                        kpr_sb[64:128, gslot, a, h * 128:(h + 1) * 128],
                        qhc_sb[64:128, gslot, h * 2048 + ch * 512:h * 2048 + (ch + 1) * 512],
                        start=True, stop=True, tile_position=(64, 0)),
                        incs=[(s_te, 1)])
                    ts = p.v(s_te)
                    se = ons(lambda e, pair=pair, a=a: e.activation(
                        ebuf_sb[:, a % 2, :], sb_pair_ap[pair],
                        AF.Exp, bias=0.0, scale=1.0),
                        waits=W((s_te, ts)))
                    last_exp_pair[pair] = se
                    se_e1[a] = se

                def pv(a, h=h, gslot=gslot, obank=obank, obk=obk):
                    w0 = [(s_se, se_e1[a])]
                    if a == 0:
                        w0.append((s_ve, obank_free[obk][1]))
                    ont(lambda e, a=a, h=h, gslot=gslot, obank=obank: e.matmul(
                        obank[0:65, :],
                        vpv_sb[:, gslot, (h * NCP + 2 * a) * 65:(h * NCP + 2 * a) * 65 + 65],
                        ebuf_sb[:, a % 2, 0:512],
                        start=(a == 0), stop=False),
                        waits=w0)
                    ont(lambda e, a=a, h=h, gslot=gslot, obank=obank: e.matmul(
                        obank[0:65, :],
                        vpv_sb[:, gslot, (h * NCP + 2 * a + 1) * 65:(h * NCP + 2 * a + 1) * 65 + 65],
                        ebuf_sb[:, a % 2, 512:1024],
                        start=False, stop=(a == NCP // 2 - 1)),
                        incs=([(s_te, 1)] if a == NCP // 2 - 1 else ()))

                if pending is not None:
                    se_ln, se_rc = tail_act(pending)
                first = (h == 0 and ch == 0)
                sc_exp(0, extra_ve=(ve_kevac if first else 0))
                sc_exp(1, extra_ve=(ve_bank3 if first else 0))
                if pending is not None:
                    ve_xo_last = tail_rest(pending, se_ln, se_rc)
                    pending = None
                for a in range(2, NCP // 2):
                    pv(a - 2)
                    sc_exp(a)
                pv(NCP // 2 - 2)
                pv(NCP // 2 - 1)
                pending = (obk, p.v(s_te), h, ch)
                emit_some_remote(4)
        se_ln, se_rc = tail_act(pending)
        ve_xo_last = tail_rest(pending, se_ln, se_rc)
        pending = None

        for mt in range(KT):
            pbank = bank[mt % 2]
            if mt < 2:
                w0 = [(s_ve, max(ve_xo_last, opjfree_ve[mt % 2])),
                      (s_se, last_exp_pair[0])]
            else:
                w0 = [(s_ve, opjfree_ve[mt % 2])]
            for k in range(KT):
                ont(lambda e, mt=mt, k=k, gslot=gslot, pbank=pbank: e.matmul(
                    pbank[:, :],
                    woT_sb[:, k, mt * 128:(mt + 1) * 128],
                    xo_sb[:, gslot, k, :],
                    start=(k == 0), stop=(k == KT - 1)),
                    waits=(w0 if k == 0 else ()),
                    incs=([(s_te, 1)] if k == KT - 1 else ()))
            te_op = p.v(s_te)
            ve_opj = onv(lambda e, mt=mt, pbank=pbank: e.tensor_copy(
                opj_sb[:, mt % 2, :], pbank[:, :]),
                waits=W((s_te, te_op), (s_ats[mt % 2], p.v(s_ats[mt % 2]))))
            opjfree_ve[mt % 2] = ve_opj
            p.on("sync", lambda e, g=g, mt=mt: e.dma_start(
                attn_dram[mt * 128:(mt + 1) * 128, g * 512:(g + 1) * 512],
                opj_sb[:, mt % 2, :]),
                waits=W((s_ve, ve_opj)), incs=[(s_ats[mt % 2], 16)])
        p.on("sync", lambda e: e.nop(),
             waits=W((s_ats[0], p.v(s_ats[0])), (s_ats[1], p.v(s_ats[1]))),
             incs=[(s_atready, 1)])

    emit_some_remote(10 ** 9)
    atready_all = p.v(s_atready)

    onv(lambda e: e.tensor_scalar(
        thrr_sb[:, 0:8], remprev_sb[:, 0:8], -1.0, float(CAP), ALU.mult, ALU.add))
    onv(lambda e: e.tensor_tensor(tmpe_sb[:], remall_sb[:], run_sb[:, 0:8], ALU.add))
    onv(lambda e: e.tensor_tensor(tmpe_sb[:], tmpe_sb[:], remprev_sb[:, 8:16], ALU.add))
    ve_thr = onv(lambda e: e.tensor_scalar(
        thrr_sb[:, 8:16], tmpe_sb[:], -1.0, float(CAP), ALU.mult, ALU.add))
    p.on("sync", lambda e: e.dma_start(thr_dram[:], thrr_sb[:]),
         waits=W((s_ve, ve_thr)), incs=[(s_msc, 16)])
    p.on("sync", lambda e: e.dma_start(
        thr1b_sb[:], thr_dram[0:1, 0:8].partition_broadcast(128)),
        waits=W((s_msc, p.v(s_msc))), incs=[(s_msc, 16)])
    p.on("sync", lambda e: e.dma_start(
        thr2b_sb[:], thr_dram[0:1, 8:16].partition_broadcast(128)),
        waits=W((s_msc, p.v(s_msc))), incs=[(s_msc, 16)])
    msc_thr = p.v(s_msc)

    barrier()
    es1.close()

    es2 = ExitStack()
    SB2 = lambda name, shape, dtype: es2.enter_context(nc.sbuf_tensor(name, shape, dtype))
    glist_sb = SB2("glist_sb", [128, 2, NQ], I32)
    dtm_sb = SB2("dtm_sb", [128, 2, D], BF16)
    dfm_sb = SB2("dfm_sb", [128, 2, KT, CHAT], BF16)
    wgb_sb = SB2("wgb_sb", [128, 2, KT, 256], BF16)
    wdh_sb = SB2("wdh_sb", [128, 2, FT, 512], BF16)
    hfm_sb = SB2("hfm_sb", [128, FT, CHAT], BF16)
    eo_sb = SB2("eo_sb", [128, 2, 512], BF16)


    CH1 = (512, 512, 256)
    dtm_use = [0, 0]
    prev_dg = (0, 0)
    eo_use = [0, 0]
    wgbfree_se = [0, 0]
    wdhfree_ve = [0, 0]
    dfmfree_se = [0, 0]
    gelu_cnt = {0: 0, 1: 0, 2: 0}
    for ei in range(n_exp):
        eslot = ei % 2
        p.on("sync", lambda e, ei=ei, eslot=eslot: e.dma_start(
            glist_sb[:, eslot, :],
            glist_dram[ei * CHAT:(ei + 1) * CHAT, 0].rearrange("(p q) -> p q", q=NQ)),
            waits=W((s_scat[0], 16 * scat_use[0]), (s_scat[1], 16 * scat_use[1]),
                    (s_glist, 16 * ei),
                    (s_dg[0], prev_dg[0]), (s_dg[1], prev_dg[1])),
            incs=[(s_glist, 16)])
        gl_n = p.v(s_glist)
        for q in range(NQ):
            dslot = q % 2
            p.on("gpsimd", lambda e, q=q, dslot=dslot, eslot=eslot: e.indirect_dma_start(
                dtm_sb[:, dslot, :], None, xn_dram[:],
                IndirectOffsetOnAxis(glist_sb[:, eslot, q:q + 1], 0),
                ),
                waits=W((s_glist, gl_n), (s_dts[dslot], 16 * dtm_use[dslot]),
                        (s_xns[0], 16 * xns_use[0]), (s_xns[1], 16 * xns_use[1]),
                        (s_const, p.v(s_const))),
                incs=[(s_dg[dslot], 16)])
            p.on("sync", lambda e, ei=ei, q=q, dslot=dslot: e.dma_start(
                disp_dram[ei * CHAT + q * 128:ei * CHAT + (q + 1) * 128, :],
                dtm_sb[:, dslot, :]),
                waits=W((s_dg[dslot], p.v(s_dg[dslot]))),
                incs=[(s_dts[dslot], 16)])
            dtm_use[dslot] += 1
        prev_dg = (p.v(s_dg[0]), p.v(s_dg[1]))
        p.on("sync", lambda e: e.nop(),
             waits=W((s_dts[0], p.v(s_dts[0])), (s_dts[1], p.v(s_dts[1]))),
             incs=[(s_dready, 1)])
        for kp in range(KT):
            p.on("sync", lambda e, ei=ei, kp=kp, eslot=eslot: e.dma_start_transpose(
                dfm_sb[:, eslot, kp, :],
                disp_dram[ei * CHAT:(ei + 1) * CHAT, kp * 128:(kp + 1) * 128]),
                waits=W((s_dready, ei + 1), (s_se, dfmfree_se[eslot]),
                        (s_se, p.v(s_se))),
                incs=[(s_dfm[eslot], 16)])
        dfm_ready = p.v(s_dfm[eslot])

        for fg in range(FT // 2):
            wslot = fg % 2
            p.on("scalar", lambda e, ei=ei, fg=fg, wslot=wslot: e.dma_start(
                wgb_sb[:, wslot, :, :],
                wgate_bf[ei, :, fg * 256:(fg + 1) * 256].rearrange(
                    "(a p) f -> p a f", p=128)),
                waits=W((s_se, wgbfree_se[wslot]), (s_wpc, 32 * ei + 16)),
                incs=[(s_wgb[wslot], 16)])
            wgb_n = p.v(s_wgb[wslot])
            for fi in range(2):
                ft = fg * 2 + fi
                co = 0
                for ci, cw in enumerate(CH1):
                    bnk = bank[ci]
                    for k in range(KT):
                        ont(lambda e, k=k, cw=cw, co=co, fi=fi, wslot=wslot,
                            eslot=eslot, bnk=bnk: e.matmul(
                                bnk[:, 0:cw],
                                wgb_sb[:, wslot, k, fi * 128:(fi + 1) * 128],
                                dfm_sb[:, eslot, k, co:co + cw],
                                start=(k == 0), stop=(k == KT - 1)),
                            waits=(W((s_wgb[wslot], wgb_n), (s_dfm[eslot], dfm_ready),
                                     (s_se, gelu_cnt[ci])) if k == 0 else ()),
                            incs=([(s_te, 1)] if k == KT - 1 else ()))
                    te_m1 = p.v(s_te)
                    gelu_cnt[ci] = ons(lambda e, ft=ft, cw=cw, co=co, bnk=bnk:
                                       e.activation(
                                           hfm_sb[:, ft, co:co + cw], bnk[:, 0:cw],
                                           AF.Gelu, bias=0.0, scale=1.0),
                                       waits=W((s_te, te_m1)))
                    if fi == 1 and ci == len(CH1) - 1:
                        wgbfree_se[wslot] = gelu_cnt[ci]
                    co += cw
        se_h = p.v(s_se)

        for dh_ in range(2):
            wdslot = dh_
            wduse = ei * 2 + dh_
            for part in range(2):
                p.on("scalar", lambda e, ei=ei, dh_=dh_, part=part, wdslot=wdslot:
                     e.dma_start(
                         wdh_sb[:, wdslot, part * 16:(part + 1) * 16, :],
                         wdown_bf[ei, part * 2048:(part + 1) * 2048,
                                  dh_ * 512:(dh_ + 1) * 512].rearrange(
                             "(a p) d -> p a d", p=128)),
                     waits=W((s_ve, wdhfree_ve[wdslot]), (s_wpc, 32 * (ei + 1))),
                     incs=[(s_wdh[wdslot], 16)])
            wdh_n = p.v(s_wdh[wdslot])
            for rt in range(NQ):
                bnk = bank[3 + rt % 2]
                for k in range(FT):
                    ont(lambda e, rt=rt, k=k, wdslot=wdslot, bnk=bnk: e.matmul(
                        bnk[:, :],
                        hfm_sb[:, k, rt * 128:(rt + 1) * 128],
                        wdh_sb[:, wdslot, k, :],
                        start=(k == 0), stop=(k == FT - 1)),
                        waits=(W((s_wdh[wdslot], wdh_n), (s_se, se_h),
                                 (s_ve, p.v(s_ve))) if k == 0 else ()),
                        incs=([(s_te, 1)] if k == FT - 1 else ()))
                te_m2 = p.v(s_te)
                eslot2 = rt % 2
                ve_eo = onv(lambda e, bnk=bnk, eslot2=eslot2: e.tensor_copy(
                    eo_sb[:, eslot2, 0:512], bnk[:, :]),
                    waits=W((s_te, te_m2), (s_eos[eslot2], 16 * eo_use[eslot2])))
                if rt == NQ - 1:
                    wdhfree_ve[wdslot] = ve_eo
                p.on("sync", lambda e, ei=ei, rt=rt, dh_=dh_, eslot2=eslot2: e.dma_start(
                    eo_dram[ei * CHAT + rt * 128:ei * CHAT + (rt + 1) * 128,
                            dh_ * 512:(dh_ + 1) * 512],
                    eo_sb[:, eslot2, 0:512]),
                    waits=W((s_ve, ve_eo)), incs=[(s_eos[eslot2], 16)])
                eo_use[eslot2] += 1
        dfmfree_se[eslot] = se_h
    p.on("sync", lambda e: e.nop(),
         waits=W((s_eos[0], p.v(s_eos[0])), (s_eos[1], p.v(s_eos[1]))),
         incs=[(s_eoready, 1)])
    eoready = p.v(s_eoready)

    barrier()
    es2.close()
    es3 = ExitStack()
    SB3 = lambda name, shape, dtype: es3.enter_context(nc.sbuf_tensor(name, shape, dtype))
    cgA_sb = SB3("cgA_sb", [128, 2, D], BF16)
    cgB_sb = SB3("cgB_sb", [128, 2, D], BF16)
    att_sb = SB3("att_sb", [128, 2, D], BF16)
    xr2_sb = SB3("xr2_sb", [128, 2, D], F32)
    outb_sb = SB3("outb_sb", [128, 2, D], F32)
    ga_sb = SB3("ga_sb", [128, 1], F32)
    gb_sb = SB3("gb_sb", [128, 1], F32)
    cmbfree_ve = [0, 0]
    attfree_ve = [0, 0]
    out_use = [0, 0]
    for t in range(nt):
        slot = t % 2
        p.on("gpsimd", lambda e, t=t, slot=slot: e.indirect_dma_start(
            cgA_sb[:, slot, :], None, eo_dram[:],
            IndirectOffsetOnAxis(cidx1_sb[:, t:t + 1], 0)),
            waits=W((s_eoready, eoready), (s_ve, cmbfree_ve[slot]),
                    (s_const, p.v(s_const))),
            incs=[(s_cmb[slot], 16)])
        p.on("gpsimd", lambda e, t=t, slot=slot: e.indirect_dma_start(
            cgB_sb[:, slot, :], None, eo_dram[:],
            IndirectOffsetOnAxis(cidx2_sb[:, t:t + 1], 0)),
            incs=[(s_cmb[slot], 16)])
        p.on("sync", lambda e, t=t, slot=slot: e.dma_start_transpose(
            att_sb[:, slot, :], attn_dram[:, t * 128:(t + 1) * 128]),
            waits=W((s_atready, atready_all), (s_ve, attfree_ve[slot]),
                    (s_se, p.v(s_se))),
            incs=[(s_att[slot], 16)])
        p.on("sync", lambda e, t=t, slot=slot: e.dma_start(
            xr2_sb[:, slot, :], x_own[t * 128:(t + 1) * 128, :]),
            waits=W((s_ve, attfree_ve[slot])),
            incs=[(s_att[slot], 16)])
        cmb_n = p.v(s_cmb[slot])
        att_n = p.v(s_att[slot])
        onv(lambda e, t=t: e.tensor_tensor(
            tmpa_sb[:], thr1b_sb[:], oh1_sb[:, t, :], ALU.mult),
            waits=W((s_msc, msc_thr)))
        onv(lambda e: e.reduce_sum(tmpc_sb[:], tmpa_sb[:], axis=AX.X))
        onv(lambda e, t=t: e.tensor_tensor(
            tmpd_sb[:], loc1_sb[:, t:t + 1], tmpc_sb[:], ALU.is_lt))
        onv(lambda e, t=t: e.tensor_tensor(
            ga_sb[:], g1n_sb[:, t:t + 1], tmpd_sb[:], ALU.mult))
        onv(lambda e, t=t: e.tensor_tensor(
            tmpa_sb[:], thr2b_sb[:], oh2_sb[:, t, :], ALU.mult))
        onv(lambda e: e.reduce_sum(tmpc_sb[:], tmpa_sb[:], axis=AX.X))
        onv(lambda e, t=t: e.tensor_tensor(
            tmpd_sb[:], loc2_sb[:, t:t + 1], tmpc_sb[:], ALU.is_lt))
        onv(lambda e, t=t: e.tensor_tensor(
            gb_sb[:], g2n_sb[:, t:t + 1], tmpd_sb[:], ALU.mult))
        onv(lambda e, slot=slot: e.tensor_scalar(
            outb_sb[:, slot, :], cgA_sb[:, slot, :], ga_sb[:], None, ALU.mult),
            waits=W((s_cmb[slot], cmb_n), (s_out[slot], 16 * out_use[slot])))
        onv(lambda e, slot=slot: e.scalar_tensor_tensor(
            outb_sb[:, slot, :], cgB_sb[:, slot, :], gb_sb[:], outb_sb[:, slot, :],
            ALU.mult, ALU.add))
        cmbfree_ve[slot] = onv(lambda e, slot=slot: e.tensor_tensor(
            outb_sb[:, slot, :], outb_sb[:, slot, :], att_sb[:, slot, :], ALU.add),
            waits=W((s_att[slot], att_n)))
        ve_out = onv(lambda e, slot=slot: e.tensor_tensor(
            outb_sb[:, slot, :], outb_sb[:, slot, :], xr2_sb[:, slot, :], ALU.add))
        attfree_ve[slot] = ve_out
        p.on("sync", lambda e, t=t, slot=slot: e.dma_start(
            y_out[t * 128:(t + 1) * 128, :], outb_sb[:, slot, :]),
            waits=W((s_ve, ve_out)), incs=[(s_out[slot], 16)])
        out_use[slot] += 1
    p.on("sync", None, waits=W((s_out[0], p.v(s_out[0])), (s_out[1], p.v(s_out[1]))))

    if dbg:
        dbg_xn = nc.dram_tensor("dbg_xn", [TOWN + 1, D], BF16, kind="ExternalOutput").ap()
        dbg_attn = nc.dram_tensor("dbg_attn", [D, TOWN], BF16, kind="ExternalOutput").ap()
        dbg_gl = nc.dram_tensor("dbg_gl", [E * CHAT + 1, 1], I32, kind="ExternalOutput").ap()
        dbg_eo = nc.dram_tensor("dbg_eo", [E * CHAT + 1, D], BF16, kind="ExternalOutput").ap()
        dbg_thr = nc.dram_tensor("dbg_thr", [1, 16], F32, kind="ExternalOutput").ap()
        dbg_rt = nc.dram_tensor("dbg_rt", [128, 8 * NT], F32, kind="ExternalOutput").ap()
        dbg_cnt = nc.dram_tensor("dbg_cnt", [1, 48], F32, kind="ExternalOutput").ap()
        dbg_disp = nc.dram_tensor("dbg_disp", [E * CHAT, D], BF16, kind="ExternalOutput").ap()
        s_dbg = p.sem("s_dbg")
        rtbuf = es3.enter_context(nc.sbuf_tensor("rtbuf", [128, 8 * NT], F32))
        ve0 = p.v(s_ve)
        onv(lambda e: e.tensor_copy(rtbuf[:, 0:NT], g1n_sb[:]))
        onv(lambda e: e.tensor_copy(rtbuf[:, NT:2 * NT], g2n_sb[:]))
        onv(lambda e: e.tensor_copy(rtbuf[:, 2 * NT:3 * NT], loc1_sb[:]))
        onv(lambda e: e.tensor_copy(rtbuf[:, 3 * NT:4 * NT], loc2_sb[:]))
        onv(lambda e: e.tensor_copy(rtbuf[:, 4 * NT:5 * NT], cidx1_sb[:]))
        onv(lambda e: e.tensor_copy(rtbuf[:, 5 * NT:6 * NT], cidx2_sb[:]))
        onv(lambda e: e.tensor_copy(rtbuf[:, 6 * NT:7 * NT], g1n_sb[:]))
        ved = onv(lambda e: e.tensor_copy(rtbuf[:, 7 * NT:8 * NT], g2n_sb[:]))
        p.on("sync", lambda e: e.dma_start(dbg_xn[:], xn_dram[:]),
             waits=W((s_out[0], p.v(s_out[0])), (s_out[1], p.v(s_out[1])),
                     (s_ve, ved)),
             incs=[(s_dbg, 16)])
        p.on("sync", lambda e: e.dma_start(dbg_attn[:], attn_dram[:]), incs=[(s_dbg, 16)])
        p.on("sync", lambda e: e.dma_start(dbg_gl[:], glist_dram[:]), incs=[(s_dbg, 16)])
        p.on("sync", lambda e: e.dma_start(dbg_eo[:], eo_dram[:]), incs=[(s_dbg, 16)])
        p.on("sync", lambda e: e.dma_start(dbg_thr[:], thr_dram[:]), incs=[(s_dbg, 16)])
        p.on("sync", lambda e: e.dma_start(dbg_rt[:], rtbuf[:]), incs=[(s_dbg, 16)])
        dbg_rem = nc.dram_tensor("dbg_rem", [1, RU * 4], BF16, kind="ExternalOutput").ap()
        def _dbgcp(e):
            with nc.allow_low_precision("dbg"):
                return e.tensor_copy(dbgrem_sb[:], dbgrem_sb[:])
        vrem = onv(_dbgcp)
        p.on("sync", lambda e: e.dma_start(dbg_rem[:], dbgrem_sb[:]),
             waits=W((s_ve, vrem)), incs=[(s_dbg, 16)])
        cntbuf = es3.enter_context(nc.sbuf_tensor("cntbuf", [1, 48], F32))
        onv(lambda e: e.tensor_copy(cntbuf[:, 0:16], remprev_sb[:]))
        onv(lambda e: e.tensor_copy(cntbuf[:, 16:24], remall_sb[:]))
        vec = onv(lambda e: e.tensor_copy(cntbuf[:, 24:48], run_sb[:]))
        p.on("sync", lambda e: e.dma_start(dbg_cnt[:], cntbuf[:]),
             waits=W((s_ve, vec)), incs=[(s_dbg, 16)])
        p.on("sync", lambda e: e.dma_start(dbg_disp[:], disp_dram[:]), incs=[(s_dbg, 16)])
        p.on("sync", None, waits=W((s_dbg, p.v(s_dbg))))

    sems = {}
    for name in p.sem_names:
        sems[name] = es.enter_context(nc.semaphore(name))
    block = es.enter_context(nc.Block())
    p.realize(block, sems)
    es3.close()
    es.close()
    return nc


def make_in_maps(x, wq, wk, wv, wo, wg, w_gate, w_down):
    in_maps = []
    for c in range(NC):
        own = np.ascontiguousarray(x[c * BPC:(c + 1) * BPC].reshape(TOWN, D))
        rem = np.ascontiguousarray(
            np.concatenate([x[b] for b in range(B) if b // BPC != c], axis=0))
        prevcols = np.zeros((128, NRB), np.float32)
        prevcols[:, :2 * c] = 1.0
        in_maps.append(dict(
            x_own=own, x_rem=rem, wq=wq, wk=wk, wv=wv, wo=wo, wg=wg,
            w_gate=w_gate, w_down=w_down, prevcols=prevcols))
    return in_maps


def kernel(L0_attn_x, L0_attn_wq, L0_attn_wk, L0_attn_wv, L0_attn_wo,
           L0_ffn_wg, L0_ffn_w_gate, L0_ffn_w_down):
    x = np.ascontiguousarray(np.asarray(L0_attn_x))
    nc = build_program()
    in_maps = make_in_maps(
        x, np.asarray(L0_attn_wq), np.asarray(L0_attn_wk), np.asarray(L0_attn_wv),
        np.asarray(L0_attn_wo), np.asarray(L0_ffn_wg), np.asarray(L0_ffn_w_gate),
        np.asarray(L0_ffn_w_down))
    res = run_bass_kernel_spmd(nc, in_maps, core_ids=list(range(NC)))
    out = np.concatenate([r["y"].reshape(BPC, S, D) for r in res.results], axis=0)
    return out.astype(np.float32)


if __name__ == "__main__":
    import reference as ref
    inputs = ref.setup_inputs()
    out = kernel(**{k: np.asarray(v) for k, v in inputs.items()})
    exp = np.asarray(ref.reference(**inputs))
    rvr = float(((out - exp) ** 2).mean() / (exp ** 2).mean())
    print("resid var ratio:", rvr)


# version 07
# speedup vs baseline: 1.3898x; 1.0745x over previous
import numpy as np
import ml_dtypes
from contextlib import ExitStack

import concourse.bass as bass
import concourse.mybir as mybir
from concourse.bass import IndirectOffsetOnAxis
from concourse.bass_utils import run_bass_kernel_spmd

F32 = mybir.dt.float32
BF16 = mybir.dt.bfloat16
I32 = mybir.dt.int32
U32 = mybir.dt.uint32
AF = mybir.ActivationFunctionType
ALU = mybir.AluOpType
AX = mybir.AxisListType

B, S, D = 16, 2048, 1024
E, F = 8, 4096
CAP = 2 * B * S // E
NC = 8
BPC = B // NC
TOWN = BPC * S
NT = TOWN // 128
NCP = D // 64
KT = D // 128
FT = F // 128
NRB = B - BPC
RU = NRB * 4
CHAT = 1280
NQ = CHAT // 128
EXPM = -16.0
ZROW_XN = TOWN
ZROW_EO = E * CHAT


def bf(a):
    return np.asarray(a).astype(ml_dtypes.bfloat16)


class Prog:
    ENGINES = ("sync", "gpsimd", "tensor", "vector", "scalar")

    def __init__(self):
        self.ops = {e: [] for e in self.ENGINES}
        self.cnt = {}
        self.sem_names = []

    def sem(self, name):
        self.sem_names.append(name)
        self.cnt[name] = 0
        return name

    def on(self, eng, build, waits=(), incs=()):
        for s, n in incs:
            self.cnt[s] += n
        self.ops[eng].append((tuple(waits), tuple(incs), build))

    def v(self, s):
        return self.cnt[s]

    def realize(self, block, sems):
        for eng_name in self.ENGINES:
            ops = self.ops[eng_name]

            def runner(e, ops=ops):
                for waits, incs, build in ops:
                    for s, val in waits:
                        if val > 0:
                            e.wait_ge(sems[s], val)
                    ins = build(e) if build is not None else None
                    if incs:
                        assert ins is not None
                        for s, n in incs:
                            ins.then_inc(sems[s], n)

            getattr(block, eng_name)(runner)


def build_program(dev=0, dbg=False):
    nc = bass.Bass("TRN2", target_bir_lowering=False, debug=False)
    p = Prog()

    nt = NT if dev == 0 else min(NT, 4 * dev)
    ng = nt // 4
    n_ru = RU if dev == 0 else min(RU, dev)
    n_exp = E if dev == 0 else min(E, 2 * dev)

    x_own = nc.dram_tensor("x_own", [TOWN, D], F32, kind="ExternalInput").ap()
    x_rem = nc.dram_tensor("x_rem", [NRB * S, D], F32, kind="ExternalInput").ap()
    wq_in = nc.dram_tensor("wq", [D, D], F32, kind="ExternalInput").ap()
    wk_in = nc.dram_tensor("wk", [D, D], F32, kind="ExternalInput").ap()
    wv_in = nc.dram_tensor("wv", [D, D], F32, kind="ExternalInput").ap()
    wo_in = nc.dram_tensor("wo", [D, D], F32, kind="ExternalInput").ap()
    wg_in = nc.dram_tensor("wg", [D, E], F32, kind="ExternalInput").ap()
    wgate_in = nc.dram_tensor("w_gate", [E, D, F], F32, kind="ExternalInput").ap()
    wdown_in = nc.dram_tensor("w_down", [E, F, D], F32, kind="ExternalInput").ap()
    prevcols_in = nc.dram_tensor("prevcols", [128, NRB], F32, kind="ExternalInput").ap()
    y_out = nc.dram_tensor("y", [TOWN, D], F32, kind="ExternalOutput").ap()

    xn_dram = nc.dram_tensor("xn_dram", [TOWN + 1, D], BF16, kind="Internal").ap()
    xrem_bf = nc.dram_tensor("xrem_bf", [NRB * S, D], BF16, kind="Internal").ap()
    wqt_d = nc.dram_tensor("wqt_d", [D, D], BF16, kind="Internal").ap()
    wkt_d = nc.dram_tensor("wkt_d", [D, D], BF16, kind="Internal").ap()
    wvt_d = nc.dram_tensor("wvt_d", [D, D], BF16, kind="Internal").ap()
    wot_d = nc.dram_tensor("wot_d", [D, D], BF16, kind="Internal").ap()
    attn_dram = nc.dram_tensor("attn_dram", [D, TOWN], BF16, kind="Internal").ap()
    disp_dram = nc.dram_tensor("disp_dram", [E * CHAT, D], BF16, kind="Internal").ap()
    eo_dram = nc.dram_tensor("eo_dram", [E * CHAT + 1, D], BF16, kind="Internal").ap()
    glist_dram = nc.dram_tensor("glist_dram", [E * CHAT + 1, 1], I32, kind="Internal").ap()
    thr_dram = nc.dram_tensor("thr_dram", [1, 16], F32, kind="Internal").ap()
    wgate_bf = nc.dram_tensor("wgate_bf", [E, D, F], BF16, kind="Internal").ap()
    wdown_bf = nc.dram_tensor("wdown_bf", [E, F, D], BF16, kind="Internal").ap()
    cs_dram = nc.dram_tensor("cs_dram", [1, E], F32, kind="Internal").ap()

    tri_d = nc.inline_tensor(bf(np.triu(np.ones((128, 128), np.float32), 1)), name="ctri")
    onesc_d = nc.inline_tensor(bf(np.ones((128, 1), np.float32)), name="conesc")
    onesr_d = nc.inline_tensor(bf(np.ones((1, 128), np.float32)), name="conesr")
    ones1f_d = nc.inline_tensor(np.ones((1, 128), np.float32), name="cones1f")
    iota8_d = nc.inline_tensor(
        np.repeat(np.arange(E, dtype=np.float32)[None, :], 128, 0), name="ciota8")
    tokid_d = nc.inline_tensor(
        (np.arange(128, dtype=np.int32)[:, None]
         + 128 * np.arange(NT, dtype=np.int32)[None, :]), name="ctokid")
    glpre_d = nc.inline_tensor(np.full((E * CHAT + 1, 1), ZROW_XN, np.int32), name="cglpre")

    es = ExitStack()
    SB = lambda name, shape, dtype: es.enter_context(nc.sbuf_tensor(name, shape, dtype))
    PSB = lambda name, n=512: es.enter_context(nc.psum_tensor(name, [128, n], F32))
    bankp = [PSB("pbp0", 1024), PSB("pbp1", 1024)]
    bank = [bankp[0][:, 0:512], bankp[0][:, 512:1024],
            bankp[1][:, 0:512], bankp[1][:, 512:1024],
            PSB("pb4"), PSB("pb5"), PSB("pb6"), PSB("pb7")]
    sb_pair_ap = [bankp[0][:, :], bankp[1][:, :]]

    tri_sb = SB("tri_sb", [128, 128], BF16)
    onesc_sb = SB("onesc_sb", [128, 1], BF16)
    onesr_sb = SB("onesr_sb", [1, 128], BF16)
    ones1f_sb = SB("ones1f_sb", [1, 128], F32)
    iota8_sb = SB("iota8_sb", [128, E], F32)
    tokid_sb = SB("tokid_sb", [128, NT], I32)
    wg_sb = SB("wg_sb", [128, KT, E], BF16)
    wgo_sb = SB("wgo_sb", [128, KT, E + 1], BF16)
    csb_sb = SB("csb_sb", [128, E], F32)
    prevc_sb = SB("prevc_sb", [128, NRB], BF16)
    oh1_sb = SB("oh1_sb", [128, NT, E], BF16)
    oh2_sb = SB("oh2_sb", [128, NT, E], BF16)
    g1n_sb = SB("g1n_sb", [128, NT], F32)
    g2n_sb = SB("g2n_sb", [128, NT], F32)
    loc1_sb = SB("loc1_sb", [128, NT], F32)
    loc2_sb = SB("loc2_sb", [128, NT], F32)
    cidx1_sb = SB("cidx1_sb", [128, NT], I32)
    cidx2_sb = SB("cidx2_sb", [128, NT], I32)
    run_sb = SB("run_sb", [1, 24], F32)
    remprev_sb = SB("remprev_sb", [1, 16], F32)
    remall_sb = SB("remall_sb", [1, 8], F32)
    thr1b_sb = SB("thr1b_sb", [128, 8], F32)
    thr2b_sb = SB("thr2b_sb", [128, 8], F32)
    scr_sb = SB("scr_sb", [128, E + 1], F32)
    scr2_sb = SB("scr2_sb", [128, E], F32)
    mx1_sb = SB("mx1_sb", [128, 1], F32)
    mx2_sb = SB("mx2_sb", [128, 1], F32)
    ix_sb = SB("ix_sb", [128, E], U32)
    i1f_sb = SB("i1f_sb", [128, 1], F32)
    i2f_sb = SB("i2f_sb", [128, 1], F32)
    l2_sb = SB("l2_sb", [128, E], F32)
    sstat_sb = SB("sstat_sb", [128, 8], F32)
    em16_sb = SB("em16_sb", [128, 1], F32)
    sc8_sb = SB("sc8_sb", [128, 1], F32)
    eps_sb = SB("eps_sb", [128, 1], F32)
    zrow_sb = SB("zrow_sb", [1, D], BF16)
    cs1_sb = SB("cs1_sb", [1, 8], F32)
    thrr_sb = SB("thrr_sb", [1, 16], F32)
    tmpe_sb = SB("tmpe_sb", [1, 8], F32)
    pos1_sb = SB("pos1_sb", [128, 1], F32)
    pos2_sb = SB("pos2_sb", [128, 1], F32)
    tmpa_sb = SB("tmpa_sb", [128, E], F32)
    tmpn_sb = SB("tmpn_sb", [128, 1], F32)
    tmpq_sb = SB("tmpq_sb", [128, 1], F32)
    tmpw_sb = SB("tmpw_sb", [128, 1], F32)
    tmpi_sb = SB("tmpi_sb", [128, 1], I32)
    scidx_sb = [SB(f"scidx{i}_sb", [128, 1], I32) for i in range(2)]
    tmpc_sb = SB("tmpc_sb", [128, 1], F32)
    tmpd_sb = SB("tmpd_sb", [128, 1], F32)
    dbgrem_sb = SB("dbgrem_sb", [1, RU * 4], BF16) if dbg else None
    ohpr_sb = SB("ohpr_sb", [128, 4, 2 * E], BF16)
    ohpr2_sb = SB("ohpr2_sb", [128, 3 * E], BF16)

    s_const = p.sem("s_const")
    s_wcast = p.sem("s_wcast")
    s_wtr = p.sem("s_wtr")
    s_xl = [p.sem(f"s_xl{i}") for i in range(2)]
    s_xns = [p.sem(f"s_xns{i}") for i in range(2)]
    s_xnready = p.sem("s_xnready")
    s_xnf = [p.sem(f"s_xnf{i}") for i in range(2)]
    s_te = p.sem("s_te")
    s_ve = p.sem("s_ve")
    s_se = p.sem("s_se")
    s_xrc = [p.sem(f"s_xrc{i}") for i in range(3)]
    s_xrf = [p.sem(f"s_xrf{i}") for i in range(2)]
    s_scat = [p.sem(f"s_scat{i}") for i in range(2)]
    s_glpre = p.sem("s_glpre")
    s_ats = [p.sem(f"s_ats{i}") for i in range(2)]
    s_atready = p.sem("s_atready")
    s_msc = p.sem("s_msc")
    s_glist = p.sem("s_glist")
    s_dg = [p.sem(f"s_dg{i}") for i in range(2)]
    s_dts = [p.sem(f"s_dts{i}") for i in range(2)]
    s_dready = p.sem("s_dready")
    s_dfm = [p.sem(f"s_dfm{i}") for i in range(2)]
    s_wgb = [p.sem(f"s_wgb{i}") for i in range(2)]
    s_wdh = [p.sem(f"s_wdh{i}") for i in range(2)]
    s_eos = [p.sem(f"s_eos{i}") for i in range(2)]
    s_eoready = p.sem("s_eoready")
    s_cmb = [p.sem(f"s_cmb{i}") for i in range(2)]
    s_att = [p.sem(f"s_att{i}") for i in range(2)]
    s_out = [p.sem(f"s_out{i}") for i in range(2)]
    s_bar = p.sem("s_bar")

    W = lambda *pairs: list(pairs)

    def onv(build, waits=(), incs=()):
        p.on("vector", build, waits=W((s_ve, p.v(s_ve)), *waits),
             incs=[(s_ve, 1), *incs])
        return p.v(s_ve)

    def ons(build, waits=(), incs=()):
        p.on("scalar", build, waits=W((s_se, p.v(s_se)), *waits),
             incs=[(s_se, 1), *incs])
        return p.v(s_se)

    def ont(build, waits=(), incs=()):
        p.on("tensor", build, waits=waits, incs=incs)
        return p.v(s_te)

    def barrier():
        tgt = p.v(s_bar) + 5
        for eng in Prog.ENGINES:
            p.on(eng, lambda e: e.nop(), incs=[(s_bar, 1)])
        for eng in Prog.ENGINES:
            p.on(eng, None, waits=W((s_bar, tgt)))

    es1 = ExitStack()
    SB1 = lambda name, shape, dtype: es1.enter_context(nc.sbuf_tensor(name, shape, dtype))
    wqT_sb = SB1("wqT_sb", [128, KT, D], BF16)
    wkT_sb = SB1("wkT_sb", [128, KT, D], BF16)
    wvT_sb = SB1("wvT_sb", [128, KT, D], BF16)
    woT_sb = SB1("woT_sb", [128, KT, D], BF16)
    xld_sb = SB1("xld_sb", [128, 2, D], F32)
    xnt_sb = SB1("xnt_sb", [128, 2, D], BF16)
    xnf_sb = SB1("xnf_sb", [128, 2, KT, 512], BF16)
    qhc_sb = SB1("qhc_sb", [128, 2, 4 * NCP * 128], BF16)
    kpr_sb = SB1("kpr_sb", [128, 2, E, 4 * 128], BF16)
    vpv_sb = SB1("vpv_sb", [128, 2, 4 * NCP * 65], BF16)
    ebuf_sb = SB1("ebuf_sb", [128, 2, 1024], BF16)
    xo_sb = SB1("xo_sb", [128, 2, KT, 512], BF16)
    rcp_sb = SB1("rcp_sb", [1, 512], BF16)
    lnr_sb = SB1("lnr_sb", [1, 512], F32)
    rcpb_sb = SB1("rcpb_sb", [64, 512], BF16)
    oev_sb = SB1("oev_sb", [64, 512], F32)
    opj_sb = SB1("opj_sb", [128, 2, 512], BF16)
    xrf_sb = SB1("xrf_sb", [128, 2, KT, 512], BF16)

    for dst, src in ((tri_sb, tri_d), (onesc_sb, onesc_d), (onesr_sb, onesr_d),
                     (ones1f_sb, ones1f_d), (iota8_sb, iota8_d), (tokid_sb, tokid_d)):
        p.on("sync", lambda e, dst=dst, src=src: e.dma_start(dst[:], src.ap()),
             incs=[(s_const, 16)])
    p.on("sync", lambda e: e.dma_start(glist_dram[:], glpre_d.ap()), incs=[(s_glpre, 16)])

    p.on("gpsimd", lambda e: e.dma_start(wqt_d[:], wq_in[:]), incs=[(s_wcast, 16)])
    p.on("gpsimd", lambda e: e.dma_start(wkt_d[:], wk_in[:]), incs=[(s_wcast, 16)])
    p.on("gpsimd", lambda e: e.dma_start(wvt_d[:], wv_in[:]), incs=[(s_wcast, 16)])
    p.on("gpsimd", lambda e: e.dma_start(wot_d[:], wo_in[:]), incs=[(s_wcast, 16)])
    p.on("gpsimd", lambda e: e.dma_start(
        wg_sb[:], wg_in.rearrange("(a p) e -> p a e", p=128)), incs=[(s_wcast, 16)])
    p.on("gpsimd", lambda e: e.dma_start(prevc_sb[:], prevcols_in[:]),
         incs=[(s_wcast, 16)])
    wcast_all = p.v(s_wcast)

    for wsrc, wdst in ((wqt_d, wqT_sb), (wkt_d, wkT_sb), (wvt_d, wvT_sb), (wot_d, woT_sb)):
        for b_ in range(KT):
            p.on("scalar",
                 lambda e, wsrc=wsrc, wdst=wdst, b_=b_: e.dma_start_transpose(
                     wdst[:, b_, :], wsrc[:, b_ * 128:(b_ + 1) * 128]),
                 waits=W((s_wcast, wcast_all)), incs=[(s_wtr, 16)])
    wtr_all = p.v(s_wtr)
    ve_wqs = onv(lambda e: e.tensor_scalar(
        wqT_sb[:].rearrange("p a d -> p (a d)"),
        wqT_sb[:].rearrange("p a d -> p (a d)"), 0.125, None, ALU.mult),
        waits=W((s_wtr, wtr_all)))

    onv(lambda e: e.memset(em16_sb[:], EXPM))
    onv(lambda e: e.memset(sc8_sb[:], 0.125))
    onv(lambda e: e.memset(eps_sb[:], 1e-5))
    onv(lambda e: e.memset(zrow_sb[:], 0.0))
    onv(lambda e: e.memset(run_sb[:], 0.0))
    onv(lambda e: e.memset(remprev_sb[:], 0.0))
    ve_zeros = onv(lambda e: e.memset(remall_sb[:], 0.0))
    p.on("sync", lambda e: e.dma_start(xn_dram[ZROW_XN:ZROW_XN + 1, :], zrow_sb[:]),
         waits=W((s_ve, ve_zeros)), incs=[(s_const, 16)])
    p.on("sync", lambda e: e.dma_start(eo_dram[ZROW_EO:ZROW_EO + 1, :], zrow_sb[:]),
         waits=W((s_ve, ve_zeros)), incs=[(s_const, 16)])

    const6 = p.v(s_const)
    p.on("vector", None, waits=W((s_const, const6)))
    p.on("gpsimd", None, waits=W((s_const, const6)))
    p.on("tensor", None, waits=W((s_const, const6)))
    onv(lambda e: e.tensor_copy(wgo_sb[:, :, 0:E], wg_sb[:]),
        waits=W((s_wcast, wcast_all)))
    ve_wgo = onv(lambda e: e.memset(wgo_sb[:, :, E:E + 1], 1.0))
    for k in range(KT):
        ont(lambda e, k=k: e.matmul(
            bank[7][0:1, 0:E], onesc_sb[:], wg_sb[:, k, :],
            start=(k == 0), stop=(k == KT - 1)),
            waits=(W((s_wcast, wcast_all)) if k == 0 else ()),
            incs=([(s_te, 1)] if k == KT - 1 else ()))
    te_cs = p.v(s_te)
    ve_cs = onv(lambda e: e.tensor_copy(cs1_sb[:], bank[7][0:1, 0:E]),
                waits=W((s_te, te_cs)))
    p.on("sync", lambda e: e.dma_start(cs_dram[:], cs1_sb[:]),
         waits=W((s_ve, ve_cs)), incs=[(s_msc, 16)])
    p.on("sync", lambda e: e.dma_start(csb_sb[:], cs_dram.partition_broadcast(128)),
         waits=W((s_msc, p.v(s_msc))), incs=[(s_msc, 16)])
    msc_csb = p.v(s_msc)
    ve_csb = onv(lambda e: e.tensor_scalar(csb_sb[:], csb_sb[:], 1.0 / D, None, ALU.mult),
                 waits=W((s_msc, msc_csb)))

    n_store = 0
    xfree_ve = [0, 0]
    xns_use = [0, 0]
    for t in range(nt):
        slot = t % 2
        p.on("sync", lambda e, t=t, slot=slot: e.dma_start(
            xld_sb[:, slot, :], x_own[t * 128:(t + 1) * 128, :]),
            waits=W((s_ve, xfree_ve[slot])), incs=[(s_xl[slot], 16)])
        ld_n = p.v(s_xl[slot])
        onv(lambda e, slot=slot: e.reduce_sum(
            sstat_sb[:, 0:1], xld_sb[:, slot, :], axis=AX.X),
            waits=W((s_xl[slot], ld_n)))
        se_sq = ons(lambda e, slot=slot: e.activation(
            xnt_sb[:, slot, :], xld_sb[:, slot, :], AF.Square, bias=0.0, scale=1.0,
            accum_out=sstat_sb[:, 1:2]),
            waits=W((s_xl[slot], ld_n), (s_xns[slot], 16 * xns_use[slot])))
        onv(lambda e: e.tensor_scalar(
            sstat_sb[:, 2:3], sstat_sb[:, 0:1], 1.0 / D, None, ALU.mult))
        onv(lambda e: e.tensor_tensor(
            sstat_sb[:, 3:4], sstat_sb[:, 2:3], sstat_sb[:, 2:3], ALU.mult))
        ve_var = onv(lambda e: e.tensor_scalar(
            sstat_sb[:, 4:5], sstat_sb[:, 1:2], 1.0 / D, sstat_sb[:, 3:4],
            ALU.mult, ALU.subtract),
            waits=W((s_se, se_sq)))
        se_sqrt = ons(lambda e: e.activation(
            sstat_sb[:, 5:6], sstat_sb[:, 4:5], AF.Sqrt, bias=eps_sb[:], scale=1.0),
            waits=W((s_ve, ve_var)))
        onv(lambda e: e.reciprocal(sstat_sb[:, 6:7], sstat_sb[:, 5:6]),
            waits=W((s_se, se_sqrt)))
        ve_xn = onv(lambda e, slot=slot: e.tensor_scalar(
            xnt_sb[:, slot, :], xld_sb[:, slot, :], sstat_sb[:, 2:3],
            sstat_sb[:, 6:7], ALU.subtract, ALU.mult),
            waits=W((s_xns[slot], 16 * xns_use[slot])))
        xfree_ve[slot] = ve_xn
        p.on("sync", lambda e, t=t, slot=slot: e.dma_start(
            xn_dram[t * 128:(t + 1) * 128, :], xnt_sb[:, slot, :]),
            waits=W((s_ve, ve_xn), (s_const, p.v(s_const))), incs=[(s_xns[slot], 16)])
        xns_use[slot] += 1
        n_store += 1
        if t % 4 == 3:
            p.on("sync", lambda e: e.nop(),
                 waits=W((s_xns[0], 16 * xns_use[0]), (s_xns[1], 16 * xns_use[1])),
                 incs=[(s_xnready, 1)])

    n_cast = (n_ru + 1) // 2
    for c in range(n_cast):
        p.on("gpsimd", lambda e, c=c: e.dma_start(
            xrem_bf[c * 1024:(c + 1) * 1024, :], x_rem[c * 1024:(c + 1) * 1024, :]),
            waits=W((s_xrc[c % 3], 16 * (c // 3)),
                    *(((s_xnready, 1),) if c < 3 else ())),
            incs=[(s_xrc[c % 3], 16)])

    s_wpc = p.sem("s_wpc")
    cast_jobs = []
    for ei2 in range(n_exp):
        for wsel in range(2):
            def cast_job(ei2=ei2, wsel=wsel):
                ci = 2 * ei2 + wsel
                src = wgate_in[ei2] if wsel == 0 else wdown_in[ei2]
                dst = wgate_bf[ei2] if wsel == 0 else wdown_bf[ei2]
                p.on("gpsimd", lambda e, dst=dst, src=src: e.dma_start(dst, src),
                     waits=W((s_wpc, 16 * ci)), incs=[(s_wpc, 16)])
            cast_jobs.append(cast_job)

    xrf_total = {u: 16 * KT * (u // 2 + 1) for u in range(n_ru)}
    xrffree_te = [0, 0]
    scrfree_ve = [ve_cs]
    ohprfree_te = [0, 0, 0, 0]
    p2free_ve = [0]
    ve_oh_rec = {}
    rem_state = {"p2s": 0}

    def make_remote_transpose_job(u):
        slot = u % 2

        def job():
            c = u // 2
            for kp in range(KT):
                p.on("sync", lambda e, u=u, kp=kp, slot=slot, c=c: e.dma_start_transpose(
                    xrf_sb[:, slot, kp, :],
                    xrem_bf[u * 512:(u + 1) * 512, kp * 128:(kp + 1) * 128]),
                    waits=W((s_xrc[c % 3], 16 * (c // 3 + 1)), (s_te, xrffree_te[slot]),
                            (s_wtr, wtr_all), (s_se, p.v(s_se))),
                    incs=[(s_xrf[slot], 16)])

        return job

    def make_remote_p1_job(u, tl):
        slot = u % 2

        def job():
            for k in range(KT):
                ont(lambda e, k=k: e.matmul(
                    bank[7][:, 0:E + 1],
                    xrf_sb[:, slot, k, tl * 128:(tl + 1) * 128],
                    wgo_sb[:, k, :],
                    start=(k == 0), stop=(k == KT - 1)),
                    waits=(W((s_xrf[slot], xrf_total[u]), (s_ve, scrfree_ve[0]))
                           if k == 0 else ()),
                    incs=([(s_te, 1)] if k == KT - 1 else ()))
            if tl == 3:
                xrffree_te[slot] = p.v(s_te)
            te_lg = p.v(s_te)
            scrfree_ve[0] = onv(
                lambda e: e.tensor_copy(scr_sb[:], bank[7][:, 0:E + 1]),
                waits=W((s_te, te_lg)))
            onv(lambda e: e.tensor_scalar(
                tmpa_sb[:], csb_sb[:], scr_sb[:, E:E + 1], None, ALU.mult),
                waits=W((s_ve, ve_csb)))
            onv(lambda e: e.tensor_tensor(
                scr2_sb[:], scr_sb[:, 0:E], tmpa_sb[:], ALU.subtract))
            onv(lambda e: e.reduce_max(mx1_sb[:], scr2_sb[:], axis=AX.X))
            onv(lambda e: e.max_index(
                ix_sb[:], mx1_sb[:, 0:1].broadcast_to((128, E)), scr2_sb[:]))
            onv(lambda e: e.tensor_copy(i1f_sb[:], ix_sb[:, 0:1]))
            onv(lambda e: e.match_replace(
                l2_sb[:], mx1_sb[:, 0:1].broadcast_to((128, E)), scr2_sb[:], -1e30))
            onv(lambda e: e.reduce_max(mx2_sb[:], l2_sb[:], axis=AX.X))
            onv(lambda e: e.max_index(
                ix_sb[:], mx2_sb[:, 0:1].broadcast_to((128, E)), l2_sb[:]))
            onv(lambda e: e.tensor_copy(i2f_sb[:], ix_sb[:, 0:1]))
            onv(lambda e, tl=tl: e.tensor_scalar(
                ohpr_sb[:, tl, 0:E], iota8_sb[:], i1f_sb[:], None, ALU.is_equal),
                waits=W((s_te, ohprfree_te[tl])))
            ve_oh_rec[(u, tl)] = onv(lambda e, tl=tl: e.tensor_scalar(
                ohpr_sb[:, tl, E:2 * E], iota8_sb[:], i2f_sb[:], None, ALU.is_equal))

        return job

    def make_remote_p2_job(u, tl):
        chunk_col = u // 4

        def job():
            ve_need = max(ve_oh_rec[(u, tl)], p2free_ve[0])
            ont(lambda e, tl=tl: e.matmul(
                bank[7][0:1, 16:32], prevc_sb[:, chunk_col:chunk_col + 1],
                ohpr_sb[:, tl, :], start=True, stop=False),
                waits=W((s_ve, ve_need)))
            ont(lambda e, tl=tl: e.matmul(
                bank[7][0:1, 32:40], onesc_sb[:], ohpr_sb[:, tl, 0:E],
                start=False, stop=True),
                incs=[(s_te, 1)])
            te_cnt = p.v(s_te)
            ohprfree_te[tl] = te_cnt
            onv(lambda e: e.tensor_tensor(
                remprev_sb[:], remprev_sb[:], bank[7][0:1, 16:32], ALU.add),
                waits=W((s_te, te_cnt)))
            if dbg:
                def _dbgred(e, u=u, tl=tl):
                    with nc.allow_low_precision("dbg count trace"):
                        return e.reduce_sum(
                            dbgrem_sb[0:1, u * 4 + tl:u * 4 + tl + 1],
                            bank[7][0:1, 32:40], axis=AX.X)
                onv(_dbgred)
            p2free_ve[0] = onv(lambda e: e.tensor_tensor(
                remall_sb[:], remall_sb[:], bank[7][0:1, 32:40], ALU.add))
            rem_state["p2s"] += 1

        return job

    remote_jobs = []
    for u in range(n_ru):
        remote_jobs.append(make_remote_transpose_job(u))
        for tl in range(4):
            remote_jobs.append(make_remote_p1_job(u, tl))
        for tl in range(4):
            remote_jobs.append(make_remote_p2_job(u, tl))
        if u % 3 == 2 and cast_jobs:
            remote_jobs.append(cast_jobs.pop(0))
    remote_jobs.extend(cast_jobs)
    remote_iter = iter(remote_jobs)

    def emit_some_remote(nmax):
        for _ in range(nmax):
            job = next(remote_iter, None)
            if job is None:
                return
            job()

    scat_use = [0, 0]

    def emit_own_routing_tile(t, gslot, xnf_ready):
        tl = t % 4
        for k in range(KT):
            ont(lambda e, k=k: e.matmul(
                bank[3][:, 0:E],
                xnf_sb[:, gslot, k, tl * 128:(tl + 1) * 128], wg_sb[:, k, :],
                start=(k == 0), stop=(k == KT - 1)),
                waits=(W((s_xnf[gslot], xnf_ready), (s_ve, p.v(s_ve)),
                         (s_se, p.v(s_se))) if k == 0 else ()),
                incs=([(s_te, 1)] if k == KT - 1 else ()))
        te_lg = p.v(s_te)
        onv(lambda e: e.tensor_copy(scr_sb[:, 0:E], bank[3][:, 0:E]),
            waits=W((s_te, te_lg)))
        onv(lambda e: e.reduce_max(mx1_sb[:], scr_sb[:, 0:E], axis=AX.X))
        onv(lambda e: e.max_index(
            ix_sb[:], mx1_sb[:, 0:1].broadcast_to((128, E)), scr_sb[:, 0:E]))
        onv(lambda e: e.tensor_copy(i1f_sb[:], ix_sb[:, 0:1]))
        onv(lambda e: e.match_replace(
            l2_sb[:], mx1_sb[:, 0:1].broadcast_to((128, E)), scr_sb[:, 0:E], -1e30))
        onv(lambda e: e.reduce_max(mx2_sb[:], l2_sb[:], axis=AX.X))
        onv(lambda e: e.max_index(
            ix_sb[:], mx2_sb[:, 0:1].broadcast_to((128, E)), l2_sb[:]))
        ve_idx = onv(lambda e: e.tensor_copy(i2f_sb[:], ix_sb[:, 0:1]))
        se_g = ons(lambda e, t=t: e.activation(
            g1n_sb[:, t:t + 1], mx2_sb[:], AF.Sigmoid, bias=mx1_sb[:], scale=-1.0),
            waits=W((s_ve, ve_idx)))
        onv(lambda e, t=t: e.tensor_scalar(
            g2n_sb[:, t:t + 1], g1n_sb[:, t:t + 1], -1.0, 1.0, ALU.mult, ALU.add),
            waits=W((s_se, se_g)))
        onv(lambda e, t=t: e.tensor_scalar(
            oh1_sb[:, t, :], iota8_sb[:], i1f_sb[:], None, ALU.is_equal))
        onv(lambda e, t=t: e.tensor_scalar(
            oh2_sb[:, t, :], iota8_sb[:], i2f_sb[:], None, ALU.is_equal))
        onv(lambda e, t=t: e.tensor_copy(ohpr2_sb[:, 0:E], oh1_sb[:, t, :]),
            waits=W((s_te, p.v(s_te))))
        onv(lambda e, t=t: e.tensor_copy(ohpr2_sb[:, E:2 * E], oh2_sb[:, t, :]))
        ve_oh3 = onv(lambda e, t=t: e.tensor_tensor(
            ohpr2_sb[:, 2 * E:3 * E], oh1_sb[:, t, :], oh2_sb[:, t, :], ALU.add))
        ont(lambda e: e.matmul(
            bank[3][:, 32:56], tri_sb[:], ohpr2_sb[:], start=True, stop=False),
            waits=W((s_ve, ve_oh3)))
        ont(lambda e: e.matmul(
            bank[3][:, 32:56], ones1f_sb[:], run_sb[:], start=False, stop=True),
            incs=[(s_te, 1)])
        te_loc = p.v(s_te)
        ont(lambda e: e.matmul(
            bank[2][0:1, 32:56], onesc_sb[:], ohpr2_sb[:], start=True, stop=True),
            incs=[(s_te, 1)])
        te_cs2 = p.v(s_te)
        onv(lambda e, t=t: e.tensor_tensor(
            tmpa_sb[:], bank[3][:, 32:40], oh1_sb[:, t, :], ALU.mult),
            waits=W((s_te, te_loc)))
        onv(lambda e, t=t: e.reduce_sum(loc1_sb[:, t:t + 1], tmpa_sb[:], axis=AX.X))
        onv(lambda e, t=t: e.tensor_tensor(
            tmpa_sb[:], bank[3][:, 40:48], oh2_sb[:, t, :], ALU.mult))
        onv(lambda e, t=t: e.reduce_sum(loc2_sb[:, t:t + 1], tmpa_sb[:], axis=AX.X))
        onv(lambda e, t=t: e.tensor_tensor(
            tmpa_sb[:], bank[3][:, 48:56], oh1_sb[:, t, :], ALU.mult))
        onv(lambda e: e.reduce_sum(pos1_sb[:], tmpa_sb[:], axis=AX.X))
        onv(lambda e, t=t: e.tensor_tensor(
            tmpa_sb[:], bank[3][:, 48:56], oh2_sb[:, t, :], ALU.mult))
        onv(lambda e: e.reduce_sum(pos2_sb[:], tmpa_sb[:], axis=AX.X))
        ve_run = onv(lambda e: e.tensor_tensor(
            run_sb[:], run_sb[:], bank[2][0:1, 32:56], ALU.add),
            waits=W((s_te, te_cs2)))
        for ch_i, (idxf, posx, cidx) in enumerate(
                ((i1f_sb, pos1_sb, cidx1_sb), (i2f_sb, pos2_sb, cidx2_sb))):
            onv(lambda e, posx=posx: e.tensor_scalar(
                tmpd_sb[:], posx[:], float(CHAT), None, ALU.is_ge))
            onv(lambda e: e.tensor_scalar(
                tmpn_sb[:], tmpd_sb[:], -1.0, 1.0, ALU.mult, ALU.add))
            onv(lambda e, idxf=idxf: e.tensor_scalar(
                tmpc_sb[:], idxf[:], float(CHAT), None, ALU.mult))
            onv(lambda e, posx=posx: e.tensor_tensor(
                tmpc_sb[:], tmpc_sb[:], posx[:], ALU.add))
            onv(lambda e: e.tensor_tensor(tmpc_sb[:], tmpc_sb[:], tmpn_sb[:], ALU.mult))
            onv(lambda e: e.scalar_tensor_tensor(
                tmpc_sb[:], tmpd_sb[:], float(E * CHAT), tmpc_sb[:], ALU.mult, ALU.add))
            ve_sc = onv(lambda e, ch_i=ch_i: e.tensor_copy(
                scidx_sb[ch_i][:], tmpc_sb[:]),
                waits=W((s_scat[ch_i], 16 * scat_use[ch_i])))
            scat_use[ch_i] += 1
            p.on("gpsimd", lambda e, t=t, ch_i=ch_i: e.indirect_dma_start(
                glist_dram[:],
                IndirectOffsetOnAxis(scidx_sb[ch_i][:], 0),
                tokid_sb[:, t:t + 1], None),
                waits=W((s_ve, ve_sc), (s_glpre, 16)), incs=[(s_scat[ch_i], 16)])
            onv(lambda e, posx=posx: e.tensor_scalar(
                tmpc_sb[:], posx[:], 1.0 / NQ, 0.5 / NQ, ALU.mult, ALU.add))
            onv(lambda e: e.tensor_copy(tmpi_sb[:], tmpc_sb[:]))
            onv(lambda e: e.tensor_copy(tmpc_sb[:], tmpi_sb[:]))
            onv(lambda e, posx=posx: e.scalar_tensor_tensor(
                tmpq_sb[:], tmpc_sb[:], -float(NQ), posx[:], ALU.mult, ALU.add))
            onv(lambda e: e.tensor_scalar(tmpw_sb[:], tmpq_sb[:], 0.0, None, ALU.is_lt))
            onv(lambda e: e.tensor_tensor(tmpc_sb[:], tmpc_sb[:], tmpw_sb[:], ALU.subtract))
            onv(lambda e: e.scalar_tensor_tensor(
                tmpq_sb[:], tmpw_sb[:], float(NQ), tmpq_sb[:], ALU.mult, ALU.add))
            onv(lambda e: e.tensor_scalar(tmpw_sb[:], tmpq_sb[:], float(NQ), None, ALU.is_ge))
            onv(lambda e: e.tensor_tensor(tmpc_sb[:], tmpc_sb[:], tmpw_sb[:], ALU.add))
            onv(lambda e: e.scalar_tensor_tensor(
                tmpq_sb[:], tmpw_sb[:], -float(NQ), tmpq_sb[:], ALU.mult, ALU.add))
            onv(lambda e: e.tensor_scalar(
                tmpq_sb[:], tmpq_sb[:], 128.0, None, ALU.mult))
            onv(lambda e: e.tensor_tensor(tmpq_sb[:], tmpq_sb[:], tmpc_sb[:], ALU.add))
            onv(lambda e, idxf=idxf: e.tensor_scalar(
                tmpc_sb[:], idxf[:], float(CHAT), None, ALU.mult))
            onv(lambda e: e.tensor_tensor(tmpq_sb[:], tmpq_sb[:], tmpc_sb[:], ALU.add))
            onv(lambda e: e.tensor_tensor(tmpq_sb[:], tmpq_sb[:], tmpn_sb[:], ALU.mult))
            onv(lambda e: e.scalar_tensor_tensor(
                tmpq_sb[:], tmpd_sb[:], float(ZROW_EO), tmpq_sb[:], ALU.mult, ALU.add))
            onv(lambda e, t=t, cidx=cidx: e.tensor_copy(cidx[:, t:t + 1], tmpq_sb[:]))
        return ve_run

    xnffree_te = [0, 0]
    last_exp_pair = [0, 0]
    obank_free = [(0, 0), (0, 0)]
    rcpbfree_ve = [0]
    terb_prev = [0]
    opjfree_ve = [0, 0]
    xnf_ready_map = {}

    def emit_xnf(g):
        gslot = g % 2
        for kp in range(KT):
            p.on("sync", lambda e, g=g, kp=kp, gslot=gslot: e.dma_start_transpose(
                xnf_sb[:, gslot, kp, :],
                xn_dram[g * 512:(g + 1) * 512, kp * 128:(kp + 1) * 128]),
                waits=W((s_xnready, g + 1), (s_te, xnffree_te[gslot]),
                        (s_wtr, wtr_all), (s_se, p.v(s_se))),
                incs=[(s_xnf[gslot], 16)])
        xnf_ready_map[g] = p.v(s_xnf[gslot])

    emit_xnf(0)
    for g in range(ng):
        gslot = g % 2
        xnf_ready = xnf_ready_map[g]

        ve_qevac = 0
        for a in range(NCP // 2):
            if a == 0:
                w_q0 = W((s_xnf[gslot], xnf_ready),
                         (s_ve, max(opjfree_ve[0], opjfree_ve[1])),
                         (s_se, max(last_exp_pair)))
            else:
                w_q0 = W((s_xnf[gslot], xnf_ready), (s_ve, ve_qevac))
            for k in range(KT):
                ont(lambda e, a=a, k=k, gslot=gslot: e.matmul(
                    bank[0][0:64, :],
                    wqT_sb[:, k, (2 * a) * 64:(2 * a) * 64 + 64],
                    xnf_sb[:, gslot, k, :],
                    start=(k == 0), stop=(k == KT - 1), tile_position=(0, 0)),
                    waits=(w_q0 if k == 0 else ()))
                ont(lambda e, a=a, k=k, gslot=gslot: e.matmul(
                    bank[1][64:128, :],
                    wqT_sb[:, k, (2 * a) * 64:(2 * a) * 64 + 64],
                    xnf_sb[:, gslot, k, :],
                    start=(k == 0), stop=(k == KT - 1), tile_position=(0, 64)))
                ont(lambda e, a=a, k=k, gslot=gslot: e.matmul(
                    bank[2][0:64, :],
                    wqT_sb[:, k, (2 * a + 1) * 64:(2 * a + 1) * 64 + 64],
                    xnf_sb[:, gslot, k, :],
                    start=(k == 0), stop=(k == KT - 1), tile_position=(0, 0)))
                ont(lambda e, a=a, k=k, gslot=gslot: e.matmul(
                    bank[3][64:128, :],
                    wqT_sb[:, k, (2 * a + 1) * 64:(2 * a + 1) * 64 + 64],
                    xnf_sb[:, gslot, k, :],
                    start=(k == 0), stop=(k == KT - 1), tile_position=(0, 64)),
                    incs=([(s_te, 1)] if k == KT - 1 else ()))
            te_q = p.v(s_te)
            for cc, bnklo, bnkhi in ((2 * a, bank[0], bank[1]),
                                     (2 * a + 1, bank[2], bank[3])):
                onv(lambda e, cc=cc, bnklo=bnklo, gslot=gslot: e.tensor_copy(
                    qhc_sb[:, gslot, :].rearrange("p (h c j) -> p h c j",
                                                  h=4, c=NCP)[0:64, :, cc, :],
                    bnklo[0:64, :].rearrange("p (h j) -> p h j", h=4)),
                    waits=W((s_te, te_q)))
                ve_qevac = onv(lambda e, cc=cc, bnkhi=bnkhi, gslot=gslot: e.tensor_copy(
                    qhc_sb[:, gslot, :].rearrange("p (h c j) -> p h c j",
                                                  h=4, c=NCP)[64:128, :, cc, :],
                    bnkhi[64:128, :].rearrange("p (h j) -> p h j", h=4)))

        ve_kevac = 0
        for a in range(NCP // 2):
            w_k0 = W((s_ve, ve_qevac if a == 0 else ve_kevac))
            for k in range(KT):
                ont(lambda e, a=a, k=k, gslot=gslot: e.matmul(
                    bank[0][0:64, :],
                    wkT_sb[:, k, (2 * a) * 64:(2 * a) * 64 + 64],
                    xnf_sb[:, gslot, k, :],
                    start=(k == 0), stop=(k == KT - 1), tile_position=(0, 0)),
                    waits=(w_k0 if k == 0 else ()))
                ont(lambda e, a=a, k=k, gslot=gslot: e.matmul(
                    bank[1][64:128, :],
                    wkT_sb[:, k, (2 * a + 1) * 64:(2 * a + 1) * 64 + 64],
                    xnf_sb[:, gslot, k, :],
                    start=(k == 0), stop=(k == KT - 1), tile_position=(0, 64)),
                    incs=([(s_te, 1)] if k == KT - 1 else ()))
            te_k = p.v(s_te)
            onv(lambda e, a=a, gslot=gslot: e.tensor_copy(
                kpr_sb[0:64, gslot, a, :], bank[0][0:64, :]),
                waits=W((s_te, te_k)))
            ve_kevac = onv(lambda e, a=a, gslot=gslot: e.tensor_copy(
                kpr_sb[64:128, gslot, a, :], bank[1][64:128, :]))

        onv(lambda e, gslot=gslot: e.memset(vpv_sb[:, gslot, :], 1.0),
            waits=W((s_te, p.v(s_te))))
        ve_vevac = {0: ve_kevac, 1: ve_kevac, 2: ve_qevac, 3: ve_qevac}
        ve_vlast = 0
        for tt in range(4):
            for oc in range(2):
                bi = (1 - tt % 2) * 2 + oc
                bnk = bank[bi]
                for k in range(KT):
                    ont(lambda e, tt=tt, oc=oc, k=k, gslot=gslot, bnk=bnk: e.matmul(
                        bnk[:, :],
                        xnf_sb[:, gslot, k, tt * 128:(tt + 1) * 128],
                        wvT_sb[:, k, oc * 512:(oc + 1) * 512],
                        start=(k == 0), stop=(k == KT - 1)),
                        waits=(W((s_ve, ve_vevac[bi])) if k == 0 else ()),
                        incs=([(s_te, 1)] if k == KT - 1 else ()))
                te_v = p.v(s_te)
                ve_vlast = onv(lambda e, tt=tt, oc=oc, gslot=gslot, bnk=bnk: e.tensor_copy(
                    vpv_sb[:, gslot, :].rearrange("p (h c w) -> p h c w",
                                                  h=4, c=NCP)[:, tt, oc * 8:(oc + 1) * 8, 0:64],
                    bnk[:, :].rearrange("p (c w) -> p c w", c=8)),
                    waits=W((s_te, te_v)))
                ve_vevac[bi] = ve_vlast

        ve_bank3 = 0
        for tl in range(4):
            ve_bank3 = emit_own_routing_tile(g * 4 + tl, gslot, xnf_ready)
        xnffree_te[gslot] = p.v(s_te)
        if g + 1 < ng:
            emit_xnf(g + 1)

        def tail_act(pend):
            pobk, pte_o, ph, pch = pend
            pobank = bank[4 + pobk]
            se_ln = ons(lambda e, pobank=pobank: e.activation(
                lnr_sb[:], pobank[64:65, :], AF.Ln, bias=0.0, scale=1.0),
                waits=W((s_te, pte_o)))
            se_rc = ons(lambda e: e.activation(
                rcp_sb[:], lnr_sb[:], AF.Exp, bias=0.0, scale=-1.0),
                waits=W((s_te, terb_prev[0])))
            return se_ln, se_rc

        def tail_rest(pend, se_ln, se_rc, gslot=gslot):
            pobk, pte_o, ph, pch = pend
            pobank = bank[4 + pobk]
            ont(lambda e: e.matmul(
                bank[6][0:64, :], onesr_sb[0:1, 0:64], rcp_sb[:],
                start=True, stop=True),
                waits=W((s_se, se_rc), (s_ve, rcpbfree_ve[0])),
                incs=[(s_te, 1)])
            te_rb = p.v(s_te)
            terb_prev[0] = te_rb
            rcpbfree_ve[0] = onv(lambda e: e.tensor_copy(
                rcpb_sb[:], bank[6][0:64, :]), waits=W((s_te, te_rb)))
            ve_oev = onv(lambda e, pobank=pobank: e.tensor_copy(
                oev_sb[:], pobank[0:64, :]))
            ve_l = ve_oev
            for ci in range(4):
                cplane = pch * 4 + ci
                ptile, phalf = cplane // 2, cplane % 2
                ve_l = onv(lambda e, ci=ci, ptile=ptile, phalf=phalf, ph=ph,
                           gslot=gslot: e.tensor_tensor(
                    xo_sb[64 * phalf:64 * phalf + 64, gslot, ptile,
                          ph * 128:(ph + 1) * 128],
                    oev_sb[:, ci * 128:(ci + 1) * 128],
                    rcpb_sb[:, ci * 128:(ci + 1) * 128], ALU.mult))
            obank_free[pobk] = (se_ln, ve_oev)
            return ve_l

        pending = None
        ve_xo_last = 0
        for h in range(4):
            for ch in range(4):
                obk = ch % 2
                obank = bank[4 + obk]
                se_e1 = {}

                def sc_exp(a, extra_ve=0, h=h, ch=ch, gslot=gslot):
                    pair = a % 2
                    sb0, sb1 = bank[2 * pair], bank[2 * pair + 1]
                    w0 = [(s_se, last_exp_pair[pair])]
                    if extra_ve:
                        w0.append((s_ve, extra_ve))
                    ont(lambda e, a=a, h=h, ch=ch, gslot=gslot, sb0=sb0: e.matmul(
                        sb0[:, :],
                        kpr_sb[0:64, gslot, a, h * 128:(h + 1) * 128],
                        qhc_sb[0:64, gslot, h * 2048 + ch * 512:h * 2048 + (ch + 1) * 512],
                        start=True, stop=True, tile_position=(0, 0)),
                        waits=w0, incs=[(s_te, 1)])
                    ont(lambda e, a=a, h=h, ch=ch, gslot=gslot, sb1=sb1: e.matmul(
                        sb1[:, :],
                        kpr_sb[64:128, gslot, a, h * 128:(h + 1) * 128],
                        qhc_sb[64:128, gslot, h * 2048 + ch * 512:h * 2048 + (ch + 1) * 512],
                        start=True, stop=True, tile_position=(64, 0)),
                        incs=[(s_te, 1)])
                    ts = p.v(s_te)
                    se = ons(lambda e, pair=pair, a=a: e.activation(
                        ebuf_sb[:, a % 2, :], sb_pair_ap[pair],
                        AF.Exp, bias=0.0, scale=1.0),
                        waits=W((s_te, ts)))
                    last_exp_pair[pair] = se
                    se_e1[a] = se

                def pv(a, h=h, gslot=gslot, obank=obank, obk=obk):
                    w0 = [(s_se, se_e1[a])]
                    if a == 0:
                        w0.append((s_ve, obank_free[obk][1]))
                    ont(lambda e, a=a, h=h, gslot=gslot, obank=obank: e.matmul(
                        obank[0:65, :],
                        vpv_sb[:, gslot, (h * NCP + 2 * a) * 65:(h * NCP + 2 * a) * 65 + 65],
                        ebuf_sb[:, a % 2, 0:512],
                        start=(a == 0), stop=False),
                        waits=w0)
                    ont(lambda e, a=a, h=h, gslot=gslot, obank=obank: e.matmul(
                        obank[0:65, :],
                        vpv_sb[:, gslot, (h * NCP + 2 * a + 1) * 65:(h * NCP + 2 * a + 1) * 65 + 65],
                        ebuf_sb[:, a % 2, 512:1024],
                        start=False, stop=(a == NCP // 2 - 1)),
                        incs=([(s_te, 1)] if a == NCP // 2 - 1 else ()))

                if pending is not None:
                    se_ln, se_rc = tail_act(pending)
                first = (h == 0 and ch == 0)
                sc_exp(0, extra_ve=(ve_vlast if first else 0))
                sc_exp(1, extra_ve=(ve_bank3 if first else 0))
                if pending is not None:
                    ve_xo_last = tail_rest(pending, se_ln, se_rc)
                    pending = None
                for a in range(2, NCP // 2):
                    pv(a - 2)
                    sc_exp(a)
                pv(NCP // 2 - 2)
                pv(NCP // 2 - 1)
                pending = (obk, p.v(s_te), h, ch)
                emit_some_remote(4)
        se_ln, se_rc = tail_act(pending)
        ve_xo_last = tail_rest(pending, se_ln, se_rc)
        pending = None

        for mt in range(KT):
            pbank = bank[mt % 2]
            if mt < 2:
                w0 = [(s_ve, max(ve_xo_last, opjfree_ve[mt % 2])),
                      (s_se, last_exp_pair[0])]
            else:
                w0 = [(s_ve, opjfree_ve[mt % 2])]
            for k in range(KT):
                ont(lambda e, mt=mt, k=k, gslot=gslot, pbank=pbank: e.matmul(
                    pbank[:, :],
                    woT_sb[:, k, mt * 128:(mt + 1) * 128],
                    xo_sb[:, gslot, k, :],
                    start=(k == 0), stop=(k == KT - 1)),
                    waits=(w0 if k == 0 else ()),
                    incs=([(s_te, 1)] if k == KT - 1 else ()))
            te_op = p.v(s_te)
            ve_opj = onv(lambda e, mt=mt, pbank=pbank: e.tensor_copy(
                opj_sb[:, mt % 2, :], pbank[:, :]),
                waits=W((s_te, te_op), (s_ats[mt % 2], p.v(s_ats[mt % 2]))))
            opjfree_ve[mt % 2] = ve_opj
            p.on("sync", lambda e, g=g, mt=mt: e.dma_start(
                attn_dram[mt * 128:(mt + 1) * 128, g * 512:(g + 1) * 512],
                opj_sb[:, mt % 2, :]),
                waits=W((s_ve, ve_opj)), incs=[(s_ats[mt % 2], 16)])
        p.on("sync", lambda e: e.nop(),
             waits=W((s_ats[0], p.v(s_ats[0])), (s_ats[1], p.v(s_ats[1]))),
             incs=[(s_atready, 1)])

    emit_some_remote(10 ** 9)
    atready_all = p.v(s_atready)

    onv(lambda e: e.tensor_scalar(
        thrr_sb[:, 0:8], remprev_sb[:, 0:8], -1.0, float(CAP), ALU.mult, ALU.add))
    onv(lambda e: e.tensor_tensor(tmpe_sb[:], remall_sb[:], run_sb[:, 0:8], ALU.add))
    onv(lambda e: e.tensor_tensor(tmpe_sb[:], tmpe_sb[:], remprev_sb[:, 8:16], ALU.add))
    ve_thr = onv(lambda e: e.tensor_scalar(
        thrr_sb[:, 8:16], tmpe_sb[:], -1.0, float(CAP), ALU.mult, ALU.add))
    p.on("sync", lambda e: e.dma_start(thr_dram[:], thrr_sb[:]),
         waits=W((s_ve, ve_thr)), incs=[(s_msc, 16)])
    p.on("sync", lambda e: e.dma_start(
        thr1b_sb[:], thr_dram[0:1, 0:8].partition_broadcast(128)),
        waits=W((s_msc, p.v(s_msc))), incs=[(s_msc, 16)])
    p.on("sync", lambda e: e.dma_start(
        thr2b_sb[:], thr_dram[0:1, 8:16].partition_broadcast(128)),
        waits=W((s_msc, p.v(s_msc))), incs=[(s_msc, 16)])
    msc_thr = p.v(s_msc)

    barrier()
    es1.close()

    es2 = ExitStack()
    SB2 = lambda name, shape, dtype: es2.enter_context(nc.sbuf_tensor(name, shape, dtype))
    glist_sb = SB2("glist_sb", [128, 2, NQ], I32)
    dtm_sb = SB2("dtm_sb", [128, 2, D], BF16)
    dfm_sb = SB2("dfm_sb", [128, 2, KT, CHAT], BF16)
    wgb_sb = SB2("wgb_sb", [128, 2, KT, 256], BF16)
    wdh_sb = SB2("wdh_sb", [128, 2, FT, 512], BF16)
    hfm_sb = SB2("hfm_sb", [128, FT, CHAT], BF16)
    eo_sb = SB2("eo_sb", [128, 2, 512], BF16)


    CH1 = (512, 512, 256)
    dtm_use = [0, 0]
    prev_dg = [0, 0]
    eo_use = [0, 0]
    wgbfree_se = [0, 0]
    wdhfree_ve = [0, 0]
    dfmfree_se = [0, 0]
    gelu_cnt = {0: 0, 1: 0, 2: 0}
    ffn_eo_ve = [0, 0]
    dfm_ready_map = {}

    def emit_dispatch(ei):
        eslot = ei % 2
        p.on("sync", lambda e, ei=ei, eslot=eslot: e.dma_start(
            glist_sb[:, eslot, :],
            glist_dram[ei * CHAT:(ei + 1) * CHAT, 0].rearrange("(p q) -> p q", q=NQ)),
            waits=W((s_scat[0], 16 * scat_use[0]), (s_scat[1], 16 * scat_use[1]),
                    (s_glist, 16 * ei),
                    (s_dg[0], prev_dg[0]), (s_dg[1], prev_dg[1])),
            incs=[(s_glist, 16)])
        gl_n = p.v(s_glist)
        for q in range(NQ):
            dslot = q % 2
            p.on("gpsimd", lambda e, q=q, dslot=dslot, eslot=eslot: e.indirect_dma_start(
                dtm_sb[:, dslot, :], None, xn_dram[:],
                IndirectOffsetOnAxis(glist_sb[:, eslot, q:q + 1], 0),
                ),
                waits=W((s_glist, gl_n), (s_dts[dslot], 16 * dtm_use[dslot]),
                        (s_xns[0], 16 * xns_use[0]), (s_xns[1], 16 * xns_use[1]),
                        (s_const, p.v(s_const))),
                incs=[(s_dg[dslot], 16)])
            p.on("sync", lambda e, ei=ei, q=q, dslot=dslot: e.dma_start(
                disp_dram[ei * CHAT + q * 128:ei * CHAT + (q + 1) * 128, :],
                dtm_sb[:, dslot, :]),
                waits=W((s_dg[dslot], p.v(s_dg[dslot]))),
                incs=[(s_dts[dslot], 16)])
            dtm_use[dslot] += 1
        prev_dg[0], prev_dg[1] = p.v(s_dg[0]), p.v(s_dg[1])
        p.on("sync", lambda e: e.nop(),
             waits=W((s_dts[0], p.v(s_dts[0])), (s_dts[1], p.v(s_dts[1]))),
             incs=[(s_dready, 1)])
        for kp in range(KT):
            p.on("sync", lambda e, ei=ei, kp=kp, eslot=eslot: e.dma_start_transpose(
                dfm_sb[:, eslot, kp, :],
                disp_dram[ei * CHAT:(ei + 1) * CHAT, kp * 128:(kp + 1) * 128]),
                waits=W((s_dready, ei + 1), (s_se, dfmfree_se[eslot])),
                incs=[(s_dfm[eslot], 16)])
        dfm_ready_map[ei] = p.v(s_dfm[eslot])

    emit_dispatch(0)
    for ei in range(n_exp):
        eslot = ei % 2
        if ei + 1 < n_exp:
            emit_dispatch(ei + 1)
        dfm_ready = dfm_ready_map[ei]

        for fg in range(FT // 2):
            wslot = fg % 2
            p.on("scalar", lambda e, ei=ei, fg=fg, wslot=wslot: e.dma_start(
                wgb_sb[:, wslot, :, :],
                wgate_bf[ei, :, fg * 256:(fg + 1) * 256].rearrange(
                    "(a p) f -> p a f", p=128)),
                waits=W((s_se, wgbfree_se[wslot]), (s_wpc, 32 * ei + 16)),
                incs=[(s_wgb[wslot], 16)])
            wgb_n = p.v(s_wgb[wslot])
            for fi in range(2):
                ft = fg * 2 + fi
                co = 0
                for ci, cw in enumerate(CH1):
                    bnk = bank[ci]
                    for k in range(KT):
                        ont(lambda e, k=k, cw=cw, co=co, fi=fi, wslot=wslot,
                            eslot=eslot, bnk=bnk: e.matmul(
                                bnk[:, 0:cw],
                                wgb_sb[:, wslot, k, fi * 128:(fi + 1) * 128],
                                dfm_sb[:, eslot, k, co:co + cw],
                                start=(k == 0), stop=(k == KT - 1)),
                            waits=(W((s_wgb[wslot], wgb_n), (s_dfm[eslot], dfm_ready),
                                     (s_se, gelu_cnt[ci])) if k == 0 else ()),
                            incs=([(s_te, 1)] if k == KT - 1 else ()))
                    te_m1 = p.v(s_te)
                    gelu_cnt[ci] = ons(lambda e, ft=ft, cw=cw, co=co, bnk=bnk:
                                       e.activation(
                                           hfm_sb[:, ft, co:co + cw], bnk[:, 0:cw],
                                           AF.Gelu, bias=0.0, scale=1.0),
                                       waits=W((s_te, te_m1)))
                    if fi == 1 and ci == len(CH1) - 1:
                        wgbfree_se[wslot] = gelu_cnt[ci]
                    co += cw
        se_h = p.v(s_se)

        wdh_n_map = {}
        for dh_ in range(2):
            for part in range(2):
                p.on("scalar", lambda e, ei=ei, dh_=dh_, part=part:
                     e.dma_start(
                         wdh_sb[:, dh_, part * 16:(part + 1) * 16, :],
                         wdown_bf[ei, part * 2048:(part + 1) * 2048,
                                  dh_ * 512:(dh_ + 1) * 512].rearrange(
                             "(a p) d -> p a d", p=128)),
                     waits=W((s_ve, wdhfree_ve[dh_]), (s_wpc, 32 * (ei + 1))),
                     incs=[(s_wdh[dh_], 16)])
            wdh_n_map[dh_] = p.v(s_wdh[dh_])
        for dh_ in range(2):
            wdslot = dh_
            wdh_n = wdh_n_map[dh_]
            for rt in range(NQ):
                bnk = bank[3 + rt % 2]
                for k in range(FT):
                    ont(lambda e, rt=rt, k=k, wdslot=wdslot, bnk=bnk: e.matmul(
                        bnk[:, :],
                        hfm_sb[:, k, rt * 128:(rt + 1) * 128],
                        wdh_sb[:, wdslot, k, :],
                        start=(k == 0), stop=(k == FT - 1)),
                        waits=(W((s_wdh[wdslot], wdh_n), (s_se, se_h),
                                 (s_ve, ffn_eo_ve[rt % 2])) if k == 0 else ()),
                        incs=([(s_te, 1)] if k == FT - 1 else ()))
                te_m2 = p.v(s_te)
                eslot2 = rt % 2
                ve_eo = onv(lambda e, bnk=bnk, eslot2=eslot2: e.tensor_copy(
                    eo_sb[:, eslot2, 0:512], bnk[:, :]),
                    waits=W((s_te, te_m2), (s_eos[eslot2], 16 * eo_use[eslot2])))
                ffn_eo_ve[rt % 2] = ve_eo
                if rt == NQ - 1:
                    wdhfree_ve[wdslot] = ve_eo
                p.on("scalar", lambda e, ei=ei, rt=rt, dh_=dh_, eslot2=eslot2: e.dma_start(
                    eo_dram[ei * CHAT + rt * 128:ei * CHAT + (rt + 1) * 128,
                            dh_ * 512:(dh_ + 1) * 512],
                    eo_sb[:, eslot2, 0:512]),
                    waits=W((s_ve, ve_eo)), incs=[(s_eos[eslot2], 16)])
                eo_use[eslot2] += 1
        dfmfree_se[eslot] = se_h
    p.on("sync", lambda e: e.nop(),
         waits=W((s_eos[0], p.v(s_eos[0])), (s_eos[1], p.v(s_eos[1]))),
         incs=[(s_eoready, 1)])
    eoready = p.v(s_eoready)

    barrier()
    es2.close()
    es3 = ExitStack()
    SB3 = lambda name, shape, dtype: es3.enter_context(nc.sbuf_tensor(name, shape, dtype))
    cgA_sb = SB3("cgA_sb", [128, 2, D], BF16)
    cgB_sb = SB3("cgB_sb", [128, 2, D], BF16)
    att_sb = SB3("att_sb", [128, 2, D], BF16)
    xr2_sb = SB3("xr2_sb", [128, 2, D], F32)
    outb_sb = SB3("outb_sb", [128, 2, D], F32)
    ga_sb = SB3("ga_sb", [128, 1], F32)
    gb_sb = SB3("gb_sb", [128, 1], F32)
    cmbfree_ve = [0, 0]
    attfree_ve = [0, 0]
    out_use = [0, 0]
    for t in range(nt):
        slot = t % 2
        p.on("gpsimd", lambda e, t=t, slot=slot: e.indirect_dma_start(
            cgA_sb[:, slot, :], None, eo_dram[:],
            IndirectOffsetOnAxis(cidx1_sb[:, t:t + 1], 0)),
            waits=W((s_eoready, eoready), (s_ve, cmbfree_ve[slot]),
                    (s_const, p.v(s_const))),
            incs=[(s_cmb[slot], 16)])
        p.on("gpsimd", lambda e, t=t, slot=slot: e.indirect_dma_start(
            cgB_sb[:, slot, :], None, eo_dram[:],
            IndirectOffsetOnAxis(cidx2_sb[:, t:t + 1], 0)),
            incs=[(s_cmb[slot], 16)])
        p.on("sync", lambda e, t=t, slot=slot: e.dma_start_transpose(
            att_sb[:, slot, :], attn_dram[:, t * 128:(t + 1) * 128]),
            waits=W((s_atready, atready_all), (s_ve, attfree_ve[slot]),
                    (s_se, p.v(s_se))),
            incs=[(s_att[slot], 16)])
        p.on("sync", lambda e, t=t, slot=slot: e.dma_start(
            xr2_sb[:, slot, :], x_own[t * 128:(t + 1) * 128, :]),
            waits=W((s_ve, attfree_ve[slot])),
            incs=[(s_att[slot], 16)])
        cmb_n = p.v(s_cmb[slot])
        att_n = p.v(s_att[slot])
        onv(lambda e, t=t: e.tensor_tensor(
            tmpa_sb[:], thr1b_sb[:], oh1_sb[:, t, :], ALU.mult),
            waits=W((s_msc, msc_thr)))
        onv(lambda e: e.reduce_sum(tmpc_sb[:], tmpa_sb[:], axis=AX.X))
        onv(lambda e, t=t: e.tensor_tensor(
            tmpd_sb[:], loc1_sb[:, t:t + 1], tmpc_sb[:], ALU.is_lt))
        onv(lambda e, t=t: e.tensor_tensor(
            ga_sb[:], g1n_sb[:, t:t + 1], tmpd_sb[:], ALU.mult))
        onv(lambda e, t=t: e.tensor_tensor(
            tmpa_sb[:], thr2b_sb[:], oh2_sb[:, t, :], ALU.mult))
        onv(lambda e: e.reduce_sum(tmpc_sb[:], tmpa_sb[:], axis=AX.X))
        onv(lambda e, t=t: e.tensor_tensor(
            tmpd_sb[:], loc2_sb[:, t:t + 1], tmpc_sb[:], ALU.is_lt))
        onv(lambda e, t=t: e.tensor_tensor(
            gb_sb[:], g2n_sb[:, t:t + 1], tmpd_sb[:], ALU.mult))
        onv(lambda e, slot=slot: e.tensor_scalar(
            outb_sb[:, slot, :], cgA_sb[:, slot, :], ga_sb[:], None, ALU.mult),
            waits=W((s_cmb[slot], cmb_n), (s_out[slot], 16 * out_use[slot])))
        onv(lambda e, slot=slot: e.scalar_tensor_tensor(
            outb_sb[:, slot, :], cgB_sb[:, slot, :], gb_sb[:], outb_sb[:, slot, :],
            ALU.mult, ALU.add))
        cmbfree_ve[slot] = onv(lambda e, slot=slot: e.tensor_tensor(
            outb_sb[:, slot, :], outb_sb[:, slot, :], att_sb[:, slot, :], ALU.add),
            waits=W((s_att[slot], att_n)))
        ve_out = onv(lambda e, slot=slot: e.tensor_tensor(
            outb_sb[:, slot, :], outb_sb[:, slot, :], xr2_sb[:, slot, :], ALU.add))
        attfree_ve[slot] = ve_out
        p.on("sync", lambda e, t=t, slot=slot: e.dma_start(
            y_out[t * 128:(t + 1) * 128, :], outb_sb[:, slot, :]),
            waits=W((s_ve, ve_out)), incs=[(s_out[slot], 16)])
        out_use[slot] += 1
    p.on("sync", None, waits=W((s_out[0], p.v(s_out[0])), (s_out[1], p.v(s_out[1]))))

    if dbg:
        dbg_xn = nc.dram_tensor("dbg_xn", [TOWN + 1, D], BF16, kind="ExternalOutput").ap()
        dbg_attn = nc.dram_tensor("dbg_attn", [D, TOWN], BF16, kind="ExternalOutput").ap()
        dbg_gl = nc.dram_tensor("dbg_gl", [E * CHAT + 1, 1], I32, kind="ExternalOutput").ap()
        dbg_eo = nc.dram_tensor("dbg_eo", [E * CHAT + 1, D], BF16, kind="ExternalOutput").ap()
        dbg_thr = nc.dram_tensor("dbg_thr", [1, 16], F32, kind="ExternalOutput").ap()
        dbg_rt = nc.dram_tensor("dbg_rt", [128, 8 * NT], F32, kind="ExternalOutput").ap()
        dbg_cnt = nc.dram_tensor("dbg_cnt", [1, 48], F32, kind="ExternalOutput").ap()
        dbg_disp = nc.dram_tensor("dbg_disp", [E * CHAT, D], BF16, kind="ExternalOutput").ap()
        s_dbg = p.sem("s_dbg")
        rtbuf = es3.enter_context(nc.sbuf_tensor("rtbuf", [128, 8 * NT], F32))
        ve0 = p.v(s_ve)
        onv(lambda e: e.tensor_copy(rtbuf[:, 0:NT], g1n_sb[:]))
        onv(lambda e: e.tensor_copy(rtbuf[:, NT:2 * NT], g2n_sb[:]))
        onv(lambda e: e.tensor_copy(rtbuf[:, 2 * NT:3 * NT], loc1_sb[:]))
        onv(lambda e: e.tensor_copy(rtbuf[:, 3 * NT:4 * NT], loc2_sb[:]))
        onv(lambda e: e.tensor_copy(rtbuf[:, 4 * NT:5 * NT], cidx1_sb[:]))
        onv(lambda e: e.tensor_copy(rtbuf[:, 5 * NT:6 * NT], cidx2_sb[:]))
        onv(lambda e: e.tensor_copy(rtbuf[:, 6 * NT:7 * NT], g1n_sb[:]))
        ved = onv(lambda e: e.tensor_copy(rtbuf[:, 7 * NT:8 * NT], g2n_sb[:]))
        p.on("sync", lambda e: e.dma_start(dbg_xn[:], xn_dram[:]),
             waits=W((s_out[0], p.v(s_out[0])), (s_out[1], p.v(s_out[1])),
                     (s_ve, ved)),
             incs=[(s_dbg, 16)])
        p.on("sync", lambda e: e.dma_start(dbg_attn[:], attn_dram[:]), incs=[(s_dbg, 16)])
        p.on("sync", lambda e: e.dma_start(dbg_gl[:], glist_dram[:]), incs=[(s_dbg, 16)])
        p.on("sync", lambda e: e.dma_start(dbg_eo[:], eo_dram[:]), incs=[(s_dbg, 16)])
        p.on("sync", lambda e: e.dma_start(dbg_thr[:], thr_dram[:]), incs=[(s_dbg, 16)])
        p.on("sync", lambda e: e.dma_start(dbg_rt[:], rtbuf[:]), incs=[(s_dbg, 16)])
        dbg_rem = nc.dram_tensor("dbg_rem", [1, RU * 4], BF16, kind="ExternalOutput").ap()
        def _dbgcp(e):
            with nc.allow_low_precision("dbg"):
                return e.tensor_copy(dbgrem_sb[:], dbgrem_sb[:])
        vrem = onv(_dbgcp)
        p.on("sync", lambda e: e.dma_start(dbg_rem[:], dbgrem_sb[:]),
             waits=W((s_ve, vrem)), incs=[(s_dbg, 16)])
        cntbuf = es3.enter_context(nc.sbuf_tensor("cntbuf", [1, 48], F32))
        onv(lambda e: e.tensor_copy(cntbuf[:, 0:16], remprev_sb[:]))
        onv(lambda e: e.tensor_copy(cntbuf[:, 16:24], remall_sb[:]))
        vec = onv(lambda e: e.tensor_copy(cntbuf[:, 24:48], run_sb[:]))
        p.on("sync", lambda e: e.dma_start(dbg_cnt[:], cntbuf[:]),
             waits=W((s_ve, vec)), incs=[(s_dbg, 16)])
        p.on("sync", lambda e: e.dma_start(dbg_disp[:], disp_dram[:]), incs=[(s_dbg, 16)])
        p.on("sync", None, waits=W((s_dbg, p.v(s_dbg))))

    sems = {}
    for name in p.sem_names:
        sems[name] = es.enter_context(nc.semaphore(name))
    block = es.enter_context(nc.Block())
    p.realize(block, sems)
    es3.close()
    es.close()
    return nc


def make_in_maps(x, wq, wk, wv, wo, wg, w_gate, w_down):
    in_maps = []
    for c in range(NC):
        own = np.ascontiguousarray(x[c * BPC:(c + 1) * BPC].reshape(TOWN, D))
        rem = np.ascontiguousarray(
            np.concatenate([x[b] for b in range(B) if b // BPC != c], axis=0))
        prevcols = np.zeros((128, NRB), np.float32)
        prevcols[:, :2 * c] = 1.0
        in_maps.append(dict(
            x_own=own, x_rem=rem, wq=wq, wk=wk, wv=wv, wo=wo, wg=wg,
            w_gate=w_gate, w_down=w_down, prevcols=prevcols))
    return in_maps


def kernel(L0_attn_x, L0_attn_wq, L0_attn_wk, L0_attn_wv, L0_attn_wo,
           L0_ffn_wg, L0_ffn_w_gate, L0_ffn_w_down):
    x = np.ascontiguousarray(np.asarray(L0_attn_x))
    nc = build_program()
    in_maps = make_in_maps(
        x, np.asarray(L0_attn_wq), np.asarray(L0_attn_wk), np.asarray(L0_attn_wv),
        np.asarray(L0_attn_wo), np.asarray(L0_ffn_wg), np.asarray(L0_ffn_w_gate),
        np.asarray(L0_ffn_w_down))
    res = run_bass_kernel_spmd(nc, in_maps, core_ids=list(range(NC)))
    out = np.concatenate([r["y"].reshape(BPC, S, D) for r in res.results], axis=0)
    return out.astype(np.float32)


if __name__ == "__main__":
    import reference as ref
    inputs = ref.setup_inputs()
    out = kernel(**{k: np.asarray(v) for k, v in inputs.items()})
    exp = np.asarray(ref.reference(**inputs))
    rvr = float(((out - exp) ** 2).mean() / (exp ** 2).mean())
    print("resid var ratio:", rvr)


# version 13
# speedup vs baseline: 1.5380x; 1.1066x over previous
import numpy as np
import ml_dtypes
from contextlib import ExitStack

import concourse.bass as bass
import concourse.mybir as mybir
from concourse.bass import IndirectOffsetOnAxis
from concourse.bass_utils import run_bass_kernel_spmd

F32 = mybir.dt.float32
BF16 = mybir.dt.bfloat16
I32 = mybir.dt.int32
U32 = mybir.dt.uint32
AF = mybir.ActivationFunctionType
ALU = mybir.AluOpType
AX = mybir.AxisListType

B, S, D = 16, 2048, 1024
E, F = 8, 4096
CAP = 2 * B * S // E
NC = 8
BPC = B // NC
TOWN = BPC * S
NT = TOWN // 128
NCP = D // 64
KT = D // 128
FT = F // 128
NRB = B - BPC
RU = NRB * 4
CHAT = 1280
NQ = CHAT // 128
EXPM = -16.0
ZROW_XN = TOWN
ZROW_EO = E * CHAT


def bf(a):
    return np.asarray(a).astype(ml_dtypes.bfloat16)


class Prog:
    ENGINES = ("sync", "gpsimd", "tensor", "vector", "scalar")

    def __init__(self):
        self.ops = {e: [] for e in self.ENGINES}
        self.cnt = {}
        self.sem_names = []

    def sem(self, name):
        self.sem_names.append(name)
        self.cnt[name] = 0
        return name

    def on(self, eng, build, waits=(), incs=()):
        for s, n in incs:
            self.cnt[s] += n
        self.ops[eng].append((tuple(waits), tuple(incs), build))

    def v(self, s):
        return self.cnt[s]

    def realize(self, block, sems):
        for eng_name in self.ENGINES:
            ops = self.ops[eng_name]

            def runner(e, ops=ops):
                for waits, incs, build in ops:
                    for s, val in waits:
                        if val > 0:
                            e.wait_ge(sems[s], val)
                    ins = build(e) if build is not None else None
                    if incs:
                        assert ins is not None
                        for s, n in incs:
                            ins.then_inc(sems[s], n)

            getattr(block, eng_name)(runner)


def build_program(dev=0, dbg=False):
    nc = bass.Bass("TRN2", target_bir_lowering=False, debug=False)
    p = Prog()

    nt = NT if dev == 0 else min(NT, 4 * dev)
    ng = nt // 4
    n_ru = RU if dev == 0 else min(RU, dev)
    n_exp = E if dev == 0 else min(E, 2 * dev)

    x_own = nc.dram_tensor("x_own", [TOWN, D], F32, kind="ExternalInput").ap()
    x_rem = nc.dram_tensor("x_rem", [NRB * S, D], F32, kind="ExternalInput").ap()
    wq_in = nc.dram_tensor("wq", [D, D], F32, kind="ExternalInput").ap()
    wk_in = nc.dram_tensor("wk", [D, D], F32, kind="ExternalInput").ap()
    wv_in = nc.dram_tensor("wv", [D, D], F32, kind="ExternalInput").ap()
    wo_in = nc.dram_tensor("wo", [D, D], F32, kind="ExternalInput").ap()
    wg_in = nc.dram_tensor("wg", [D, E], F32, kind="ExternalInput").ap()
    wgate_in = nc.dram_tensor("w_gate", [E, D, F], F32, kind="ExternalInput").ap()
    wdown_in = nc.dram_tensor("w_down", [E, F, D], F32, kind="ExternalInput").ap()
    prevcols_in = nc.dram_tensor("prevcols", [128, NRB], F32, kind="ExternalInput").ap()
    y_out = nc.dram_tensor("y", [TOWN, D], F32, kind="ExternalOutput").ap()

    xn_dram = nc.dram_tensor("xn_dram", [TOWN + 1, D], BF16, kind="Internal").ap()
    xrem_bf = nc.dram_tensor("xrem_bf", [NRB * S, D], BF16, kind="Internal").ap()
    wqt_d = nc.dram_tensor("wqt_d", [D, D], BF16, kind="Internal").ap()
    wkt_d = nc.dram_tensor("wkt_d", [D, D], BF16, kind="Internal").ap()
    wvt_d = nc.dram_tensor("wvt_d", [D, D], BF16, kind="Internal").ap()
    wot_d = nc.dram_tensor("wot_d", [D, D], BF16, kind="Internal").ap()
    attn_dram = nc.dram_tensor("attn_dram", [D, TOWN], BF16, kind="Internal").ap()
    disp_dram = nc.dram_tensor("disp_dram", [E * CHAT, D], BF16, kind="Internal").ap()
    eo_dram = nc.dram_tensor("eo_dram", [E * CHAT + 1, D], BF16, kind="Internal").ap()
    glist_dram = nc.dram_tensor("glist_dram", [E * CHAT + 1, 1], I32, kind="Internal").ap()
    thr_dram = nc.dram_tensor("thr_dram", [1, 16], F32, kind="Internal").ap()
    wgate_bf = nc.dram_tensor("wgate_bf", [E, D, F], BF16, kind="Internal").ap()
    wdown_bf = nc.dram_tensor("wdown_bf", [E, F, D], BF16, kind="Internal").ap()
    cs_dram = nc.dram_tensor("cs_dram", [1, E], F32, kind="Internal").ap()

    tri_d = nc.inline_tensor(bf(np.triu(np.ones((128, 128), np.float32), 1)), name="ctri")
    onesc_d = nc.inline_tensor(bf(np.ones((128, 1), np.float32)), name="conesc")
    onesr_d = nc.inline_tensor(bf(np.ones((1, 128), np.float32)), name="conesr")
    ones1f_d = nc.inline_tensor(np.ones((1, 128), np.float32), name="cones1f")
    iota8_d = nc.inline_tensor(
        np.repeat(np.arange(E, dtype=np.float32)[None, :], 128, 0), name="ciota8")
    tokid_d = nc.inline_tensor(
        (np.arange(128, dtype=np.int32)[:, None]
         + 128 * np.arange(NT, dtype=np.int32)[None, :]), name="ctokid")
    glpre_d = nc.inline_tensor(np.full((E * CHAT + 1, 1), ZROW_XN, np.int32), name="cglpre")

    es = ExitStack()
    SB = lambda name, shape, dtype: es.enter_context(nc.sbuf_tensor(name, shape, dtype))
    PSB = lambda name, n=512: es.enter_context(nc.psum_tensor(name, [128, n], F32))
    bankp = [PSB("pbp0", 1024), PSB("pbp1", 1024)]
    bank = [bankp[0][:, 0:512], bankp[0][:, 512:1024],
            bankp[1][:, 0:512], bankp[1][:, 512:1024],
            PSB("pb4"), PSB("pb5"), PSB("pb6"), PSB("pb7")]
    sb_pair_ap = [bankp[0][:, :], bankp[1][:, :]]

    tri_sb = SB("tri_sb", [128, 128], BF16)
    onesc_sb = SB("onesc_sb", [128, 1], BF16)
    onesr_sb = SB("onesr_sb", [1, 128], BF16)
    ones1f_sb = SB("ones1f_sb", [1, 128], F32)
    iota8_sb = SB("iota8_sb", [128, E], F32)
    tokid_sb = SB("tokid_sb", [128, NT], I32)
    wg_sb = SB("wg_sb", [128, KT, E], BF16)
    wgo_sb = SB("wgo_sb", [128, KT, E + 1], BF16)
    csb_sb = SB("csb_sb", [128, E], F32)
    prevc_sb = SB("prevc_sb", [128, NRB], BF16)
    oh1_sb = SB("oh1_sb", [128, NT, E], BF16)
    oh2_sb = SB("oh2_sb", [128, NT, E], BF16)
    g1n_sb = SB("g1n_sb", [128, NT], F32)
    g2n_sb = SB("g2n_sb", [128, NT], F32)
    loc1_sb = SB("loc1_sb", [128, NT], F32)
    loc2_sb = SB("loc2_sb", [128, NT], F32)
    cidx1_sb = SB("cidx1_sb", [128, NT], I32)
    cidx2_sb = SB("cidx2_sb", [128, NT], I32)
    run_sb = SB("run_sb", [1, 24], F32)
    remprev_sb = SB("remprev_sb", [1, 16], F32)
    remall_sb = SB("remall_sb", [1, 8], F32)
    thr1b_sb = SB("thr1b_sb", [128, 8], F32)
    thr2b_sb = SB("thr2b_sb", [128, 8], F32)
    scr_sb = SB("scr_sb", [128, E + 1], F32)
    scr2_sb = SB("scr2_sb", [128, E], F32)
    mx1_sb = SB("mx1_sb", [128, 1], F32)
    mx2_sb = SB("mx2_sb", [128, 1], F32)
    ix_sb = SB("ix_sb", [128, E], U32)
    i1f_sb = SB("i1f_sb", [128, 1], F32)
    i2f_sb = SB("i2f_sb", [128, 1], F32)
    l2_sb = SB("l2_sb", [128, E], F32)
    sstat_sb = SB("sstat_sb", [128, 8], F32)
    em16_sb = SB("em16_sb", [128, 1], F32)
    sc8_sb = SB("sc8_sb", [128, 1], F32)
    eps_sb = SB("eps_sb", [128, 1], F32)
    zrow_sb = SB("zrow_sb", [1, D], BF16)
    cs1_sb = SB("cs1_sb", [1, 8], F32)
    thrr_sb = SB("thrr_sb", [1, 16], F32)
    tmpe_sb = SB("tmpe_sb", [1, 8], F32)
    pos1_sb = SB("pos1_sb", [128, 1], F32)
    pos2_sb = SB("pos2_sb", [128, 1], F32)
    tmpa_sb = SB("tmpa_sb", [128, E], F32)
    tmpn_sb = SB("tmpn_sb", [128, 1], F32)
    tmpq_sb = SB("tmpq_sb", [128, 1], F32)
    tmpw_sb = SB("tmpw_sb", [128, 1], F32)
    tmpi_sb = SB("tmpi_sb", [128, 1], I32)
    scidx_sb = [SB(f"scidx{i}_sb", [128, 1], I32) for i in range(2)]
    tmpc_sb = SB("tmpc_sb", [128, 1], F32)
    tmpd_sb = SB("tmpd_sb", [128, 1], F32)
    dbgrem_sb = SB("dbgrem_sb", [1, RU * 4], BF16) if dbg else None
    ohpr_sb = SB("ohpr_sb", [128, 4, 2 * E], BF16)
    ohpr2_sb = SB("ohpr2_sb", [128, 3 * E], BF16)

    s_const = p.sem("s_const")
    s_wcast = p.sem("s_wcast")
    s_wtr = p.sem("s_wtr")
    s_xl = [p.sem(f"s_xl{i}") for i in range(2)]
    s_xns = [p.sem(f"s_xns{i}") for i in range(2)]
    s_xnready = p.sem("s_xnready")
    s_xnf = [p.sem(f"s_xnf{i}") for i in range(2)]
    s_te = p.sem("s_te")
    s_ve = p.sem("s_ve")
    s_se = p.sem("s_se")
    s_xrc = [p.sem(f"s_xrc{i}") for i in range(3)]
    s_xrf = [p.sem(f"s_xrf{i}") for i in range(2)]
    s_scat = [p.sem(f"s_scat{i}") for i in range(2)]
    s_glpre = p.sem("s_glpre")
    s_ats = [p.sem(f"s_ats{i}") for i in range(2)]
    s_atready = p.sem("s_atready")
    s_msc = p.sem("s_msc")
    s_glist = p.sem("s_glist")
    s_dg = [p.sem(f"s_dg{i}") for i in range(2)]
    s_dts = [p.sem(f"s_dts{i}") for i in range(2)]
    s_dready = p.sem("s_dready")
    s_dfm = [p.sem(f"s_dfm{i}") for i in range(2)]
    s_wgb = [p.sem(f"s_wgb{i}") for i in range(2)]
    s_wdh = [p.sem(f"s_wdh{i}") for i in range(2)]
    s_eos = [p.sem(f"s_eos{i}") for i in range(2)]
    s_eoready = p.sem("s_eoready")
    s_cmb = [p.sem(f"s_cmb{i}") for i in range(2)]
    s_att = [p.sem(f"s_att{i}") for i in range(2)]
    s_out = [p.sem(f"s_out{i}") for i in range(2)]
    s_bar = p.sem("s_bar")

    W = lambda *pairs: list(pairs)

    def onv(build, waits=(), incs=()):
        p.on("vector", build, waits=W((s_ve, p.v(s_ve)), *waits),
             incs=[(s_ve, 1), *incs])
        return p.v(s_ve)

    def ons(build, waits=(), incs=()):
        p.on("scalar", build, waits=W((s_se, p.v(s_se)), *waits),
             incs=[(s_se, 1), *incs])
        return p.v(s_se)

    def ont(build, waits=(), incs=()):
        p.on("tensor", build, waits=waits, incs=incs)
        return p.v(s_te)

    def barrier():
        tgt = p.v(s_bar) + 5
        for eng in Prog.ENGINES:
            p.on(eng, lambda e: e.nop(), incs=[(s_bar, 1)])
        for eng in Prog.ENGINES:
            p.on(eng, None, waits=W((s_bar, tgt)))

    es1 = ExitStack()
    SB1 = lambda name, shape, dtype: es1.enter_context(nc.sbuf_tensor(name, shape, dtype))
    wqT_sb = SB1("wqT_sb", [128, KT, D], BF16)
    wkT_sb = SB1("wkT_sb", [128, KT, D], BF16)
    wvT_sb = SB1("wvT_sb", [128, KT, D], BF16)
    woT_sb = SB1("woT_sb", [128, KT, D], BF16)
    xld_sb = SB1("xld_sb", [128, 2, D], F32)
    xnt_sb = SB1("xnt_sb", [128, 2, D], BF16)
    xnf_sb = SB1("xnf_sb", [128, 2, KT, 512], BF16)
    qhc_sb = SB1("qhc_sb", [128, 2, 4 * NCP * 128], BF16)
    kpr_sb = SB1("kpr_sb", [128, 2, E, 4 * 128], BF16)
    vpv_sb = SB1("vpv_sb", [128, 2, 4 * NCP * 65], BF16)
    ebuf_sb = SB1("ebuf_sb", [128, 2, 1024], BF16)
    xo_sb = SB1("xo_sb", [128, 2, KT, 512], BF16)
    rcp_sb = SB1("rcp_sb", [1, 512], BF16)
    lnr_sb = SB1("lnr_sb", [1, 512], F32)
    rcpb_sb = SB1("rcpb_sb", [64, 512], BF16)
    oev_sb = SB1("oev_sb", [64, 512], F32)
    opj_sb = SB1("opj_sb", [128, 2, 512], BF16)
    xrf_sb = SB1("xrf_sb", [128, 2, KT, 512], BF16)

    for dst, src in ((tri_sb, tri_d), (onesc_sb, onesc_d), (onesr_sb, onesr_d),
                     (ones1f_sb, ones1f_d), (iota8_sb, iota8_d), (tokid_sb, tokid_d)):
        p.on("sync", lambda e, dst=dst, src=src: e.dma_start(dst[:], src.ap()),
             incs=[(s_const, 16)])
    p.on("sync", lambda e: e.dma_start(glist_dram[:], glpre_d.ap()), incs=[(s_glpre, 16)])

    p.on("gpsimd", lambda e: e.dma_start(wqt_d[:], wq_in[:]), incs=[(s_wcast, 16)])
    p.on("gpsimd", lambda e: e.dma_start(wkt_d[:], wk_in[:]), incs=[(s_wcast, 16)])
    p.on("gpsimd", lambda e: e.dma_start(wvt_d[:], wv_in[:]), incs=[(s_wcast, 16)])
    p.on("gpsimd", lambda e: e.dma_start(wot_d[:], wo_in[:]), incs=[(s_wcast, 16)])
    p.on("gpsimd", lambda e: e.dma_start(
        wg_sb[:], wg_in.rearrange("(a p) e -> p a e", p=128)), incs=[(s_wcast, 16)])
    p.on("gpsimd", lambda e: e.dma_start(prevc_sb[:], prevcols_in[:]),
         incs=[(s_wcast, 16)])
    wcast_all = p.v(s_wcast)

    for wsrc, wdst in ((wqt_d, wqT_sb), (wkt_d, wkT_sb), (wvt_d, wvT_sb), (wot_d, woT_sb)):
        for b_ in range(KT):
            p.on("scalar",
                 lambda e, wsrc=wsrc, wdst=wdst, b_=b_: e.dma_start_transpose(
                     wdst[:, b_, :], wsrc[:, b_ * 128:(b_ + 1) * 128]),
                 waits=W((s_wcast, wcast_all)), incs=[(s_wtr, 16)])
    wtr_all = p.v(s_wtr)
    ve_wqs = onv(lambda e: e.tensor_scalar(
        wqT_sb[:].rearrange("p a d -> p (a d)"),
        wqT_sb[:].rearrange("p a d -> p (a d)"), 0.125, None, ALU.mult),
        waits=W((s_wtr, wtr_all)))

    onv(lambda e: e.memset(em16_sb[:], EXPM))
    onv(lambda e: e.memset(sc8_sb[:], 0.125))
    onv(lambda e: e.memset(eps_sb[:], 1e-5))
    onv(lambda e: e.memset(zrow_sb[:], 0.0))
    onv(lambda e: e.memset(run_sb[:], 0.0))
    onv(lambda e: e.memset(remprev_sb[:], 0.0))
    ve_zeros = onv(lambda e: e.memset(remall_sb[:], 0.0))
    p.on("sync", lambda e: e.dma_start(xn_dram[ZROW_XN:ZROW_XN + 1, :], zrow_sb[:]),
         waits=W((s_ve, ve_zeros)), incs=[(s_const, 16)])
    p.on("sync", lambda e: e.dma_start(eo_dram[ZROW_EO:ZROW_EO + 1, :], zrow_sb[:]),
         waits=W((s_ve, ve_zeros)), incs=[(s_const, 16)])

    const6 = p.v(s_const)
    p.on("vector", None, waits=W((s_const, const6)))
    p.on("gpsimd", None, waits=W((s_const, const6)))
    p.on("tensor", None, waits=W((s_const, const6)))
    onv(lambda e: e.tensor_copy(wgo_sb[:, :, 0:E], wg_sb[:]),
        waits=W((s_wcast, wcast_all)))
    ve_wgo = onv(lambda e: e.memset(wgo_sb[:, :, E:E + 1], 1.0))
    for k in range(KT):
        ont(lambda e, k=k: e.matmul(
            bank[7][0:1, 0:E], onesc_sb[:], wg_sb[:, k, :],
            start=(k == 0), stop=(k == KT - 1)),
            waits=(W((s_wcast, wcast_all)) if k == 0 else ()),
            incs=([(s_te, 1)] if k == KT - 1 else ()))
    te_cs = p.v(s_te)
    ve_cs = onv(lambda e: e.tensor_copy(cs1_sb[:], bank[7][0:1, 0:E]),
                waits=W((s_te, te_cs)))
    p.on("sync", lambda e: e.dma_start(cs_dram[:], cs1_sb[:]),
         waits=W((s_ve, ve_cs)), incs=[(s_msc, 16)])
    p.on("sync", lambda e: e.dma_start(csb_sb[:], cs_dram.partition_broadcast(128)),
         waits=W((s_msc, p.v(s_msc))), incs=[(s_msc, 16)])
    msc_csb = p.v(s_msc)
    ve_csb = onv(lambda e: e.tensor_scalar(csb_sb[:], csb_sb[:], 1.0 / D, None, ALU.mult),
                 waits=W((s_msc, msc_csb)))

    n_store = 0
    xfree_ve = [0, 0]
    xns_use = [0, 0]
    for t in range(nt):
        slot = t % 2
        p.on("sync", lambda e, t=t, slot=slot: e.dma_start(
            xld_sb[:, slot, :], x_own[t * 128:(t + 1) * 128, :]),
            waits=W((s_ve, xfree_ve[slot])), incs=[(s_xl[slot], 16)])
        ld_n = p.v(s_xl[slot])
        onv(lambda e, slot=slot: e.reduce_sum(
            sstat_sb[:, 0:1], xld_sb[:, slot, :], axis=AX.X),
            waits=W((s_xl[slot], ld_n)))
        se_sq = ons(lambda e, slot=slot: e.activation(
            xnt_sb[:, slot, :], xld_sb[:, slot, :], AF.Square, bias=0.0, scale=1.0,
            accum_out=sstat_sb[:, 1:2]),
            waits=W((s_xl[slot], ld_n), (s_xns[slot], 16 * xns_use[slot])))
        onv(lambda e: e.tensor_scalar(
            sstat_sb[:, 2:3], sstat_sb[:, 0:1], 1.0 / D, None, ALU.mult))
        onv(lambda e: e.tensor_tensor(
            sstat_sb[:, 3:4], sstat_sb[:, 2:3], sstat_sb[:, 2:3], ALU.mult))
        ve_var = onv(lambda e: e.tensor_scalar(
            sstat_sb[:, 4:5], sstat_sb[:, 1:2], 1.0 / D, sstat_sb[:, 3:4],
            ALU.mult, ALU.subtract),
            waits=W((s_se, se_sq)))
        se_sqrt = ons(lambda e: e.activation(
            sstat_sb[:, 5:6], sstat_sb[:, 4:5], AF.Sqrt, bias=eps_sb[:], scale=1.0),
            waits=W((s_ve, ve_var)))
        onv(lambda e: e.reciprocal(sstat_sb[:, 6:7], sstat_sb[:, 5:6]),
            waits=W((s_se, se_sqrt)))
        ve_xn = onv(lambda e, slot=slot: e.tensor_scalar(
            xnt_sb[:, slot, :], xld_sb[:, slot, :], sstat_sb[:, 2:3],
            sstat_sb[:, 6:7], ALU.subtract, ALU.mult),
            waits=W((s_xns[slot], 16 * xns_use[slot])))
        xfree_ve[slot] = ve_xn
        p.on("sync", lambda e, t=t, slot=slot: e.dma_start(
            xn_dram[t * 128:(t + 1) * 128, :], xnt_sb[:, slot, :]),
            waits=W((s_ve, ve_xn), (s_const, p.v(s_const))), incs=[(s_xns[slot], 16)])
        xns_use[slot] += 1
        n_store += 1
        if t % 4 == 3:
            p.on("sync", lambda e: e.nop(),
                 waits=W((s_xns[0], 16 * xns_use[0]), (s_xns[1], 16 * xns_use[1])),
                 incs=[(s_xnready, 1)])

    n_cast = (n_ru + 1) // 2
    for c in range(n_cast):
        p.on("gpsimd", lambda e, c=c: e.dma_start(
            xrem_bf[c * 1024:(c + 1) * 1024, :], x_rem[c * 1024:(c + 1) * 1024, :]),
            waits=W((s_xrc[c % 3], 16 * (c // 3)),
                    *(((s_xnready, 1),) if c < 3 else ())),
            incs=[(s_xrc[c % 3], 16)])

    s_wpc = p.sem("s_wpc")
    cast_jobs = []
    for ei2 in range(n_exp):
        for wsel in range(2):
            def cast_job(ei2=ei2, wsel=wsel):
                ci = 2 * ei2 + wsel
                src = wgate_in[ei2] if wsel == 0 else wdown_in[ei2]
                dst = wgate_bf[ei2] if wsel == 0 else wdown_bf[ei2]
                p.on("gpsimd", lambda e, dst=dst, src=src: e.dma_start(dst, src),
                     waits=W((s_wpc, 16 * ci)), incs=[(s_wpc, 16)])
            cast_jobs.append(cast_job)

    xrf_total = {u: 16 * KT * (u // 2 + 1) for u in range(n_ru)}
    xrffree_te = [0, 0]
    scrfree_ve = [ve_cs]
    ohprfree_te = [0, 0, 0, 0]
    p2free_ve = [0]
    ve_oh_rec = {}
    rem_state = {"p1s": 0, "p2s": 0}

    def make_remote_transpose_job(u):
        slot = u % 2

        def job():
            c = u // 2
            for kp in range(KT):
                p.on("sync", lambda e, u=u, kp=kp, slot=slot, c=c: e.dma_start_transpose(
                    xrf_sb[:, slot, kp, :],
                    xrem_bf[u * 512:(u + 1) * 512, kp * 128:(kp + 1) * 128]),
                    waits=W((s_xrc[c % 3], 16 * (c // 3 + 1)), (s_te, xrffree_te[slot]),
                            (s_wtr, wtr_all), (s_se, p.v(s_se))),
                    incs=[(s_xrf[slot], 16)])

        return job

    def make_remote_p1_job(u, tl):
        slot = u % 2

        def job():
            for k in range(KT):
                ont(lambda e, k=k: e.matmul(
                    bank[7][:, 0:E + 1],
                    xrf_sb[:, slot, k, tl * 128:(tl + 1) * 128],
                    wgo_sb[:, k, :],
                    start=(k == 0), stop=(k == KT - 1)),
                    waits=(W((s_xrf[slot], xrf_total[u]), (s_ve, scrfree_ve[0]))
                           if k == 0 else ()),
                    incs=([(s_te, 1)] if k == KT - 1 else ()))
            rem_state["p1s"] += 1
            if tl == 3:
                xrffree_te[slot] = p.v(s_te)
            te_lg = p.v(s_te)
            scrfree_ve[0] = onv(
                lambda e: e.tensor_copy(scr_sb[:], bank[7][:, 0:E + 1]),
                waits=W((s_te, te_lg)))
            onv(lambda e: e.tensor_scalar(
                tmpa_sb[:], csb_sb[:], scr_sb[:, E:E + 1], None, ALU.mult),
                waits=W((s_ve, ve_csb)))
            onv(lambda e: e.tensor_tensor(
                scr2_sb[:], scr_sb[:, 0:E], tmpa_sb[:], ALU.subtract))
            onv(lambda e: e.reduce_max(mx1_sb[:], scr2_sb[:], axis=AX.X))
            onv(lambda e: e.max_index(
                ix_sb[:], mx1_sb[:, 0:1].broadcast_to((128, E)), scr2_sb[:]))
            onv(lambda e: e.tensor_copy(i1f_sb[:], ix_sb[:, 0:1]))
            onv(lambda e: e.match_replace(
                l2_sb[:], mx1_sb[:, 0:1].broadcast_to((128, E)), scr2_sb[:], -1e30))
            onv(lambda e: e.reduce_max(mx2_sb[:], l2_sb[:], axis=AX.X))
            onv(lambda e: e.max_index(
                ix_sb[:], mx2_sb[:, 0:1].broadcast_to((128, E)), l2_sb[:]))
            onv(lambda e: e.tensor_copy(i2f_sb[:], ix_sb[:, 0:1]))
            onv(lambda e, tl=tl: e.tensor_scalar(
                ohpr_sb[:, tl, 0:E], iota8_sb[:], i1f_sb[:], None, ALU.is_equal),
                waits=W((s_te, ohprfree_te[tl])))
            ve_oh_rec[(u, tl)] = onv(lambda e, tl=tl: e.tensor_scalar(
                ohpr_sb[:, tl, E:2 * E], iota8_sb[:], i2f_sb[:], None, ALU.is_equal))

        return job

    def make_remote_p2_job(u, tl):
        chunk_col = u // 4

        def job():
            ve_need = max(ve_oh_rec[(u, tl)], p2free_ve[0])
            ont(lambda e, tl=tl: e.matmul(
                bank[7][0:1, 16:32], prevc_sb[:, chunk_col:chunk_col + 1],
                ohpr_sb[:, tl, :], start=True, stop=False),
                waits=W((s_ve, ve_need)))
            ont(lambda e, tl=tl: e.matmul(
                bank[7][0:1, 32:40], onesc_sb[:], ohpr_sb[:, tl, 0:E],
                start=False, stop=True),
                incs=[(s_te, 1)])
            te_cnt = p.v(s_te)
            ohprfree_te[tl] = te_cnt
            onv(lambda e: e.tensor_tensor(
                remprev_sb[:], remprev_sb[:], bank[7][0:1, 16:32], ALU.add),
                waits=W((s_te, te_cnt)))
            if dbg:
                def _dbgred(e, u=u, tl=tl):
                    with nc.allow_low_precision("dbg count trace"):
                        return e.reduce_sum(
                            dbgrem_sb[0:1, u * 4 + tl:u * 4 + tl + 1],
                            bank[7][0:1, 32:40], axis=AX.X)
                onv(_dbgred)
            p2free_ve[0] = onv(lambda e: e.tensor_tensor(
                remall_sb[:], remall_sb[:], bank[7][0:1, 32:40], ALU.add))
            rem_state["p2s"] += 1

        return job

    remote_jobs = []
    for u in range(n_ru):
        remote_jobs.append(make_remote_transpose_job(u))
        for tl in range(4):
            remote_jobs.append(make_remote_p1_job(u, tl))
        for tl in range(4):
            remote_jobs.append(make_remote_p2_job(u, tl))
        if u % 3 == 2 and cast_jobs:
            remote_jobs.append(cast_jobs.pop(0))
    remote_jobs.extend(cast_jobs)
    remote_iter = iter(remote_jobs)

    def emit_some_tr():
        pass

    def emit_some_remote(nmax):
        for _ in range(nmax):
            job = next(remote_iter, None)
            if job is None:
                return
            job()

    scat_use = [0, 0]

    def emit_own_routing_tile(t, gslot, xnf_ready):
        tl = t % 4
        for k in range(KT):
            ont(lambda e, k=k: e.matmul(
                bank[3][:, 0:E],
                xnf_sb[:, gslot, k, tl * 128:(tl + 1) * 128], wg_sb[:, k, :],
                start=(k == 0), stop=(k == KT - 1)),
                waits=(W((s_xnf[gslot], xnf_ready), (s_ve, p.v(s_ve)),
                         (s_se, p.v(s_se))) if k == 0 else ()),
                incs=([(s_te, 1)] if k == KT - 1 else ()))
        te_lg = p.v(s_te)
        onv(lambda e: e.tensor_copy(scr_sb[:, 0:E], bank[3][:, 0:E]),
            waits=W((s_te, te_lg)))
        onv(lambda e: e.reduce_max(mx1_sb[:], scr_sb[:, 0:E], axis=AX.X))
        onv(lambda e: e.max_index(
            ix_sb[:], mx1_sb[:, 0:1].broadcast_to((128, E)), scr_sb[:, 0:E]))
        onv(lambda e: e.tensor_copy(i1f_sb[:], ix_sb[:, 0:1]))
        onv(lambda e: e.match_replace(
            l2_sb[:], mx1_sb[:, 0:1].broadcast_to((128, E)), scr_sb[:, 0:E], -1e30))
        onv(lambda e: e.reduce_max(mx2_sb[:], l2_sb[:], axis=AX.X))
        onv(lambda e: e.max_index(
            ix_sb[:], mx2_sb[:, 0:1].broadcast_to((128, E)), l2_sb[:]))
        ve_idx = onv(lambda e: e.tensor_copy(i2f_sb[:], ix_sb[:, 0:1]))
        se_g = ons(lambda e, t=t: e.activation(
            g1n_sb[:, t:t + 1], mx2_sb[:], AF.Sigmoid, bias=mx1_sb[:], scale=-1.0),
            waits=W((s_ve, ve_idx)))
        onv(lambda e, t=t: e.tensor_scalar(
            g2n_sb[:, t:t + 1], g1n_sb[:, t:t + 1], -1.0, 1.0, ALU.mult, ALU.add),
            waits=W((s_se, se_g)))
        onv(lambda e, t=t: e.tensor_scalar(
            oh1_sb[:, t, :], iota8_sb[:], i1f_sb[:], None, ALU.is_equal))
        onv(lambda e, t=t: e.tensor_scalar(
            oh2_sb[:, t, :], iota8_sb[:], i2f_sb[:], None, ALU.is_equal))
        onv(lambda e, t=t: e.tensor_copy(ohpr2_sb[:, 0:E], oh1_sb[:, t, :]),
            waits=W((s_te, p.v(s_te))))
        onv(lambda e, t=t: e.tensor_copy(ohpr2_sb[:, E:2 * E], oh2_sb[:, t, :]))
        ve_oh3 = onv(lambda e, t=t: e.tensor_tensor(
            ohpr2_sb[:, 2 * E:3 * E], oh1_sb[:, t, :], oh2_sb[:, t, :], ALU.add))
        ont(lambda e: e.matmul(
            bank[3][:, 32:56], tri_sb[:], ohpr2_sb[:], start=True, stop=False),
            waits=W((s_ve, ve_oh3)))
        ont(lambda e: e.matmul(
            bank[3][:, 32:56], ones1f_sb[:], run_sb[:], start=False, stop=True),
            incs=[(s_te, 1)])
        te_loc = p.v(s_te)
        ont(lambda e: e.matmul(
            bank[2][0:1, 32:56], onesc_sb[:], ohpr2_sb[:], start=True, stop=True),
            incs=[(s_te, 1)])
        te_cs2 = p.v(s_te)
        onv(lambda e, t=t: e.tensor_tensor(
            tmpa_sb[:], bank[3][:, 32:40], oh1_sb[:, t, :], ALU.mult),
            waits=W((s_te, te_loc)))
        onv(lambda e, t=t: e.reduce_sum(loc1_sb[:, t:t + 1], tmpa_sb[:], axis=AX.X))
        onv(lambda e, t=t: e.tensor_tensor(
            tmpa_sb[:], bank[3][:, 40:48], oh2_sb[:, t, :], ALU.mult))
        onv(lambda e, t=t: e.reduce_sum(loc2_sb[:, t:t + 1], tmpa_sb[:], axis=AX.X))
        onv(lambda e, t=t: e.tensor_tensor(
            tmpa_sb[:], bank[3][:, 48:56], oh1_sb[:, t, :], ALU.mult))
        onv(lambda e: e.reduce_sum(pos1_sb[:], tmpa_sb[:], axis=AX.X))
        onv(lambda e, t=t: e.tensor_tensor(
            tmpa_sb[:], bank[3][:, 48:56], oh2_sb[:, t, :], ALU.mult))
        onv(lambda e: e.reduce_sum(pos2_sb[:], tmpa_sb[:], axis=AX.X))
        ve_run = onv(lambda e: e.tensor_tensor(
            run_sb[:], run_sb[:], bank[2][0:1, 32:56], ALU.add),
            waits=W((s_te, te_cs2)))
        for ch_i, (idxf, posx, cidx) in enumerate(
                ((i1f_sb, pos1_sb, cidx1_sb), (i2f_sb, pos2_sb, cidx2_sb))):
            onv(lambda e, posx=posx: e.tensor_scalar(
                tmpd_sb[:], posx[:], float(CHAT), None, ALU.is_ge))
            onv(lambda e: e.tensor_scalar(
                tmpn_sb[:], tmpd_sb[:], -1.0, 1.0, ALU.mult, ALU.add))
            onv(lambda e, idxf=idxf: e.tensor_scalar(
                tmpc_sb[:], idxf[:], float(CHAT), None, ALU.mult))
            onv(lambda e, posx=posx: e.tensor_tensor(
                tmpc_sb[:], tmpc_sb[:], posx[:], ALU.add))
            onv(lambda e: e.tensor_tensor(tmpc_sb[:], tmpc_sb[:], tmpn_sb[:], ALU.mult))
            onv(lambda e: e.scalar_tensor_tensor(
                tmpc_sb[:], tmpd_sb[:], float(E * CHAT), tmpc_sb[:], ALU.mult, ALU.add))
            ve_sc = onv(lambda e, ch_i=ch_i: e.tensor_copy(
                scidx_sb[ch_i][:], tmpc_sb[:]),
                waits=W((s_scat[ch_i], 16 * scat_use[ch_i])))
            scat_use[ch_i] += 1
            p.on("gpsimd", lambda e, t=t, ch_i=ch_i: e.indirect_dma_start(
                glist_dram[:],
                IndirectOffsetOnAxis(scidx_sb[ch_i][:], 0),
                tokid_sb[:, t:t + 1], None),
                waits=W((s_ve, ve_sc), (s_glpre, 16)), incs=[(s_scat[ch_i], 16)])
            onv(lambda e, posx=posx: e.tensor_scalar(
                tmpc_sb[:], posx[:], 1.0 / NQ, 0.5 / NQ, ALU.mult, ALU.add))
            onv(lambda e: e.tensor_copy(tmpi_sb[:], tmpc_sb[:]))
            onv(lambda e: e.tensor_copy(tmpc_sb[:], tmpi_sb[:]))
            onv(lambda e, posx=posx: e.scalar_tensor_tensor(
                tmpq_sb[:], tmpc_sb[:], -float(NQ), posx[:], ALU.mult, ALU.add))
            onv(lambda e: e.tensor_scalar(tmpw_sb[:], tmpq_sb[:], 0.0, None, ALU.is_lt))
            onv(lambda e: e.tensor_tensor(tmpc_sb[:], tmpc_sb[:], tmpw_sb[:], ALU.subtract))
            onv(lambda e: e.scalar_tensor_tensor(
                tmpq_sb[:], tmpw_sb[:], float(NQ), tmpq_sb[:], ALU.mult, ALU.add))
            onv(lambda e: e.tensor_scalar(tmpw_sb[:], tmpq_sb[:], float(NQ), None, ALU.is_ge))
            onv(lambda e: e.tensor_tensor(tmpc_sb[:], tmpc_sb[:], tmpw_sb[:], ALU.add))
            onv(lambda e: e.scalar_tensor_tensor(
                tmpq_sb[:], tmpw_sb[:], -float(NQ), tmpq_sb[:], ALU.mult, ALU.add))
            onv(lambda e: e.tensor_scalar(
                tmpq_sb[:], tmpq_sb[:], 128.0, None, ALU.mult))
            onv(lambda e: e.tensor_tensor(tmpq_sb[:], tmpq_sb[:], tmpc_sb[:], ALU.add))
            onv(lambda e, idxf=idxf: e.tensor_scalar(
                tmpc_sb[:], idxf[:], float(CHAT), None, ALU.mult))
            onv(lambda e: e.tensor_tensor(tmpq_sb[:], tmpq_sb[:], tmpc_sb[:], ALU.add))
            onv(lambda e: e.tensor_tensor(tmpq_sb[:], tmpq_sb[:], tmpn_sb[:], ALU.mult))
            onv(lambda e: e.scalar_tensor_tensor(
                tmpq_sb[:], tmpd_sb[:], float(ZROW_EO), tmpq_sb[:], ALU.mult, ALU.add))
            onv(lambda e, t=t, cidx=cidx: e.tensor_copy(cidx[:, t:t + 1], tmpq_sb[:]))
        return ve_run

    xnffree_te = [0, 0]
    last_exp_pair = [0, 0]
    obank_free = [(0, 0), (0, 0)]
    rcpbfree_ve = [0]
    terb_prev = [0]
    opjfree_ve = [0, 0]
    xnf_ready_map = {}

    def emit_xnf(g):
        gslot = g % 2
        for kp in range(KT):
            p.on("sync", lambda e, g=g, kp=kp, gslot=gslot: e.dma_start_transpose(
                xnf_sb[:, gslot, kp, :],
                xn_dram[g * 512:(g + 1) * 512, kp * 128:(kp + 1) * 128]),
                waits=W((s_xnready, g + 1), (s_te, xnffree_te[gslot]),
                        (s_wtr, wtr_all), (s_se, p.v(s_se))),
                incs=[(s_xnf[gslot], 16)])
        xnf_ready_map[g] = p.v(s_xnf[gslot])

    emit_xnf(0)
    for g in range(ng):
        gslot = g % 2
        xnf_ready = xnf_ready_map[g]

        ve_qevac = 0
        for a in range(NCP // 2):
            if a == 0:
                w_q0 = W((s_xnf[gslot], xnf_ready),
                         (s_ve, max(opjfree_ve[0], opjfree_ve[1])),
                         (s_se, max(last_exp_pair)))
            else:
                w_q0 = W((s_xnf[gslot], xnf_ready), (s_ve, ve_qevac))
            for k in range(KT):
                ont(lambda e, a=a, k=k, gslot=gslot: e.matmul(
                    bank[0][0:64, :],
                    wqT_sb[:, k, (2 * a) * 64:(2 * a) * 64 + 64],
                    xnf_sb[:, gslot, k, :],
                    start=(k == 0), stop=(k == KT - 1), tile_position=(0, 0)),
                    waits=(w_q0 if k == 0 else ()))
                ont(lambda e, a=a, k=k, gslot=gslot: e.matmul(
                    bank[1][64:128, :],
                    wqT_sb[:, k, (2 * a) * 64:(2 * a) * 64 + 64],
                    xnf_sb[:, gslot, k, :],
                    start=(k == 0), stop=(k == KT - 1), tile_position=(0, 64)))
                ont(lambda e, a=a, k=k, gslot=gslot: e.matmul(
                    bank[2][0:64, :],
                    wqT_sb[:, k, (2 * a + 1) * 64:(2 * a + 1) * 64 + 64],
                    xnf_sb[:, gslot, k, :],
                    start=(k == 0), stop=(k == KT - 1), tile_position=(0, 0)))
                ont(lambda e, a=a, k=k, gslot=gslot: e.matmul(
                    bank[3][64:128, :],
                    wqT_sb[:, k, (2 * a + 1) * 64:(2 * a + 1) * 64 + 64],
                    xnf_sb[:, gslot, k, :],
                    start=(k == 0), stop=(k == KT - 1), tile_position=(0, 64)),
                    incs=([(s_te, 1)] if k == KT - 1 else ()))
            te_q = p.v(s_te)
            for cc, bnklo, bnkhi in ((2 * a, bank[0], bank[1]),
                                     (2 * a + 1, bank[2], bank[3])):
                onv(lambda e, cc=cc, bnklo=bnklo, gslot=gslot: e.tensor_copy(
                    qhc_sb[:, gslot, :].rearrange("p (h c j) -> p h c j",
                                                  h=4, c=NCP)[0:64, :, cc, :],
                    bnklo[0:64, :].rearrange("p (h j) -> p h j", h=4)),
                    waits=W((s_te, te_q)))
                ve_qevac = onv(lambda e, cc=cc, bnkhi=bnkhi, gslot=gslot: e.tensor_copy(
                    qhc_sb[:, gslot, :].rearrange("p (h c j) -> p h c j",
                                                  h=4, c=NCP)[64:128, :, cc, :],
                    bnkhi[64:128, :].rearrange("p (h j) -> p h j", h=4)))

        ve_kevac = 0
        for a in range(NCP // 2):
            w_k0 = W((s_ve, ve_qevac if a == 0 else ve_kevac))
            for k in range(KT):
                ont(lambda e, a=a, k=k, gslot=gslot: e.matmul(
                    bank[0][0:64, :],
                    wkT_sb[:, k, (2 * a) * 64:(2 * a) * 64 + 64],
                    xnf_sb[:, gslot, k, :],
                    start=(k == 0), stop=(k == KT - 1), tile_position=(0, 0)),
                    waits=(w_k0 if k == 0 else ()))
                ont(lambda e, a=a, k=k, gslot=gslot: e.matmul(
                    bank[1][64:128, :],
                    wkT_sb[:, k, (2 * a + 1) * 64:(2 * a + 1) * 64 + 64],
                    xnf_sb[:, gslot, k, :],
                    start=(k == 0), stop=(k == KT - 1), tile_position=(0, 64)),
                    incs=([(s_te, 1)] if k == KT - 1 else ()))
            te_k = p.v(s_te)
            onv(lambda e, a=a, gslot=gslot: e.tensor_copy(
                kpr_sb[0:64, gslot, a, :], bank[0][0:64, :]),
                waits=W((s_te, te_k)))
            ve_kevac = onv(lambda e, a=a, gslot=gslot: e.tensor_copy(
                kpr_sb[64:128, gslot, a, :], bank[1][64:128, :]))

        onv(lambda e, gslot=gslot: e.memset(vpv_sb[:, gslot, :], 1.0),
            waits=W((s_te, p.v(s_te))))
        ve_vevac = {0: ve_kevac, 1: ve_kevac, 2: ve_qevac, 3: ve_qevac}
        ve_vlast = 0
        for tt in range(4):
            for oc in range(2):
                bi = (1 - tt % 2) * 2 + oc
                bnk = bank[bi]
                for k in range(KT):
                    ont(lambda e, tt=tt, oc=oc, k=k, gslot=gslot, bnk=bnk: e.matmul(
                        bnk[:, :],
                        xnf_sb[:, gslot, k, tt * 128:(tt + 1) * 128],
                        wvT_sb[:, k, oc * 512:(oc + 1) * 512],
                        start=(k == 0), stop=(k == KT - 1)),
                        waits=(W((s_ve, ve_vevac[bi])) if k == 0 else ()),
                        incs=([(s_te, 1)] if k == KT - 1 else ()))
                te_v = p.v(s_te)
                ve_vlast = onv(lambda e, tt=tt, oc=oc, gslot=gslot, bnk=bnk: e.tensor_copy(
                    vpv_sb[:, gslot, :].rearrange("p (h c w) -> p h c w",
                                                  h=4, c=NCP)[:, tt, oc * 8:(oc + 1) * 8, 0:64],
                    bnk[:, :].rearrange("p (c w) -> p c w", c=8)),
                    waits=W((s_te, te_v)))
                ve_vevac[bi] = ve_vlast

        ve_bank3 = 0
        for tl in range(4):
            ve_bank3 = emit_own_routing_tile(g * 4 + tl, gslot, xnf_ready)
        xnffree_te[gslot] = p.v(s_te)
        if g + 1 < ng:
            emit_xnf(g + 1)

        def tail_act(pend):
            pobk, pte_o, ph, pch = pend
            pobank = bank[4 + pobk]
            se_ln = ons(lambda e, pobank=pobank: e.activation(
                lnr_sb[:], pobank[64:65, :], AF.Ln, bias=0.0, scale=1.0),
                waits=W((s_te, pte_o)))
            se_rc = ons(lambda e: e.activation(
                rcp_sb[:], lnr_sb[:], AF.Exp, bias=0.0, scale=-1.0),
                waits=W((s_te, terb_prev[0])))
            return se_ln, se_rc

        def tail_rest(pend, se_ln, se_rc, gslot=gslot):
            pobk, pte_o, ph, pch = pend
            pobank = bank[4 + pobk]
            ont(lambda e: e.matmul(
                bank[6][0:64, :], onesr_sb[0:1, 0:64], rcp_sb[:],
                start=True, stop=True),
                waits=W((s_se, se_rc), (s_ve, rcpbfree_ve[0])),
                incs=[(s_te, 1)])
            te_rb = p.v(s_te)
            terb_prev[0] = te_rb
            rcpbfree_ve[0] = onv(lambda e: e.tensor_copy(
                rcpb_sb[:], bank[6][0:64, :]), waits=W((s_te, te_rb)))
            ve_oev = onv(lambda e, pobank=pobank: e.tensor_copy(
                oev_sb[:], pobank[0:64, :]))
            ve_l = ve_oev
            for ci in range(4):
                cplane = pch * 4 + ci
                ptile, phalf = cplane // 2, cplane % 2
                ve_l = onv(lambda e, ci=ci, ptile=ptile, phalf=phalf, ph=ph,
                           gslot=gslot: e.tensor_tensor(
                    xo_sb[64 * phalf:64 * phalf + 64, gslot, ptile,
                          ph * 128:(ph + 1) * 128],
                    oev_sb[:, ci * 128:(ci + 1) * 128],
                    rcpb_sb[:, ci * 128:(ci + 1) * 128], ALU.mult))
            obank_free[pobk] = (se_ln, ve_oev)
            return ve_l

        pending = None
        ve_xo_last = 0
        for h in range(4):
            for ch in range(4):
                emit_some_tr()
                obk = ch % 2
                obank = bank[4 + obk]
                se_e1 = {}

                def sc_exp(a, extra_ve=0, h=h, ch=ch, gslot=gslot):
                    pair = a % 2
                    sb0, sb1 = bank[2 * pair], bank[2 * pair + 1]
                    w0 = [(s_se, last_exp_pair[pair])]
                    if extra_ve:
                        w0.append((s_ve, extra_ve))
                    ont(lambda e, a=a, h=h, ch=ch, gslot=gslot, sb0=sb0: e.matmul(
                        sb0[:, :],
                        kpr_sb[0:64, gslot, a, h * 128:(h + 1) * 128],
                        qhc_sb[0:64, gslot, h * 2048 + ch * 512:h * 2048 + (ch + 1) * 512],
                        start=True, stop=True, tile_position=(0, 0)),
                        waits=w0, incs=[(s_te, 1)])
                    ont(lambda e, a=a, h=h, ch=ch, gslot=gslot, sb1=sb1: e.matmul(
                        sb1[:, :],
                        kpr_sb[64:128, gslot, a, h * 128:(h + 1) * 128],
                        qhc_sb[64:128, gslot, h * 2048 + ch * 512:h * 2048 + (ch + 1) * 512],
                        start=True, stop=True, tile_position=(64, 0)),
                        incs=[(s_te, 1)])
                    ts = p.v(s_te)
                    se = ons(lambda e, pair=pair, a=a: e.activation(
                        ebuf_sb[:, a % 2, :], sb_pair_ap[pair],
                        AF.Exp, bias=0.0, scale=1.0),
                        waits=W((s_te, ts)))
                    last_exp_pair[pair] = se
                    se_e1[a] = se

                def pv(a, h=h, gslot=gslot, obank=obank, obk=obk):
                    w0 = [(s_se, se_e1[a])]
                    if a == 0:
                        w0.append((s_ve, obank_free[obk][1]))
                    ont(lambda e, a=a, h=h, gslot=gslot, obank=obank: e.matmul(
                        obank[0:65, :],
                        vpv_sb[:, gslot, (h * NCP + 2 * a) * 65:(h * NCP + 2 * a) * 65 + 65],
                        ebuf_sb[:, a % 2, 0:512],
                        start=(a == 0), stop=False),
                        waits=w0)
                    ont(lambda e, a=a, h=h, gslot=gslot, obank=obank: e.matmul(
                        obank[0:65, :],
                        vpv_sb[:, gslot, (h * NCP + 2 * a + 1) * 65:(h * NCP + 2 * a + 1) * 65 + 65],
                        ebuf_sb[:, a % 2, 512:1024],
                        start=False, stop=(a == NCP // 2 - 1)),
                        incs=([(s_te, 1)] if a == NCP // 2 - 1 else ()))

                if pending is not None:
                    se_ln, se_rc = tail_act(pending)
                first = (h == 0 and ch == 0)
                sc_exp(0, extra_ve=(ve_vlast if first else 0))
                sc_exp(1, extra_ve=(ve_bank3 if first else 0))
                if pending is not None:
                    ve_xo_last = tail_rest(pending, se_ln, se_rc)
                    pending = None
                for a in range(2, NCP // 2):
                    pv(a - 2)
                    sc_exp(a)
                pv(NCP // 2 - 2)
                pv(NCP // 2 - 1)
                pending = (obk, p.v(s_te), h, ch)
                emit_some_remote(4)
        se_ln, se_rc = tail_act(pending)
        ve_xo_last = tail_rest(pending, se_ln, se_rc)
        pending = None

        for mt in range(KT):
            pbank = bank[mt % 2]
            if mt < 2:
                w0 = [(s_ve, max(ve_xo_last, opjfree_ve[mt % 2])),
                      (s_se, last_exp_pair[0])]
            else:
                w0 = [(s_ve, opjfree_ve[mt % 2])]
            for k in range(KT):
                ont(lambda e, mt=mt, k=k, gslot=gslot, pbank=pbank: e.matmul(
                    pbank[:, :],
                    woT_sb[:, k, mt * 128:(mt + 1) * 128],
                    xo_sb[:, gslot, k, :],
                    start=(k == 0), stop=(k == KT - 1)),
                    waits=(w0 if k == 0 else ()),
                    incs=([(s_te, 1)] if k == KT - 1 else ()))
            te_op = p.v(s_te)
            ve_opj = onv(lambda e, mt=mt, pbank=pbank: e.tensor_copy(
                opj_sb[:, mt % 2, :], pbank[:, :]),
                waits=W((s_te, te_op), (s_ats[mt % 2], p.v(s_ats[mt % 2]))))
            opjfree_ve[mt % 2] = ve_opj
            p.on("sync", lambda e, g=g, mt=mt: e.dma_start(
                attn_dram[mt * 128:(mt + 1) * 128, g * 512:(g + 1) * 512],
                opj_sb[:, mt % 2, :]),
                waits=W((s_ve, ve_opj)), incs=[(s_ats[mt % 2], 16)])
        p.on("sync", lambda e: e.nop(),
             waits=W((s_ats[0], p.v(s_ats[0])), (s_ats[1], p.v(s_ats[1]))),
             incs=[(s_atready, 1)])

    emit_some_remote(10 ** 9)
    atready_all = p.v(s_atready)

    onv(lambda e: e.tensor_scalar(
        thrr_sb[:, 0:8], remprev_sb[:, 0:8], -1.0, float(CAP), ALU.mult, ALU.add))
    onv(lambda e: e.tensor_tensor(tmpe_sb[:], remall_sb[:], run_sb[:, 0:8], ALU.add))
    onv(lambda e: e.tensor_tensor(tmpe_sb[:], tmpe_sb[:], remprev_sb[:, 8:16], ALU.add))
    ve_thr = onv(lambda e: e.tensor_scalar(
        thrr_sb[:, 8:16], tmpe_sb[:], -1.0, float(CAP), ALU.mult, ALU.add))
    p.on("sync", lambda e: e.dma_start(thr_dram[:], thrr_sb[:]),
         waits=W((s_ve, ve_thr)), incs=[(s_msc, 16)])
    p.on("sync", lambda e: e.dma_start(
        thr1b_sb[:], thr_dram[0:1, 0:8].partition_broadcast(128)),
        waits=W((s_msc, p.v(s_msc))), incs=[(s_msc, 16)])
    p.on("sync", lambda e: e.dma_start(
        thr2b_sb[:], thr_dram[0:1, 8:16].partition_broadcast(128)),
        waits=W((s_msc, p.v(s_msc))), incs=[(s_msc, 16)])
    msc_thr = p.v(s_msc)

    barrier()
    es1.close()

    es2 = ExitStack()
    SB2 = lambda name, shape, dtype: es2.enter_context(nc.sbuf_tensor(name, shape, dtype))
    glist_sb = SB2("glist_sb", [128, 2, NQ], I32)
    dtm_sb = SB2("dtm_sb", [128, 2, D], BF16)
    dfm_sb = SB2("dfm_sb", [128, 2, KT, CHAT], BF16)
    wgb_sb = SB2("wgb_sb", [128, 2, KT, 256], BF16)
    wdh_sb = SB2("wdh_sb", [128, 2, FT, 512], BF16)
    hfm_sb = SB2("hfm_sb", [128, FT, CHAT], BF16)
    eo_sb = SB2("eo_sb", [128, 2, 512], BF16)


    CH1 = (512, 512, 256)
    dtm_use = [0, 0]
    prev_dg = [0, 0]
    eo_use = [0, 0]
    wgbfree_se = [0, 0]
    wdhfree_ve = [0, 0]
    dfmfree_se = [0, 0]
    gelu_cnt = {0: 0, 1: 0, 2: 0}
    ffn_eo_ve = [0, 0]
    dfm_ready_map = {}

    def emit_dispatch(ei):
        eslot = ei % 2
        p.on("sync", lambda e, ei=ei, eslot=eslot: e.dma_start(
            glist_sb[:, eslot, :],
            glist_dram[ei * CHAT:(ei + 1) * CHAT, 0].rearrange("(p q) -> p q", q=NQ)),
            waits=W((s_scat[0], 16 * scat_use[0]), (s_scat[1], 16 * scat_use[1]),
                    (s_glist, 16 * ei),
                    (s_dg[0], prev_dg[0]), (s_dg[1], prev_dg[1])),
            incs=[(s_glist, 16)])
        gl_n = p.v(s_glist)
        for q in range(NQ):
            dslot = q % 2
            p.on("gpsimd", lambda e, q=q, dslot=dslot, eslot=eslot: e.indirect_dma_start(
                dtm_sb[:, dslot, :], None, xn_dram[:],
                IndirectOffsetOnAxis(glist_sb[:, eslot, q:q + 1], 0),
                ),
                waits=W((s_glist, gl_n), (s_dts[dslot], 16 * dtm_use[dslot]),
                        (s_xns[0], 16 * xns_use[0]), (s_xns[1], 16 * xns_use[1]),
                        (s_const, p.v(s_const))),
                incs=[(s_dg[dslot], 16)])
            p.on("sync", lambda e, ei=ei, q=q, dslot=dslot: e.dma_start(
                disp_dram[ei * CHAT + q * 128:ei * CHAT + (q + 1) * 128, :],
                dtm_sb[:, dslot, :]),
                waits=W((s_dg[dslot], p.v(s_dg[dslot]))),
                incs=[(s_dts[dslot], 16)])
            dtm_use[dslot] += 1
        prev_dg[0], prev_dg[1] = p.v(s_dg[0]), p.v(s_dg[1])
        p.on("sync", lambda e: e.nop(),
             waits=W((s_dts[0], p.v(s_dts[0])), (s_dts[1], p.v(s_dts[1]))),
             incs=[(s_dready, 1)])
        for kp in range(KT):
            p.on("sync", lambda e, ei=ei, kp=kp, eslot=eslot: e.dma_start_transpose(
                dfm_sb[:, eslot, kp, :],
                disp_dram[ei * CHAT:(ei + 1) * CHAT, kp * 128:(kp + 1) * 128]),
                waits=W((s_dready, ei + 1), (s_se, dfmfree_se[eslot])),
                incs=[(s_dfm[eslot], 16)])
        dfm_ready_map[ei] = p.v(s_dfm[eslot])

    emit_dispatch(0)
    for ei in range(n_exp):
        eslot = ei % 2
        if ei + 1 < n_exp:
            emit_dispatch(ei + 1)
        dfm_ready = dfm_ready_map[ei]

        wgb_n_map = {}

        def load_wgb(fg, ei=ei):
            wslot = fg % 2
            p.on("scalar", lambda e, ei=ei, fg=fg, wslot=wslot: e.dma_start(
                wgb_sb[:, wslot, :, :],
                wgate_bf[ei, :, fg * 256:(fg + 1) * 256].rearrange(
                    "(a p) f -> p a f", p=128)),
                waits=W((s_se, wgbfree_se[wslot]), (s_wpc, 32 * ei + 16)),
                incs=[(s_wgb[wslot], 16)])
            wgb_n_map[fg] = p.v(s_wgb[wslot])

        load_wgb(0)
        load_wgb(1)
        for fg in range(FT // 2):
            wslot = fg % 2
            wgb_n = wgb_n_map[fg]
            for fi in range(2):
                ft = fg * 2 + fi
                co = 0
                for ci, cw in enumerate(CH1):
                    bnk = bank[ci]
                    for k in range(KT):
                        ont(lambda e, k=k, cw=cw, co=co, fi=fi, wslot=wslot,
                            eslot=eslot, bnk=bnk: e.matmul(
                                bnk[:, 0:cw],
                                wgb_sb[:, wslot, k, fi * 128:(fi + 1) * 128],
                                dfm_sb[:, eslot, k, co:co + cw],
                                start=(k == 0), stop=(k == KT - 1)),
                            waits=(W((s_wgb[wslot], wgb_n), (s_dfm[eslot], dfm_ready),
                                     (s_se, gelu_cnt[ci])) if k == 0 else ()),
                            incs=([(s_te, 1)] if k == KT - 1 else ()))
                    te_m1 = p.v(s_te)
                    gelu_cnt[ci] = ons(lambda e, ft=ft, cw=cw, co=co, bnk=bnk:
                                       e.activation(
                                           hfm_sb[:, ft, co:co + cw], bnk[:, 0:cw],
                                           AF.Gelu, bias=0.0, scale=1.0),
                                       waits=W((s_te, te_m1)))
                    if fi == 1 and ci == len(CH1) - 1:
                        wgbfree_se[wslot] = gelu_cnt[ci]
                    co += cw
            if fg + 2 < FT // 2:
                load_wgb(fg + 2)
        se_h = p.v(s_se)

        wdh_n_map = {}
        for dh_ in range(2):
            for part in range(2):
                p.on("scalar", lambda e, ei=ei, dh_=dh_, part=part:
                     e.dma_start(
                         wdh_sb[:, dh_, part * 16:(part + 1) * 16, :],
                         wdown_bf[ei, part * 2048:(part + 1) * 2048,
                                  dh_ * 512:(dh_ + 1) * 512].rearrange(
                             "(a p) d -> p a d", p=128)),
                     waits=W((s_ve, wdhfree_ve[dh_]), (s_wpc, 32 * (ei + 1))),
                     incs=[(s_wdh[dh_], 16)])
            wdh_n_map[dh_] = p.v(s_wdh[dh_])
        for dh_ in range(2):
            wdslot = dh_
            wdh_n = wdh_n_map[dh_]
            for rt in range(NQ):
                bnk = bank[3 + rt % 2]
                for k in range(FT):
                    ont(lambda e, rt=rt, k=k, wdslot=wdslot, bnk=bnk: e.matmul(
                        bnk[:, :],
                        hfm_sb[:, k, rt * 128:(rt + 1) * 128],
                        wdh_sb[:, wdslot, k, :],
                        start=(k == 0), stop=(k == FT - 1)),
                        waits=(W((s_wdh[wdslot], wdh_n), (s_se, se_h),
                                 (s_ve, ffn_eo_ve[rt % 2])) if k == 0 else ()),
                        incs=([(s_te, 1)] if k == FT - 1 else ()))
                te_m2 = p.v(s_te)
                eslot2 = rt % 2
                ve_eo = onv(lambda e, bnk=bnk, eslot2=eslot2: e.tensor_copy(
                    eo_sb[:, eslot2, 0:512], bnk[:, :]),
                    waits=W((s_te, te_m2), (s_eos[eslot2], 16 * eo_use[eslot2])))
                ffn_eo_ve[rt % 2] = ve_eo
                if rt == NQ - 1:
                    wdhfree_ve[wdslot] = ve_eo
                p.on("scalar", lambda e, ei=ei, rt=rt, dh_=dh_, eslot2=eslot2: e.dma_start(
                    eo_dram[ei * CHAT + rt * 128:ei * CHAT + (rt + 1) * 128,
                            dh_ * 512:(dh_ + 1) * 512],
                    eo_sb[:, eslot2, 0:512]),
                    waits=W((s_ve, ve_eo)), incs=[(s_eos[eslot2], 16)])
                eo_use[eslot2] += 1
        dfmfree_se[eslot] = se_h
    p.on("sync", lambda e: e.nop(),
         waits=W((s_eos[0], p.v(s_eos[0])), (s_eos[1], p.v(s_eos[1]))),
         incs=[(s_eoready, 1)])
    eoready = p.v(s_eoready)

    barrier()
    es2.close()
    es3 = ExitStack()
    SB3 = lambda name, shape, dtype: es3.enter_context(nc.sbuf_tensor(name, shape, dtype))
    cgA_sb = SB3("cgA_sb", [128, 2, D], BF16)
    cgB_sb = SB3("cgB_sb", [128, 2, D], BF16)
    att_sb = SB3("att_sb", [128, 2, D], BF16)
    xr2_sb = SB3("xr2_sb", [128, 2, D], F32)
    outb_sb = SB3("outb_sb", [128, 2, D], F32)
    ga_sb = SB3("ga_sb", [128, 1], F32)
    gb_sb = SB3("gb_sb", [128, 1], F32)
    cmbfree_ve = [0, 0]
    attfree_ve = [0, 0]
    out_use = [0, 0]
    for t in range(nt):
        slot = t % 2
        p.on("gpsimd", lambda e, t=t, slot=slot: e.indirect_dma_start(
            cgA_sb[:, slot, :], None, eo_dram[:],
            IndirectOffsetOnAxis(cidx1_sb[:, t:t + 1], 0)),
            waits=W((s_eoready, eoready), (s_ve, cmbfree_ve[slot]),
                    (s_const, p.v(s_const))),
            incs=[(s_cmb[slot], 16)])
        p.on("gpsimd", lambda e, t=t, slot=slot: e.indirect_dma_start(
            cgB_sb[:, slot, :], None, eo_dram[:],
            IndirectOffsetOnAxis(cidx2_sb[:, t:t + 1], 0)),
            incs=[(s_cmb[slot], 16)])
        p.on("sync", lambda e, t=t, slot=slot: e.dma_start_transpose(
            att_sb[:, slot, :], attn_dram[:, t * 128:(t + 1) * 128]),
            waits=W((s_atready, atready_all), (s_ve, attfree_ve[slot]),
                    (s_se, p.v(s_se))),
            incs=[(s_att[slot], 16)])
        p.on("sync", lambda e, t=t, slot=slot: e.dma_start(
            xr2_sb[:, slot, :], x_own[t * 128:(t + 1) * 128, :]),
            waits=W((s_ve, attfree_ve[slot])),
            incs=[(s_att[slot], 16)])
        cmb_n = p.v(s_cmb[slot])
        att_n = p.v(s_att[slot])
        onv(lambda e, t=t: e.tensor_tensor(
            tmpa_sb[:], thr1b_sb[:], oh1_sb[:, t, :], ALU.mult),
            waits=W((s_msc, msc_thr)))
        onv(lambda e: e.reduce_sum(tmpc_sb[:], tmpa_sb[:], axis=AX.X))
        onv(lambda e, t=t: e.tensor_tensor(
            tmpd_sb[:], loc1_sb[:, t:t + 1], tmpc_sb[:], ALU.is_lt))
        onv(lambda e, t=t: e.tensor_tensor(
            ga_sb[:], g1n_sb[:, t:t + 1], tmpd_sb[:], ALU.mult))
        onv(lambda e, t=t: e.tensor_tensor(
            tmpa_sb[:], thr2b_sb[:], oh2_sb[:, t, :], ALU.mult))
        onv(lambda e: e.reduce_sum(tmpc_sb[:], tmpa_sb[:], axis=AX.X))
        onv(lambda e, t=t: e.tensor_tensor(
            tmpd_sb[:], loc2_sb[:, t:t + 1], tmpc_sb[:], ALU.is_lt))
        onv(lambda e, t=t: e.tensor_tensor(
            gb_sb[:], g2n_sb[:, t:t + 1], tmpd_sb[:], ALU.mult))
        onv(lambda e, slot=slot: e.tensor_scalar(
            outb_sb[:, slot, :], cgA_sb[:, slot, :], ga_sb[:], None, ALU.mult),
            waits=W((s_cmb[slot], cmb_n), (s_out[slot], 16 * out_use[slot])))
        onv(lambda e, slot=slot: e.scalar_tensor_tensor(
            outb_sb[:, slot, :], cgB_sb[:, slot, :], gb_sb[:], outb_sb[:, slot, :],
            ALU.mult, ALU.add))
        cmbfree_ve[slot] = onv(lambda e, slot=slot: e.tensor_tensor(
            outb_sb[:, slot, :], outb_sb[:, slot, :], att_sb[:, slot, :], ALU.add),
            waits=W((s_att[slot], att_n)))
        ve_out = onv(lambda e, slot=slot: e.tensor_tensor(
            outb_sb[:, slot, :], outb_sb[:, slot, :], xr2_sb[:, slot, :], ALU.add))
        attfree_ve[slot] = ve_out
        p.on("sync", lambda e, t=t, slot=slot: e.dma_start(
            y_out[t * 128:(t + 1) * 128, :], outb_sb[:, slot, :]),
            waits=W((s_ve, ve_out)), incs=[(s_out[slot], 16)])
        out_use[slot] += 1
    p.on("sync", None, waits=W((s_out[0], p.v(s_out[0])), (s_out[1], p.v(s_out[1]))))

    if dbg:
        dbg_xn = nc.dram_tensor("dbg_xn", [TOWN + 1, D], BF16, kind="ExternalOutput").ap()
        dbg_attn = nc.dram_tensor("dbg_attn", [D, TOWN], BF16, kind="ExternalOutput").ap()
        dbg_gl = nc.dram_tensor("dbg_gl", [E * CHAT + 1, 1], I32, kind="ExternalOutput").ap()
        dbg_eo = nc.dram_tensor("dbg_eo", [E * CHAT + 1, D], BF16, kind="ExternalOutput").ap()
        dbg_thr = nc.dram_tensor("dbg_thr", [1, 16], F32, kind="ExternalOutput").ap()
        dbg_rt = nc.dram_tensor("dbg_rt", [128, 8 * NT], F32, kind="ExternalOutput").ap()
        dbg_cnt = nc.dram_tensor("dbg_cnt", [1, 48], F32, kind="ExternalOutput").ap()
        dbg_disp = nc.dram_tensor("dbg_disp", [E * CHAT, D], BF16, kind="ExternalOutput").ap()
        s_dbg = p.sem("s_dbg")
        rtbuf = es3.enter_context(nc.sbuf_tensor("rtbuf", [128, 8 * NT], F32))
        ve0 = p.v(s_ve)
        onv(lambda e: e.tensor_copy(rtbuf[:, 0:NT], g1n_sb[:]))
        onv(lambda e: e.tensor_copy(rtbuf[:, NT:2 * NT], g2n_sb[:]))
        onv(lambda e: e.tensor_copy(rtbuf[:, 2 * NT:3 * NT], loc1_sb[:]))
        onv(lambda e: e.tensor_copy(rtbuf[:, 3 * NT:4 * NT], loc2_sb[:]))
        onv(lambda e: e.tensor_copy(rtbuf[:, 4 * NT:5 * NT], cidx1_sb[:]))
        onv(lambda e: e.tensor_copy(rtbuf[:, 5 * NT:6 * NT], cidx2_sb[:]))
        onv(lambda e: e.tensor_copy(rtbuf[:, 6 * NT:7 * NT], g1n_sb[:]))
        ved = onv(lambda e: e.tensor_copy(rtbuf[:, 7 * NT:8 * NT], g2n_sb[:]))
        p.on("sync", lambda e: e.dma_start(dbg_xn[:], xn_dram[:]),
             waits=W((s_out[0], p.v(s_out[0])), (s_out[1], p.v(s_out[1])),
                     (s_ve, ved)),
             incs=[(s_dbg, 16)])
        p.on("sync", lambda e: e.dma_start(dbg_attn[:], attn_dram[:]), incs=[(s_dbg, 16)])
        p.on("sync", lambda e: e.dma_start(dbg_gl[:], glist_dram[:]), incs=[(s_dbg, 16)])
        p.on("sync", lambda e: e.dma_start(dbg_eo[:], eo_dram[:]), incs=[(s_dbg, 16)])
        p.on("sync", lambda e: e.dma_start(dbg_thr[:], thr_dram[:]), incs=[(s_dbg, 16)])
        p.on("sync", lambda e: e.dma_start(dbg_rt[:], rtbuf[:]), incs=[(s_dbg, 16)])
        dbg_rem = nc.dram_tensor("dbg_rem", [1, RU * 4], BF16, kind="ExternalOutput").ap()
        def _dbgcp(e):
            with nc.allow_low_precision("dbg"):
                return e.tensor_copy(dbgrem_sb[:], dbgrem_sb[:])
        vrem = onv(_dbgcp)
        p.on("sync", lambda e: e.dma_start(dbg_rem[:], dbgrem_sb[:]),
             waits=W((s_ve, vrem)), incs=[(s_dbg, 16)])
        cntbuf = es3.enter_context(nc.sbuf_tensor("cntbuf", [1, 48], F32))
        onv(lambda e: e.tensor_copy(cntbuf[:, 0:16], remprev_sb[:]))
        onv(lambda e: e.tensor_copy(cntbuf[:, 16:24], remall_sb[:]))
        vec = onv(lambda e: e.tensor_copy(cntbuf[:, 24:48], run_sb[:]))
        p.on("sync", lambda e: e.dma_start(dbg_cnt[:], cntbuf[:]),
             waits=W((s_ve, vec)), incs=[(s_dbg, 16)])
        p.on("sync", lambda e: e.dma_start(dbg_disp[:], disp_dram[:]), incs=[(s_dbg, 16)])
        p.on("sync", None, waits=W((s_dbg, p.v(s_dbg))))

    sems = {}
    for name in p.sem_names:
        sems[name] = es.enter_context(nc.semaphore(name))
    block = es.enter_context(nc.Block())
    p.realize(block, sems)
    es3.close()
    es.close()
    return nc


def make_in_maps(x, wq, wk, wv, wo, wg, w_gate, w_down):
    in_maps = []
    for c in range(NC):
        own = np.ascontiguousarray(x[c * BPC:(c + 1) * BPC].reshape(TOWN, D))
        rem = np.ascontiguousarray(
            np.concatenate([x[b] for b in range(B) if b // BPC != c], axis=0))
        prevcols = np.zeros((128, NRB), np.float32)
        prevcols[:, :2 * c] = 1.0
        in_maps.append(dict(
            x_own=own, x_rem=rem, wq=wq, wk=wk, wv=wv, wo=wo, wg=wg,
            w_gate=w_gate, w_down=w_down, prevcols=prevcols))
    return in_maps


def kernel(L0_attn_x, L0_attn_wq, L0_attn_wk, L0_attn_wv, L0_attn_wo,
           L0_ffn_wg, L0_ffn_w_gate, L0_ffn_w_down):
    x = np.ascontiguousarray(np.asarray(L0_attn_x))
    nc = build_program()
    in_maps = make_in_maps(
        x, np.asarray(L0_attn_wq), np.asarray(L0_attn_wk), np.asarray(L0_attn_wv),
        np.asarray(L0_attn_wo), np.asarray(L0_ffn_wg), np.asarray(L0_ffn_w_gate),
        np.asarray(L0_ffn_w_down))
    res = run_bass_kernel_spmd(nc, in_maps, core_ids=list(range(NC)))
    out = np.concatenate([r["y"].reshape(BPC, S, D) for r in res.results], axis=0)
    return out.astype(np.float32)


if __name__ == "__main__":
    import reference as ref
    inputs = ref.setup_inputs()
    out = kernel(**{k: np.asarray(v) for k, v in inputs.items()})
    exp = np.asarray(ref.reference(**inputs))
    rvr = float(((out - exp) ** 2).mean() / (exp ** 2).mean())
    print("resid var ratio:", rvr)
